# version 49
# baseline (speedup 1.0000x reference)
import contextlib
import numpy as np
import concourse.bass as bass
import concourse.mybir as mybir
from concourse.bass_utils import run_bass_kernel_spmd

F32 = mybir.dt.float32
BF16 = mybir.dt.bfloat16
I32 = mybir.dt.int32
AF = mybir.ActivationFunctionType
ALU = mybir.AluOpType

D = 2048
T = 2048
NC = D // 128
TB = 512
NTB = T // TB
SSN = 16
INTER = 5504
NO = 43

ENGS = ("sync", "tensor", "vector", "scalar", "gpsimd")
DMA_ENGS = ("sync", "scalar", "gpsimd", "vector")


class Prog:

    NSLOT = 8

    def __init__(self):
        self.ops = []
        self.lastw = {}
        self.readers = {}
        self.last_per_eng = {}
        self.last_dma_slot = {}
        self.dma_ord = {}
        self.bar = None

    def op(self, eng, fn, r=(), w=(), dma=False, qchain=False):
        ps_r = [k for k in r if isinstance(k, tuple) and len(k) > 0 and k[0] == "PS"]
        if ps_r:
            r = [k for k in r if k not in ps_r]
            w = list(w) + ps_r
        deps = set()
        for k in r:
            pr = self.lastw.get(k)
            if pr is not None:
                deps.add(pr)
        for k in w:
            pr = self.lastw.get(k)
            if pr is not None:
                deps.add(pr)
            deps.update(self.readers.get(k, ()))
        if self.bar is not None:
            deps.add(self.bar)
        i = len(self.ops)
        deps.discard(i)
        slot = None
        if dma:
            o = self.dma_ord.get(eng, 0)
            self.dma_ord[eng] = o + 1
            slot = o % self.NSLOT
            self.last_dma_slot[(eng, slot)] = i
        else:
            self.last_per_eng[eng] = i
        self.ops.append([eng, fn, dma, deps, False, slot, qchain])
        for k in r:
            self.readers.setdefault(k, set()).add(i)
        for k in w:
            self.lastw[k] = i
            self.readers[k] = set()
        return i

    def barrier(self):
        deps = set(self.last_per_eng.values()) | set(self.last_dma_slot.values())
        i = len(self.ops)
        self.ops.append(["vector", lambda v: v.nop(), False, deps, False, None, False])
        self.last_per_eng["vector"] = i
        self.bar = i

    def finalize(self, sems):
        ops = self.ops
        for o in ops:
            for d in o[3]:
                ops[d][4] = True
        for o in ops:
            if o[2]:
                o[4] = True
        cnt = {}
        val = [None] * len(ops)
        for i, (eng, fn, dma, deps, need, slot, qc) in enumerate(ops):
            if need:
                key = ("d", eng, slot) if dma else ("c", eng)
                cnt[key] = cnt.get(key, 0) + (16 if dma else 1)
                val[i] = (key, cnt[key])
        streams = {e: [] for e in ENGS}
        for i, (eng, fn, dma, deps, need, slot, qc) in enumerate(ops):
            waits = {}
            for d in deps:
                if eng == "tensor" and not dma and ops[d][0] == "tensor" and not ops[d][2]:
                    continue
                assert val[d] is not None
                k, v = val[d]
                if waits.get(k, -1) < v:
                    waits[k] = v
            streams[eng].append((i, fn, sorted(waits.items()), val[i]))
        counts = {e: (len(streams[e]), sum(len(w) for _, _, w, _ in streams[e]))
                  for e in ENGS}

        def mk(eng_name):
            def emit(eng):
                known = {}
                for i, fn, waits, v in streams[eng_name]:
                    for k, req in waits:
                        if known.get(k, -1) < req:
                            eng.wait_ge(sems[k], req)
                            known[k] = req
                    inst = fn(eng)
                    if v is not None:
                        assert inst is not None, f"op {i} on {eng_name} needs inc but emitted nothing"
                        inst.then_inc(sems[v[0]], 16 if v[0][0] == "d" else 1)
            return emit
        return mk, counts


class PS:
    def __init__(self, stack, nc, name, dt=F32):
        n = 512 if dt == F32 else 1024
        self.t = stack.enter_context(nc.psum_tensor(name, [128, n], dt))
        self.k = ("PS", name)

    def __getitem__(self, sl):
        return self.t[sl]


def build_nc(phases=("A", "B", "C", "D"), dbg=False, c_sub=(1, 2, 3)):
    nc = bass.Bass("TRN2", target_bir_lowering=False)
    p = Prog()

    def din(name, shape, dt=F32):
        return nc.dram_tensor(name, list(shape), dt, kind="ExternalInput").ap()

    xT = din("xT", [D, T])
    w_in = din("w_in", [D, 2 * D])
    cw_h = din("cw_h", [NC, 4, D, 128])
    conv_b = din("conv_b", [128, NC])
    w_x = din("w_x", [D, 160])
    w_dt = din("w_dt", [128, D])
    b_dt = din("b_dt", [128, NC])
    a_log = din("a_log", [128, NC * SSN])
    d_vec = din("d_vec", [128, NC])
    w_out = din("w_out", [D, D])
    wqT = din("wqT", [D, D])
    wkT = din("wkT", [D, D])
    wvT = din("wvT", [D, D])
    woT = din("woT", [D, D])
    wg1T = din("wg1T", [D, D])
    wg2T = din("wg2T", [D, D])
    bg1 = din("bg1", [128, NC])
    bg2 = din("bg2", [128, NC])
    alpha = din("alpha", [1, 2])
    lut = din("lut", [1, 16])
    gidxT = din("gidxT", [D, INTER], I32)
    uidxT = din("uidxT", [D, INTER], I32)
    didxT = din("didxT", [INTER, D], I32)

    outT = nc.dram_tensor("outT", [D, T], F32, kind="ExternalOutput").ap()

    def scr(name, shape, dt, addr_space="Local"):
        kind = "ExternalOutput" if dbg else "Internal"
        return nc.dram_tensor(name, list(shape), dt, kind=kind, addr_space=addr_space).ap()

    XC = scr("XC", [D, 3 + T], BF16)
    if dbg:
        DBG_DL = scr("DBG_DL", [D, TB], BF16)
        DBG_DU = scr("DBG_DU", [D, TB], BF16)
        DBG_B = scr("DBG_B", [128, TB], BF16)
        DBG_C = scr("DBG_C", [128, TB], F32)
        DBG_H = scr("DBG_H", [128, NC * TB], F32)
        DBG_Y = scr("DBG_Y", [D, TB], F32)
        DBG_U = scr("DBG_U", [D, TB], BF16)
        DBG_DP = scr("DBG_DP", [128, TB], BF16)
        DBG_BC = scr("DBG_BC", [16, TB], BF16)
        DBG_CC = scr("DBG_CC", [16, TB], BF16)
        DBG_X2 = scr("DBG_X2", [D, TB], BF16)
    RR = scr("RR", [D, T], BF16)
    XS = scr("XS", [D, T], BF16)
    OIN = scr("OIN", [D, T], BF16)
    R1 = scr("R1", [D, T], F32)
    R2 = scr("R2", [D, T], F32)
    R3 = scr("R3", [D, T], F32)
    WG = scr("WGd", [D, INTER], BF16)
    WU = scr("WUd", [D, INTER], BF16)
    WD = scr("WDd", [INTER, D], BF16)
    HB = scr("HBd", [INTER, T], BF16)
    QF = scr("QFd", [D, T], BF16)
    KF = scr("KFd", [D, T], BF16)
    VF = scr("VFd", [D, T], BF16)
    G1 = scr("G1d", [D, T], BF16)
    G2 = scr("G2d", [D, T], BF16)

    last_phase = phases[-1]
    res_out = {"A": R1, "B": R2, "C": R3, "D": outT}
    res_out[last_phase] = outT
    res_key = {"A": "RES_A", "B": "RES_B", "C": "RES_C", "D": "RES_D"}

    st = contextlib.ExitStack()
    with st:
        sems = {}
        for e in ENGS:
            sems[("c", e)] = st.enter_context(nc.semaphore(f"c_{e}"))
        for e in DMA_ENGS:
            for s_ in range(Prog.NSLOT):
                sems[("d", e, s_)] = st.enter_context(nc.semaphore(f"d_{e}{s_}"))

        sb = lambda name, shape, dt: st.enter_context(nc.sbuf_tensor(name, list(shape), dt))
        ident = sb("ident", [128, 128], BF16)
        ones_k1_f = sb("ones_k1_f", [1, 128], F32)
        ones_m1_bf = sb("ones_m1_bf", [128, 1], BF16)
        zeros512 = sb("zeros512", [128, 512], BF16)
        sel = sb("sel", [16, 16 * 128], BF16)
        onesrow = sb("onesrow", [1, 128], BF16)
        epscol = sb("epscol", [128, 1], F32)
        onecol = sb("onecol", [128, 1], F32)

        def g_init(g):
            g.memset(ident[:], 1.0)
            g.affine_select(out=ident[:], in_=ident[:], pattern=[[1, 128]],
                            compare_op=ALU.is_ge, fill=0.0, channel_multiplier=-1)
            return g.affine_select(out=ident[:], in_=ident[:], pattern=[[-1, 128]],
                                   compare_op=ALU.is_ge, fill=0.0, channel_multiplier=1)
        p.op("gpsimd", g_init, w=[("ident",)])
        p.op("gpsimd", lambda g: g.memset(ones_k1_f[:], 1.0), w=[("o2",)])
        p.op("gpsimd", lambda g: g.memset(ones_m1_bf[:], 1.0), w=[("o3",)])
        p.op("gpsimd", lambda g: g.memset(zeros512[:], 0.0), w=[("z5",)])
        p.op("gpsimd", lambda g: g.memset(sel[:], 0.0), w=[("sel",)])
        p.op("gpsimd", lambda g: g.memset(onesrow[:], 1.0), w=[("orow",)])
        p.op("gpsimd", lambda g: g.memset(epscol[:], 1e-5), w=[("eps",)])
        p.op("gpsimd", lambda g: g.memset(onecol[:], 1.0), w=[("onec",)])
        for n in range(16):
            p.op("gpsimd", lambda g, n=n: g.dma_start(out=sel[n:n + 1, 128 * n:128 * (n + 1)], in_=onesrow[:]),
                 r=[("orow",), ("sel",)], w=[("sel", n)], dma=True)

        def sel_n(n):
            return sel[0:16, 128 * n:128 * (n + 1)]

        mask_ut = sb("mask_ut", [128, 128], BF16)
        p.op("gpsimd", lambda g: g.memset(mask_ut[:], 1.0), w=[("mut",)])
        p.op("gpsimd", lambda g: g.affine_select(out=mask_ut[:], in_=mask_ut[:], pattern=[[1, 128]],
                                                 compare_op=ALU.is_ge, fill=0.0, channel_multiplier=-1),
             r=[("mut",)], w=[("mut",)])
        lutbc = sb("lutbc", [128, 16], F32)
        p.op("sync", lambda e: e.dma_start(out=lutbc[0:1, :], in_=lut), w=[("lut", 0)], dma=True)
        for dbl in range(7):
            lo, hi = 1 << dbl, 1 << (dbl + 1)
            p.op("sync", lambda e, lo=lo, hi=hi: e.dma_start(out=lutbc[lo:hi, :], in_=lutbc[0:lo, :]),
                 r=[("lut", dbl)], w=[("lut", dbl + 1)], dma=True)
        LUTK = ("lut", 7)

        def ln_phase(tag, src_dram, src_key, xn_pool, scope, psA, psB, psC, tick=None):
            sbp = lambda name, shape, dt: scope.enter_context(nc.sbuf_tensor(name, list(shape), dt))
            xf = [sbp(f"{tag}xf{i}", [128, TB], BF16) for i in range(4)]
            sq = [sbp(f"{tag}sq{i}", [128, TB], BF16) for i in range(2)]
            rows = [sbp(f"{tag}rw{i}", [1, TB], F32) for i in range(6)]
            bca = [sbp(f"{tag}bc{i}", [128, TB], F32) for i in range(2)]
            K = lambda *a: (tag,) + a
            for tb in range(NTB):
                for c in range(NC):
                    p.op("gpsimd", lambda e, c=c, tb=tb: e.dma_start(
                        out=xf[c % 4][:], in_=src_dram[128 * c:128 * (c + 1), TB * tb:TB * (tb + 1)]),
                        r=[src_key(c, tb)], w=[K("xf", c % 4)], dma=True)
                    if tick:
                        tick(1)
                    p.op("scalar", lambda s, c=c: s.square(sq[c % 2][:], xf[c % 4][:]),
                         r=[K("xf", c % 4)], w=[K("sq", c % 2)])
                    p.op("tensor", lambda t, c=c: t.matmul(psA[0:1, :], ones_m1_bf[:], xf[c % 4][:],
                                                           start=(c == 0), stop=(c == NC - 1)),
                         r=[K("xf", c % 4), ("o3",)], w=[psA.k])
                    p.op("tensor", lambda t, c=c: t.matmul(psB[0:1, :], ones_m1_bf[:], sq[c % 2][:],
                                                           start=(c == 0), stop=(c == NC - 1)),
                         r=[K("sq", c % 2), ("o3",)], w=[psB.k])
                p.op("scalar", lambda s: s.mul(rows[0][:], psA[0:1, :], 1.0 / D),
                     r=[psA.k], w=[K("r0")])
                p.op("scalar", lambda s: s.mul(rows[1][:], psB[0:1, :], 1.0 / D),
                     r=[psB.k], w=[K("r1")])
                p.op("vector", lambda v: v.tensor_mul(rows[2][:], rows[0][:], rows[0][:]),
                     r=[K("r0")], w=[K("r2")])
                p.op("vector", lambda v: v.tensor_sub(rows[3][:], rows[1][:], rows[2][:]),
                     r=[K("r1"), K("r2")], w=[K("r3")])
                p.op("scalar", lambda s: s.activation(rows[4][:], rows[3][:], AF.Ln, bias=epscol[0:1, 0:1]),
                     r=[K("r3"), ("eps",)], w=[K("r4")])
                p.op("scalar", lambda s: s.activation(rows[4][:], rows[4][:], AF.Exp, scale=-0.5),
                     r=[K("r4")], w=[K("r4")])
                p.op("vector", lambda v: v.tensor_mul(rows[5][:], rows[0][:], rows[4][:]),
                     r=[K("r0"), K("r4")], w=[K("r5")])
                p.op("tensor", lambda t: t.matmul(psC[:, 0:TB], ones_k1_f[:], rows[4][:], start=True, stop=True),
                     r=[K("r4"), ("o2",)], w=[psC.k])
                p.op("vector", lambda v: v.tensor_copy(bca[0][:], psC[:, 0:TB]),
                     r=[psC.k], w=[K("bc0")])
                p.op("tensor", lambda t: t.matmul(psC[:, 0:TB], ones_k1_f[:], rows[5][:], start=True, stop=True),
                     r=[K("r5"), ("o2",)], w=[psC.k])
                p.op("vector", lambda v: v.tensor_copy(bca[1][:], psC[:, 0:TB]),
                     r=[psC.k], w=[K("bc1")])
                for c in range(NC):
                    p.op("gpsimd", lambda e, c=c, tb=tb: e.dma_start(
                        out=xf[c % 4][:], in_=src_dram[128 * c:128 * (c + 1), TB * tb:TB * (tb + 1)]),
                        r=[src_key(c, tb)], w=[K("xf", c % 4)], dma=True)
                    p.op("vector", lambda v, c=c, tb=tb: v.scalar_tensor_tensor(
                        xn_pool[c][:, TB * tb:TB * (tb + 1)],
                        xf[c % 4][:], 1.0, bca[0][:], ALU.mult, ALU.mult),
                        r=[K("xf", c % 4), K("bc0")], w=[K("xnt", c)])
                    p.op("vector", lambda v, c=c, tb=tb: v.tensor_sub(
                        xn_pool[c][:, TB * tb:TB * (tb + 1)],
                        xn_pool[c][:, TB * tb:TB * (tb + 1)], bca[1][:]),
                        r=[K("xnt", c), K("bc1")], w=[K("xn", c, tb)])
            return lambda c, tb: K("xn", c, tb)

        def phase_A01(RA):
            with contextlib.ExitStack() as ph:
                sbp = lambda name, shape, dt: ph.enter_context(nc.sbuf_tensor(name, list(shape), dt))
                ps0 = PS(ph, nc, "pA0"); ps1 = PS(ph, nc, "pA1"); ps2 = PS(ph, nc, "pA2")
                xn = [sbp(f"Axn{i}", [128, T], BF16) for i in range(NC)]
                xnk = ln_phase("A0", xT, lambda c, tb: ("xTd", c, tb), xn, ph, ps0, ps1, ps2,
                               tick=lambda n: tick_v(1))
                wh = [sbp(f"Awh{i}", [128, D // 2], BF16) for i in range(NC)]
                ev = [sbp(f"Aev{i}", [128, TB], BF16) for i in range(2)]
                for hf in range(2):
                  for mh in range(2):
                    for c in range(NC):
                        p.op("gpsimd", lambda e, c=c, hf=hf, mh=mh: e.dma_start(
                            out=wh[c][:], in_=w_in[128 * c:128 * (c + 1),
                                                   D * hf + 1024 * mh:D * hf + 1024 * (mh + 1)]),
                            w=[("A.wh", c)], dma=True)
                    for m in range(8 * mh, 8 * mh + 8):
                        for tb in range(NTB):
                            pp = (ps0, ps1)[(m * NTB + tb) % 2]
                            def fm(t, m=m, tb=tb, pp=pp, mh=mh):
                                for c in range(NC):
                                    ins = t.matmul(pp[:, 0:TB],
                                                   wh[c][:, 128 * (m - 8 * mh):128 * (m - 8 * mh + 1)],
                                                   xn[c][:, TB * tb:TB * (tb + 1)],
                                                   start=(c == 0), stop=(c == NC - 1))
                                return ins
                            p.op("tensor", fm,
                                 r=[("A.wh", c) for c in range(NC)] + [xnk(c, tb) for c in range(NC)],
                                 w=[pp.k])
                            eb = ev[(m * NTB + tb) % 2]
                            ebk = ("A.ev", (m * NTB + tb) % 2)
                            p.op("scalar", lambda s, pp=pp, eb=eb: s.copy(eb[:], pp[:, 0:TB]),
                                 r=[pp.k], w=[ebk])
                            if hf == 0:
                                p.op("scalar", lambda e, m=m, tb=tb, eb=eb: e.dma_start(
                                    out=XC[128 * m:128 * (m + 1), 3 + TB * tb:3 + TB * (tb + 1)], in_=eb[:]),
                                    r=[ebk], w=[("XC", m, tb)], dma=True)
                            else:
                                p.op("scalar", lambda e, m=m, tb=tb, eb=eb: e.dma_start(
                                    out=RR[128 * m:128 * (m + 1), TB * tb:TB * (tb + 1)], in_=eb[:]),
                                    r=[ebk], w=[("RR", m, tb)], dma=True)
                            tick_v(4)
                zpad = sbp("Azpad", [128, 3], BF16)
                p.op("gpsimd", lambda g: g.memset(zpad[:], 0.0), w=[("A.zpad",)])
                for m in range(NC):
                    p.op("scalar", lambda e, m=m: e.dma_start(out=XC[128 * m:128 * (m + 1), 0:3], in_=zpad[:]),
                         r=[("A.zpad",)], w=[("XC", m, "pad")], dma=True)

        def phase_A2():
            with contextlib.ExitStack() as ph:
                p.barrier()
                sbp = lambda name, shape, dt: ph.enter_context(nc.sbuf_tensor(name, list(shape), dt))
                ps0 = PS(ph, nc, "pA20"); ps1 = PS(ph, nc, "pA21")
                xc = [sbp(f"A2xc{i}", [128, 3 + T], BF16) for i in range(NC)]
                cwb = [sbp(f"A2cw{i}", [128, D], BF16) for i in range(8)]
                cb16 = sbp("A2cb", [128, 16], F32)
                ev = [sbp(f"A2ev{i}", [128, TB], BF16) for i in range(2)]
                p.op("sync", lambda e: e.dma_start(out=cb16[:], in_=conv_b),
                     w=[("A.cb16",)], dma=True)
                for c in range(NC):
                    p.op("sync", lambda e, c=c: e.dma_start(out=xc[c][:], in_=XC[128 * c:128 * (c + 1), :]),
                         r=[("XC", c, "pad")] + [("XC", c, tb) for tb in range(NTB)],
                         w=[("A.xc", c)], dma=True)
                for m in range(NC):
                    par = m % 2
                    for k in range(4):
                        p.op("gpsimd", lambda e, m=m, k=k, par=par: e.dma_start(
                            out=cwb[4 * par + k][:].rearrange("p (c o) -> p c o", c=NC),
                            in_=cw_h[m, k].rearrange("(c p) o -> p c o", p=128)),
                            w=[("A.cwb", 4 * par + k)], dma=True)
                    for tb in range(NTB):
                        pp = (ps0, ps1)[tb % 2]
                        def fm(t, m=m, tb=tb, pp=pp, par=par):
                            for k in range(4):
                                for c in range(NC):
                                    ins = t.matmul(pp[:, 0:TB], cwb[4 * par + k][:, 128 * c:128 * (c + 1)],
                                                   xc[c][:, TB * tb + k:TB * tb + k + TB],
                                                   start=(k == 0 and c == 0), stop=(k == 3 and c == NC - 1))
                            return ins
                        p.op("tensor", fm,
                             r=[("A.cwb", 4 * par + k) for k in range(4)] + [("A.xc", c) for c in range(NC)],
                             w=[pp.k])
                        eb = ev[tb % 2]
                        ebk = ("A.ev2", tb % 2)
                        p.op("vector", lambda v, pp=pp, eb=eb, m=m: v.tensor_scalar(
                            eb[:], pp[:, 0:TB], cb16[:, m:m + 1], None, ALU.add),
                            r=[pp.k, ("A.cb16",)], w=[ebk])
                        p.op("scalar", lambda e, m=m, tb=tb, eb=eb: e.dma_start(
                            out=XS[128 * m:128 * (m + 1), TB * tb:TB * (tb + 1)], in_=eb[:]),
                            r=[ebk], w=[("XS", m, tb)], dma=True)
                        tick_v(10)

        def phase_A3():
            with contextlib.ExitStack() as ph:
                p.barrier()
                sbp = lambda name, shape, dt: ph.enter_context(nc.sbuf_tensor(name, list(shape), dt))
                ps0 = PS(ph, nc, "pA30"); ps1 = PS(ph, nc, "pA31")
                ps2 = PS(ph, nc, "pA32"); ps3 = PS(ph, nc, "pA33")
                wxr = [sbp(f"A3wx{i}", [128, 160], BF16) for i in range(NC)]
                wdtr = sbp("A3wdt", [128, D], BF16)
                nA = sbp("A3nA", [128, 256], F32)
                bdt16 = sbp("A3bdt", [128, 16], F32)
                dv16 = sbp("A3dv", [128, 16], F32)
                xs = [sbp(f"A3xs{i}", [128, TB], BF16) for i in range(2)]
                u = [sbp(f"A3u{i}", [128, TB], BF16) for i in range(8)]
                dpw = sbp("A3dpw", [128, T], BF16)
                Bw = sbp("A3Bw", [16, T], BF16)
                Cw = sbp("A3Cw", [16, T], BF16)
                uw = [sbp(f"A3uw{i}", [128, T], BF16) for i in range(2)]
                saw = sbp("A3saw", [128, T], BF16)
                sbw = sbp("A3sbw", [128, T], BF16)
                dlt = [sbp(f"A3dl{i}", [128, T], BF16) for i in range(2)]
                duw = [sbp(f"A3du{i}", [128, T], BF16) for i in range(2)]
                dAw = [sbp(f"A3dA{i}", [128, T], BF16) for i in range(2)]
                dBw = [sbp(f"A3dB{i}", [128, T], BF16) for i in range(2)]
                Hw = [sbp(f"A3H{i}", [128, T], BF16) for i in range(2)]
                tmpw = [sbp(f"A3tp{i}", [128, T], BF16) for i in range(4)]
                Bbw = [sbp(f"A3Bb{i}", [128, T], BF16) for i in range(2)]
                Cbw = [sbp(f"A3Cb{i}", [128, T], BF16) for i in range(2)]
                yaw = [sbp(f"A3ya{i}", [128, T], F32) for i in range(2)]
                rrw = [sbp(f"A3rr{i}", [128, T], BF16) for i in range(2)]
                oinw = [sbp(f"A3oi{i}", [128, T], BF16) for i in range(2)]

                for c in range(NC):
                    p.op("gpsimd", lambda e, c=c: e.dma_start(out=wxr[c][:], in_=w_x[128 * c:128 * (c + 1), :]),
                         w=[("A.wx", c)], dma=True)
                p.op("gpsimd", lambda e: e.dma_start(out=wdtr[:], in_=w_dt), w=[("A.wdt",)], dma=True)
                p.op("sync", lambda e: e.dma_start(out=nA[:], in_=a_log),
                     w=[("A.nA",)], dma=True)
                p.op("scalar", lambda s: s.activation(nA[:], nA[:], AF.Exp), r=[("A.nA",)], w=[("A.nA",)])
                p.op("vector", lambda v: v.tensor_scalar(nA[:], nA[:], -1.0, None, ALU.mult),
                     r=[("A.nA",)], w=[("A.nA",)])
                p.op("sync", lambda e: e.dma_start(out=bdt16[:], in_=b_dt),
                     w=[("A.bdt",)], dma=True)
                p.op("sync", lambda e: e.dma_start(out=dv16[:], in_=d_vec),
                     w=[("A.dv",)], dma=True)

                for tb in range(NTB):
                    for chf in range(2):
                        for c8 in range(8):
                            c = 8 * chf + c8
                            p.op("sync", lambda e, c=c, tb=tb: e.dma_start(
                                out=xs[c % 2][:], in_=XS[128 * c:128 * (c + 1), TB * tb:TB * (tb + 1)]),
                                r=[("XS", c, tb)], w=[("A.xs", c % 2)], dma=True)
                            p.op("scalar", lambda s, c=c, c8=c8: s.activation(u[c8][:], xs[c % 2][:], AF.Silu),
                                 r=[("A.xs", c % 2)], w=[("A.u", c8)])
                        uks = [("A.u", c8) for c8 in range(8)]
                        wxk = [("A.wx", 8 * chf + c8) for c8 in range(8)]

                        def f_dp(t, chf=chf):
                            for c8 in range(8):
                                ins = t.matmul(ps0[:, 0:TB], wxr[8 * chf + c8][:, 0:128], u[c8][:],
                                               start=(chf == 0 and c8 == 0), stop=(chf == 1 and c8 == 7))
                            return ins
                        p.op("tensor", f_dp, r=wxk + uks, w=[ps0.k])

                        def f_B(t, chf=chf):
                            for c8 in range(8):
                                ins = t.matmul(ps1[0:16, 0:TB], wxr[8 * chf + c8][:, 128:144], u[c8][:],
                                               start=(chf == 0 and c8 == 0), stop=(chf == 1 and c8 == 7))
                            return ins
                        p.op("tensor", f_B, r=wxk + uks, w=[ps1.k])

                        def f_C(t, chf=chf):
                            for c8 in range(8):
                                ins = t.matmul(ps2[0:16, 0:TB], wxr[8 * chf + c8][:, 144:160], u[c8][:],
                                               start=(chf == 0 and c8 == 0), stop=(chf == 1 and c8 == 7))
                            return ins
                        p.op("tensor", f_C, r=wxk + uks, w=[ps2.k])
                    p.op("scalar", lambda s, tb=tb: s.copy(dpw[:, TB * tb:TB * (tb + 1)], ps0[:, 0:TB]),
                         r=[ps0.k], w=[("A.dpw", tb)])
                    p.op("scalar", lambda s, tb=tb: s.copy(Bw[:, TB * tb:TB * (tb + 1)], ps1[0:16, 0:TB]),
                         r=[ps1.k], w=[("A.Bw", tb)])
                    p.op("scalar", lambda s, tb=tb: s.copy(Cw[:, TB * tb:TB * (tb + 1)], ps2[0:16, 0:TB]),
                         r=[ps2.k], w=[("A.Cw", tb)])
                dpk = [("A.dpw", tb) for tb in range(NTB)]
                Bwk = [("A.Bw", tb) for tb in range(NTB)]
                Cwk = [("A.Cw", tb) for tb in range(NTB)]

                for cg in range(8):
                    for ci in range(2):
                        c = 2 * cg + ci
                        kuw = ("A.uw", c % 2)
                        p.op("sync", lambda e, c=c: e.dma_start(
                            out=uw[c % 2][:], in_=XS[128 * c:128 * (c + 1), :]),
                            r=[("XS", c, tb) for tb in range(NTB)], w=[kuw], dma=True)
                        p.op("scalar", lambda s, c=c: s.activation(uw[c % 2][:], uw[c % 2][:], AF.Silu),
                             r=[kuw], w=[kuw])
                        for tb in range(NTB):
                            pp = (ps0, ps3)[tb % 2]
                            p.op("tensor", lambda t, c=c, tb=tb, pp=pp: t.matmul(
                                pp[:, 0:TB], wdtr[:, 128 * c:128 * (c + 1)], dpw[:, TB * tb:TB * (tb + 1)],
                                start=True, stop=True),
                                r=[("A.wdt",)] + dpk, w=[pp.k])
                            p.op("scalar", lambda s, pp=pp, tb=tb, c=c: s.activation(
                                saw[:, TB * tb:TB * (tb + 1)], pp[:, 0:TB], AF.Abs, bias=bdt16[:, c:c + 1]),
                                r=[pp.k, ("A.bdt",)], w=[("A.saw", tb)])
                            p.op("scalar", lambda s, pp=pp, tb=tb, c=c: s.activation(
                                sbw[:, TB * tb:TB * (tb + 1)], pp[:, 0:TB], AF.Relu, bias=bdt16[:, c:c + 1]),
                                r=[pp.k, ("A.bdt",)], w=[("A.sbw", tb)])
                        sawk = [("A.saw", tb) for tb in range(NTB)]
                        sbwk = [("A.sbw", tb) for tb in range(NTB)]
                        p.op("scalar", lambda s: s.activation(saw[:], saw[:], AF.Exp, scale=-1.0),
                             r=sawk, w=[("A.sawf",)])
                        p.op("scalar", lambda s: s.activation(saw[:], saw[:], AF.Ln, bias=onecol[:, 0:1]),
                             r=[("A.sawf",), ("onec",)], w=[("A.sawf",)])
                        p.op("vector", lambda v, ci=ci: v.tensor_add(dlt[ci][:], saw[:], sbw[:]),
                             r=[("A.sawf",)] + sbwk, w=[("A.dl", ci)])
                        p.op("vector", lambda v, ci=ci: v.tensor_mul(duw[ci][:], dlt[ci][:], uw[ci][:]),
                             r=[("A.dl", ci), ("A.uw", ci)], w=[("A.du", ci)])
                    for n in range(16):
                        npar = n % 2
                        Bb, Cb = Bbw[npar], Cbw[npar]
                        KB, KC = ("A.Bbw", npar), ("A.Cbw", npar)
                        for tb in range(NTB):
                            pp = (ps1, ps2)[tb % 2]
                            p.op("tensor", lambda t, n=n, tb=tb, pp=pp: t.matmul(
                                pp[:, 0:TB], sel_n(n), Bw[:, TB * tb:TB * (tb + 1)], start=True, stop=True),
                                r=Bwk + [("sel", n)], w=[pp.k])
                            p.op("scalar", lambda s, tb=tb, pp=pp, Bb=Bb: s.copy(
                                Bb[:, TB * tb:TB * (tb + 1)], pp[:, 0:TB]), r=[pp.k], w=[(KB, tb)])
                            p.op("tensor", lambda t, n=n, tb=tb, pp=pp: t.matmul(
                                pp[:, 0:TB], sel_n(n), Cw[:, TB * tb:TB * (tb + 1)], start=True, stop=True),
                                r=Cwk + [("sel", n), (KB, tb)], w=[pp.k])
                            p.op("scalar", lambda s, tb=tb, pp=pp, Cb=Cb: s.copy(
                                Cb[:, TB * tb:TB * (tb + 1)], pp[:, 0:TB]), r=[pp.k], w=[(KC, tb)])
                        KBf = [(KB, tb) for tb in range(NTB)]
                        KCf = [(KC, tb) for tb in range(NTB)]
                        for ci in range(2):
                            c = 2 * cg + ci
                            rot = (2 * n + ci) % 2
                            p.op("scalar", lambda s, c=c, n=n, rot=rot, ci=ci: s.activation(
                                dAw[rot][:], dlt[ci][:], AF.Exp, scale=nA[:, 16 * c + n:16 * c + n + 1]),
                                r=[("A.dl", ci), ("A.nA",)], w=[("A.dAw", rot)])
                            p.op("vector", lambda v, ci=ci, rot=rot, Bb=Bb: v.tensor_mul(
                                dBw[rot][:], duw[ci][:], Bb[:]),
                                r=[("A.du", ci)] + KBf, w=[("A.dBw", rot)])
                            p.op("vector", lambda v, rot=rot: v.tensor_tensor_scan(
                                Hw[rot][:], dAw[rot][:], dBw[rot][:], 0.0, ALU.mult, ALU.add),
                                r=[("A.dAw", rot), ("A.dBw", rot)], w=[("A.Hw", rot)])
                            tpi = (2 * n + ci) % 4
                            p.op("vector", lambda v, rot=rot, tpi=tpi, Cb=Cb: v.tensor_mul(
                                tmpw[tpi][:], Hw[rot][:], Cb[:]),
                                r=[("A.Hw", rot)] + KCf, w=[("A.tpw", tpi)])
                            if n == 0:
                                p.op("gpsimd", lambda g, ci=ci, tpi=tpi: g.dma_start(
                                    out=yaw[ci][:], in_=tmpw[tpi][:], max_dma_last_dim=2048),
                                    r=[("A.tpw", tpi)], w=[("A.yaw", ci)], dma=True, qchain=True)
                            else:
                                p.op("gpsimd", lambda g, ci=ci, tpi=tpi: g.dma_start(
                                    out=yaw[ci][:], in_=tmpw[tpi][:], accum_op=ALU.add,
                                    max_dma_last_dim=2048),
                                    r=[("A.tpw", tpi), ("A.yaw", ci)], w=[("A.yaw", ci)], dma=True, qchain=True)
                    for ci in range(2):
                        c = 2 * cg + ci
                        p.op("sync", lambda e, c=c, ci=ci: e.dma_start(
                            out=rrw[ci][:], in_=RR[128 * c:128 * (c + 1), :]),
                            r=[("RR", c, tb) for tb in range(NTB)], w=[("A.rrw", ci)], dma=True)
                        p.op("scalar", lambda s, ci=ci: s.activation(rrw[ci][:], rrw[ci][:], AF.Silu),
                             r=[("A.rrw", ci)], w=[("A.rrw", ci)])
                        p.op("vector", lambda v, c=c, ci=ci: v.scalar_tensor_tensor(
                            oinw[ci][:], uw[ci][:], dv16[:, c:c + 1], yaw[ci][:], ALU.mult, ALU.add),
                            r=[("A.uw", ci), ("A.dv",), ("A.yaw", ci)], w=[("A.oiw", ci)])
                        p.op("vector", lambda v, ci=ci: v.tensor_mul(oinw[ci][:], oinw[ci][:], rrw[ci][:]),
                             r=[("A.oiw", ci), ("A.rrw", ci)], w=[("A.oiw", ci)])
                        p.op("scalar", lambda e, c=c, ci=ci: e.dma_start(
                            out=OIN[128 * c:128 * (c + 1), :], in_=oinw[ci][:]),
                            r=[("A.oiw", ci)], w=[("OIN", c, 0), ("OIN", c, 1), ("OIN", c, 2), ("OIN", c, 3)],
                            dma=True)

        def phase_A4(RA):
            with contextlib.ExitStack() as ph:
                p.barrier()
                sbp = lambda name, shape, dt: ph.enter_context(nc.sbuf_tensor(name, list(shape), dt))
                ps0 = PS(ph, nc, "pA40"); ps1 = PS(ph, nc, "pA41")
                oin = [sbp(f"A4o{i}", [128, TB], BF16) for i in range(NC)]
                xs = [sbp(f"A4x{i}", [128, TB], BF16) for i in range(NC)]
                wob = [sbp(f"A4w{i}", [128, TB], BF16) for i in range(2 * NC)]
                xf = [sbp(f"A4f{i}", [128, TB], F32) for i in range(2)]
                rof = [sbp(f"A4r{i}", [128, TB], F32) for i in range(2)]
                for tb in range(NTB):
                    for c in range(NC):
                        p.op("sync", lambda e, c=c, tb=tb: e.dma_start(
                            out=oin[c][:], in_=OIN[128 * c:128 * (c + 1), TB * tb:TB * (tb + 1)]),
                            r=[("OIN", c, tb)], w=[("A4.oin", c)], dma=True)
                        p.op("sync", lambda e, c=c, tb=tb: e.dma_start(
                            out=xs[c][:], in_=XS[128 * c:128 * (c + 1), TB * tb:TB * (tb + 1)]),
                            r=[("XS", c, tb)], w=[("A4.xs", c)], dma=True)
                    for og in range(4):
                        par = og % 2
                        for c in range(NC):
                            p.op("gpsimd", lambda e, c=c, og=og, par=par: e.dma_start(
                                out=wob[NC * par + c][:], in_=w_out[128 * c:128 * (c + 1), TB * og:TB * (og + 1)]),
                                w=[("A4.wo", NC * par + c)], dma=True)
                        for j in range(4):
                            m = 4 * og + j
                            pp = (ps0, ps1)[j % 2]
                            def fm(t, pp=pp, par=par, j=j):
                                for c in range(NC):
                                    ins = t.matmul(pp[:, 0:TB], wob[NC * par + c][:, 128 * j:128 * (j + 1)],
                                                   oin[c][:], start=(c == 0), stop=(c == NC - 1))
                                return ins
                            p.op("tensor", fm,
                                 r=[("A4.wo", NC * par + c) for c in range(NC)] + [("A4.oin", c) for c in range(NC)],
                                 w=[pp.k])
                            jp = j % 2
                            p.op("sync", lambda e, m=m, tb=tb, jp=jp: e.dma_start(
                                out=xf[jp][:], in_=xT[128 * m:128 * (m + 1), TB * tb:TB * (tb + 1)]),
                                w=[("A4.xf", jp)], dma=True)
                            p.op("vector", lambda v, pp=pp, m=m, jp=jp: v.scalar_tensor_tensor(
                                rof[jp][:], pp[:, 0:TB], 1.0, xs[m][:], ALU.mult, ALU.add),
                                r=[pp.k, ("A4.xs", m)], w=[("A4.ro", jp)])
                            p.op("vector", lambda v, jp=jp: v.tensor_add(rof[jp][:], rof[jp][:], xf[jp][:]),
                                 r=[("A4.ro", jp), ("A4.xf", jp)], w=[("A4.ro", jp)])
                            p.op("scalar", lambda e, m=m, tb=tb, jp=jp: e.dma_start(
                                out=RA[128 * m:128 * (m + 1), TB * tb:TB * (tb + 1)], in_=rof[jp][:]),
                                r=[("A4.ro", jp)], w=[(res_key["A"], m, tb)], dma=True)
                            tick_v(8)

        dc_iv = [sb(f"DCiv{i}", [128, INTER // 2], BF16) for i in range(2)]
        dc_pv = sb("DCpv", [128, INTER // 2], BF16)
        dc_av = [sb(f"DCav{i}", [128, INTER // 2], BF16) for i in range(2)]
        dc_ig = [sb(f"DCig{i}", [128, D], BF16) for i in range(2)]
        dc_pg = sb("DCpg", [128, D], BF16)
        dc_ag = [sb(f"DCag{i}", [128, D], BF16) for i in range(2)]

        def decode_gen(eng, ld_eng, st_eng, idxb, prodb, accb, blocks):
            def gen():
                hb = 0
                for (srcv, dstv, F, wkey) in blocks:
                    par = hb % 2
                    hb += 1
                    ib, ac = idxb[par], accb[par]
                    ibk = ("DC", eng, par, "ib")
                    ack = ("DC", eng, par, "ac")
                    prk = ("DC", eng, "pr")
                    p.op(ld_eng, lambda e, srcv=srcv, ib=ib, F=F: e.dma_start(
                        out=ib[:, 0:F], in_=srcv),
                        w=[ibk], dma=True)
                    yield
                    p.op(eng, lambda v, ib=ib, ac=ac, F=F: v.tensor_scalar(
                        ac[:, 0:F], ib[:, 0:F], 0.0, lutbc[:, 0:1],
                        ALU.is_equal, ALU.mult),
                        r=[ibk, LUTK], w=[ack])
                    yield
                    for j in range(1, 16):
                        p.op(eng, lambda v, ib=ib, F=F, j=j: v.tensor_scalar(
                            prodb[:, 0:F], ib[:, 0:F], float(j), lutbc[:, j:j + 1],
                            ALU.is_equal, ALU.mult),
                            r=[ibk, LUTK], w=[prk])
                        yield
                        p.op(eng, lambda v, ac=ac, F=F: v.tensor_add(
                            ac[:, 0:F], ac[:, 0:F], prodb[:, 0:F]),
                            r=[ack, prk], w=[ack])
                        yield
                    p.op(st_eng, lambda e, dstv=dstv, ac=ac, F=F: e.dma_start(
                        out=dstv, in_=ac[:, 0:F]),
                        r=[ack], w=[wkey], dma=True)
                    yield
            return gen()

        HB0 = INTER // 2
        blk_gu = []
        for h in range(2):
            lo, hi = HB0 * h, HB0 * (h + 1)
            for c in range(NC):
                blk_gu.append((gidxT[128 * c:128 * (c + 1), lo:hi],
                               WG[128 * c:128 * (c + 1), lo:hi], HB0, ("WG", c, h)))
                blk_gu.append((uidxT[128 * c:128 * (c + 1), lo:hi],
                               WU[128 * c:128 * (c + 1), lo:hi], HB0, ("WU", c, h)))
        blk_d = [(didxT[128 * o:128 * (o + 1), :], WD[128 * o:128 * (o + 1), :],
                  D, ("WD", o)) for o in range(NO)]
        gen_v = decode_gen("vector", "gpsimd", "scalar", dc_iv, dc_pv, dc_av, blk_gu)
        gen_g = decode_gen("vector", "gpsimd", "sync", dc_ig, dc_pg, dc_ag, blk_d)
        n_v_left = [len(blk_gu) * 33]

        def tick_v(n):
            for _ in range(n):
                if next(gen_v, "end") == "end":
                    if next(gen_g, "end") == "end":
                        return
                else:
                    n_v_left[0] -= 1
        def drain_v_h0(n=10 ** 9):
            while n_v_left[0] > len(blk_gu) // 2 * 33:
                tick_v(1)
        def tick_g(n):
            for _ in range(n):
                if next(gen_g, "end") == "end":
                    return

        def ffn_phase(tag, src_dram, src_keyf, dst_dram, dst_keyn):
            with contextlib.ExitStack() as ph:
                p.barrier()
                sbp = lambda name, shape, dt: ph.enter_context(nc.sbuf_tensor(name, list(shape), dt))
                psA = PS(ph, nc, tag + "p0"); psB = PS(ph, nc, tag + "p1")
                psC = PS(ph, nc, tag + "p2"); psD = PS(ph, nc, tag + "p3")
                xn = [sbp(f"{tag}xn{i}", [128, T], BF16) for i in range(NC)]
                xnk = ln_phase(tag + "0", src_dram, src_keyf, xn, ph, psA, psB, psC,
                               tick=lambda n: tick_v(2))
                wgb = [sbp(f"{tag}wg{i}", [128, TB], BF16) for i in range(2 * NC)]
                wub = [sbp(f"{tag}wu{i}", [128, TB], BF16) for i in range(2 * NC)]
                hg = [sbp(f"{tag}hg{i}", [128, TB], BF16) for i in range(2)]
                hb = [sbp(f"{tag}hb{i}", [128, TB], BF16) for i in range(2)]
                psrot = [psA, psB, psC, psD]
                for og in range(11):
                    if og == 5:
                        tick_v(10 ** 9)
                    par = og % 2
                    ow = 512 if og < 10 else INTER - 5120
                    nj = 4 if og < 10 else 3
                    hks = [0] if og < 5 else ([0, 1] if og == 5 else [1])
                    for c in range(NC):
                        p.op("sync", lambda e, c=c, og=og, par=par, ow=ow: e.dma_start(
                            out=wgb[NC * par + c][:, 0:ow], in_=WG[128 * c:128 * (c + 1), 512 * og:512 * og + ow]),
                            r=[("WG", c, h) for h in hks], w=[(tag, "wg", NC * par + c)], dma=True)
                        p.op("sync", lambda e, c=c, og=og, par=par, ow=ow: e.dma_start(
                            out=wub[NC * par + c][:, 0:ow], in_=WU[128 * c:128 * (c + 1), 512 * og:512 * og + ow]),
                            r=[("WU", c, h) for h in hks], w=[(tag, "wu", NC * par + c)], dma=True)
                    for j in range(nj):
                        ot = 4 * og + j
                        for tb in range(NTB):
                            pg = psrot[(2 * (ot * NTB + tb)) % 4]
                            pu = psrot[(2 * (ot * NTB + tb) + 1) % 4]
                            def fg(t, par=par, j=j, tb=tb, pg=pg):
                                for c in range(NC):
                                    ins = t.matmul(pg[:, 0:TB], wgb[NC * par + c][:, 128 * j:128 * (j + 1)],
                                                   xn[c][:, TB * tb:TB * (tb + 1)],
                                                   start=(c == 0), stop=(c == NC - 1))
                                return ins
                            p.op("tensor", fg,
                                 r=[(tag, "wg", NC * par + c) for c in range(NC)] + [xnk(c, tb) for c in range(NC)],
                                 w=[pg.k])
                            def fu(t, par=par, j=j, tb=tb, pu=pu):
                                for c in range(NC):
                                    ins = t.matmul(pu[:, 0:TB], wub[NC * par + c][:, 128 * j:128 * (j + 1)],
                                                   xn[c][:, TB * tb:TB * (tb + 1)],
                                                   start=(c == 0), stop=(c == NC - 1))
                                return ins
                            p.op("tensor", fu,
                                 r=[(tag, "wu", NC * par + c) for c in range(NC)] + [xnk(c, tb) for c in range(NC)],
                                 w=[pu.k])
                            hpar = (ot * NTB + tb) % 2
                            p.op("scalar", lambda s, hpar=hpar, pg=pg: s.activation(hg[hpar][:], pg[:, 0:TB], AF.Silu),
                                 r=[pg.k], w=[(tag, "hg", hpar)])
                            p.op("vector", lambda v, hpar=hpar, pu=pu: v.tensor_mul(hb[hpar][:], hg[hpar][:], pu[:, 0:TB]),
                                 r=[(tag, "hg", hpar), pu.k], w=[(tag, "hb", hpar)])
                            p.op("scalar", lambda e, ot=ot, tb=tb, hpar=hpar: e.dma_start(
                                out=HB[128 * ot:128 * (ot + 1), TB * tb:TB * (tb + 1)], in_=hb[hpar][:]),
                                r=[(tag, "hb", hpar)], w=[(tag, "HB", ot, tb)], dma=True)
                            tick_v(8)
            tick_g(10 ** 9)
            with contextlib.ExitStack() as ph:
                p.barrier()
                sbp = lambda name, shape, dt: ph.enter_context(nc.sbuf_tensor(name, list(shape), dt))
                pso = [PS(ph, nc, f"{tag}q{i}") for i in range(8)]
                hcol = [sbp(f"{tag}hc{i}", [128, 1024], BF16) for i in range(NO)]
                r1t = [sbp(f"{tag}r1{i}", [128, TB], F32) for i in range(4)]
                wdb = [sbp(f"{tag}wd{i}", [128, TB], BF16) for i in range(2)]
                ev = [sbp(f"{tag}ev{i}", [128, TB], F32) for i in range(2)]
                for th in range(2):
                    for o in range(NO):
                        p.op("sync", lambda e, o=o, th=th: e.dma_start(
                            out=hcol[o][:], in_=HB[128 * o:128 * (o + 1), 1024 * th:1024 * (th + 1)]),
                            r=[(tag, "HB", o, 2 * th), (tag, "HB", o, 2 * th + 1)],
                            w=[(tag, "hc", o)], dma=True)
                    for dmg in range(4):
                        for o in range(NO):
                            par = o % 2
                            p.op("gpsimd", lambda e, o=o, dmg=dmg, par=par: e.dma_start(
                                out=wdb[par][:], in_=WD[128 * o:128 * (o + 1), TB * dmg:TB * (dmg + 1)]),
                                r=[("WD", o)], w=[(tag, "wd", par)], dma=True)
                            def fd(t, o=o, par=par):
                                for dj in range(4):
                                    for t2 in range(2):
                                        ins = t.matmul(pso[2 * dj + t2][:, 0:TB],
                                                       wdb[par][:, 128 * dj:128 * (dj + 1)],
                                                       hcol[o][:, TB * t2:TB * (t2 + 1)],
                                                       start=(o == 0), stop=(o == NO - 1))
                                return ins
                            p.op("tensor", fd,
                                 r=[(tag, "wd", par), (tag, "hc", o)],
                                 w=[pso[i].k for i in range(8)])
                        for dj in range(4):
                            m = 4 * dmg + dj
                            for t2 in range(2):
                                epar = t2
                                tb = 2 * th + t2
                                rb = (2 * dj + t2) % 4
                                p.op("sync", lambda e, m=m, tb=tb, rb=rb: e.dma_start(
                                    out=r1t[rb][:], in_=src_dram[128 * m:128 * (m + 1), TB * tb:TB * (tb + 1)]),
                                    r=[src_keyf(m, tb)], w=[(tag, "r1", rb)], dma=True)
                                p.op("vector", lambda v, dj=dj, t2=t2, m=m, epar=epar, rb=rb: v.scalar_tensor_tensor(
                                    ev[epar][:], pso[2 * dj + t2][:, 0:TB], 1.0,
                                    r1t[rb][:], ALU.mult, ALU.add),
                                    r=[pso[2 * dj + t2].k, (tag, "r1", rb)], w=[(tag, "ev", epar)])
                                p.op("scalar", lambda e, m=m, tb=tb, epar=epar: e.dma_start(
                                    out=dst_dram[128 * m:128 * (m + 1), TB * tb:TB * (tb + 1)], in_=ev[epar][:]),
                                    r=[(tag, "ev", epar)], w=[(dst_keyn, m, tb)], dma=True)

        def phase_C(RC):
            with contextlib.ExitStack() as ph:
                p.barrier()
                sbp = lambda name, shape, dt: ph.enter_context(nc.sbuf_tensor(name, list(shape), dt))
                psA = PS(ph, nc, "pC0"); psB = PS(ph, nc, "pC1"); psC = PS(ph, nc, "pC2")
                xn = [sbp(f"Cxn{i}", [128, T], BF16) for i in range(NC)]
                xnk = ln_phase("C0", R2, lambda c, tb: ("RES_B", c, tb), xn, ph, psA, psB, psC)
                wpb = [sbp(f"Cwp{i}", [128, TB], BF16) for i in range(2 * NC)]
                evb = [sbp(f"Cev{i}", [128, TB], BF16) for i in range(2)]
                za = [sbp(f"Cza{i}", [128, TB], F32) for i in range(2)]
                zm = [sbp(f"Czm{i}", [128, TB], F32) for i in range(2)]
                bgt = [sbp(f"Cbg{i}", [128, NC], F32) for i in range(2)]
                p.op("sync", lambda e: e.dma_start(out=bgt[0][:], in_=bg1), w=[("C.bg", 0)], dma=True)
                p.op("sync", lambda e: e.dma_start(out=bgt[1][:], in_=bg2), w=[("C.bg", 1)], dma=True)
                projs = [(wqT, QF, "QF", None), (wkT, KF, "KF", None), (wvT, VF, "VF", None),
                         (wg1T, G1, "G1", 0), (wg2T, G2, "G2", 1)]
                if 1 not in c_sub:
                    projs = []
                elif 4 in c_sub:
                    projs = projs[:3]
                elif 5 in c_sub:
                    projs = projs[3:4]
                elif 6 in c_sub:
                    projs = projs[:4]
                elif 7 in c_sub:
                    projs = projs[3:5]
                for wsrc, dst, dkey, gate_i in projs:
                    for og in range(4):
                        par = og % 2
                        for c in range(NC):
                            p.op("gpsimd", lambda e, c=c, og=og, par=par, wsrc=wsrc: e.dma_start(
                                out=wpb[NC * par + c][:], in_=wsrc[128 * c:128 * (c + 1), TB * og:TB * (og + 1)]),
                                w=[("C.wp", NC * par + c)], dma=True)
                        for j in range(4):
                            it = 4 * og + j
                            for tb in range(NTB):
                                pp = (psA, psB)[(j * NTB + tb) % 2]
                                def fp_(t, par=par, j=j, tb=tb, pp=pp):
                                    for c in range(NC):
                                        ins = t.matmul(pp[:, 0:TB], wpb[NC * par + c][:, 128 * j:128 * (j + 1)],
                                                       xn[c][:, TB * tb:TB * (tb + 1)],
                                                       start=(c == 0), stop=(c == NC - 1))
                                    return ins
                                p.op("tensor", fp_,
                                     r=[("C.wp", NC * par + c) for c in range(NC)] + [xnk(c, tb) for c in range(NC)],
                                     w=[pp.k])
                                ep = (j * NTB + tb) % 2
                                if gate_i is None:
                                    p.op("scalar", lambda s, pp=pp, ep=ep: s.copy(evb[ep][:], pp[:, 0:TB]),
                                         r=[pp.k], w=[("C.ev", ep)])
                                elif 8 in c_sub:
                                    gi = gate_i
                                    p.op("vector", lambda v, pp=pp, ep=ep, it=it, gi=gi: v.tensor_scalar(
                                        evb[ep][:], pp[:, 0:TB], bgt[gi][:, it:it + 1], 0.0, ALU.add, ALU.min),
                                        r=[pp.k, ("C.bg", gi)], w=[("C.ev", ep)])
                                elif 9 in c_sub:
                                    gi = gate_i
                                    p.op("scalar", lambda s, pp=pp, ep=ep, it=it, gi=gi: s.activation(
                                        za[ep][:], pp[:, 0:TB], AF.Abs, bias=bgt[gi][:, it:it + 1]),
                                        r=[pp.k, ("C.bg", gi)], w=[("C.za", ep)])
                                    p.op("vector", lambda v, ep=ep: v.tensor_copy(evb[ep][:], za[ep][:]),
                                         r=[("C.za", ep)], w=[("C.ev", ep)])
                                else:
                                    gi = gate_i
                                    p.op("scalar", lambda s, pp=pp, ep=ep, it=it, gi=gi: s.activation(
                                        za[ep][:], pp[:, 0:TB], AF.Abs, bias=bgt[gi][:, it:it + 1]),
                                        r=[pp.k, ("C.bg", gi)], w=[("C.za", ep)])
                                    p.op("scalar", lambda s, ep=ep: s.activation(za[ep][:], za[ep][:], AF.Exp, scale=-1.0),
                                         r=[("C.za", ep)], w=[("C.za", ep)])
                                    p.op("scalar", lambda s, ep=ep: s.activation(za[ep][:], za[ep][:], AF.Ln, bias=onecol[:, 0:1]),
                                         r=[("C.za", ep), ("onec",)], w=[("C.za", ep)])
                                    p.op("vector", lambda v, pp=pp, ep=ep, it=it, gi=gi: v.tensor_scalar(
                                        zm[ep][:], pp[:, 0:TB], bgt[gi][:, it:it + 1], 0.0, ALU.add, ALU.min),
                                        r=[pp.k, ("C.bg", gi)], w=[("C.zm", ep)])
                                    p.op("vector", lambda v, ep=ep: v.tensor_sub(zm[ep][:], zm[ep][:], za[ep][:]),
                                         r=[("C.zm", ep), ("C.za", ep)], w=[("C.zm", ep)])
                                    p.op("vector", lambda v, ep=ep: v.tensor_scalar(
                                        evb[ep][:], zm[ep][:], 1.0 / 16.0, -50.0, ALU.mult, ALU.max),
                                        r=[("C.zm", ep)], w=[("C.ev", ep)])
                                p.op("scalar", lambda e, dst=dst, it=it, tb=tb, ep=ep: e.dma_start(
                                    out=dst[128 * it:128 * (it + 1), TB * tb:TB * (tb + 1)], in_=evb[ep][:]),
                                    r=[("C.ev", ep)], w=[(dkey, it, tb)], dma=True)

            with contextlib.ExitStack() as ph:
                p.barrier()
                sbp = lambda name, shape, dt: ph.enter_context(nc.sbuf_tensor(name, list(shape), dt))
                psS_ = [PS(ph, nc, f"pS{i}") for i in range(2)]
                psO_ = [PS(ph, nc, f"pO{i}") for i in range(2)]
                psSD_ = [PS(ph, nc, f"pSD{i}") for i in range(2)]
                psT1 = PS(ph, nc, "pT1", BF16); psT2 = PS(ph, nc, "pT2", BF16)
                psAl = psS_[0]
                qc = [sbp(f"Cq{i}", [128, T], BF16) for i in range(2)]
                kc = [sbp(f"Ck{i}", [128, T], BF16) for i in range(2)]
                vc = [sbp(f"Cv{i}", [128, T], BF16) for i in range(2)]
                g1c = [sbp(f"Cg1{i}", [128, T], BF16) for i in range(2)]
                g2c = [sbp(f"Cg2{i}", [128, T], BF16) for i in range(2)]
                vtm = sbp("Cvtm", [128, T], BF16)
                ocomb = sbp("Coc", [128, T], BF16)
                st_ = [sbp(f"Cst{i}", [128, T], F32) for i in range(2)]
                NRT = 8
                Gt_ = [sbp(f"CG{i}", [128, 128], F32) for i in range(NRT)]
                eq_ = [sbp(f"Ceq{i}", [128, 128], BF16) for i in range(NRT)]
                ek_ = [sbp(f"Cek{i}", [128, 128], BF16) for i in range(NRT)]
                qls_ = [sbp(f"Cqls{i}", [128, 128], BF16) for i in range(NRT)]
                kx_ = [sbp(f"Ckx{i}", [128, 128], BF16) for i in range(NRT)]
                kd_ = [sbp(f"Ckd{i}", [128, 128], BF16) for i in range(NRT)]
                ek2_ = [sbp(f"Cek2{i}", [128, 128], BF16) for i in range(NRT)]
                kdtm_ = [sbp(f"Ckdt{i}", [128, 128], BF16) for i in range(NRT)]
                Ssb_ = [sbp(f"CSsb{i}", [128, 128], BF16) for i in range(NRT)]
                ccol_ = [sbp(f"Ccc{i}", [128, 4], F32) for i in range(NRT)]
                stbf_ = [sbp(f"Cstb{i}", [128, 128], BF16) for i in range(NRT)]
                ob_ = [sbp(f"Cob{i}", [128, 128], BF16) for i in range(2)]
                al = sbp("Cal", [1, 2], F32)
                al2 = sbp("Cal2", [1, 2], F32)
                mx1 = sbp("Cmx", [1, 1], F32)
                a01 = sbp("Ca01", [128, 2], F32)
                p.op("sync", lambda e: e.dma_start(out=al[:], in_=alpha), w=[("C.al",)], dma=True)
                p.op("vector", lambda v: v.reduce_max(mx1[:], al[:], axis=mybir.AxisListType.X),
                     r=[("C.al",)], w=[("C.mx",)])
                p.op("vector", lambda v: v.tensor_scalar(al2[:], al[:], mx1[0:1, 0:1], None, ALU.subtract),
                     r=[("C.al",), ("C.mx",)], w=[("C.al2",)])
                p.op("scalar", lambda s: s.activation(al2[:], al2[:], AF.Exp),
                     r=[("C.al2",)], w=[("C.al2",)])
                p.op("vector", lambda v: v.reduce_sum(mx1[:], al2[:], axis=mybir.AxisListType.X),
                     r=[("C.al2",)], w=[("C.mx",)])
                p.op("vector", lambda v: v.reciprocal(mx1[:], mx1[:]), r=[("C.mx",)], w=[("C.mx",)])
                p.op("vector", lambda v: v.tensor_scalar(al2[:], al2[:], mx1[0:1, 0:1], None, ALU.mult),
                     r=[("C.al2",), ("C.mx",)], w=[("C.al2",)])
                p.op("tensor", lambda t: t.matmul(psAl[:, 0:2], ones_k1_f[:], al2[:], start=True, stop=True),
                     r=[("C.al2",), ("o2",)], w=[psAl.k])
                p.op("scalar", lambda s: s.copy(a01[:], psAl[:, 0:2]), r=[psAl.k], w=[("C.a01",)])

                LNS = float(-0.5 * np.log(128.0))
                p.op("gpsimd", lambda g: g.memset(st_[0][:], 0.0), w=[("C.st", 0, h2) for h2 in range(NC)])
                p.op("gpsimd", lambda g: g.memset(st_[1][:], 0.0), w=[("C.st", 1, h2) for h2 in range(NC)])
                NCH = T // 128
                ld_engs = ("sync", "scalar", "gpsimd")
                for ch in (range(NCH) if 2 in c_sub else []):
                    cb = ch % 2
                    sl = slice(128 * ch, 128 * (ch + 1))
                    for ti, (buf, dsrc, dkey) in enumerate(((qc, QF, "QF"), (kc, KF, "KF"), (vc, VF, "VF"),
                                                            (g1c, G1, "G1"), (g2c, G2, "G2"))):
                        for h in range(NC):
                            eng = ld_engs[(ti * NC + h) % 3]
                            p.op(eng, lambda e, buf=buf, dsrc=dsrc, h=h, cb=cb, sl=sl: e.dma_start(
                                out=buf[cb][:, 128 * h:128 * (h + 1)], in_=dsrc[128 * h:128 * (h + 1), sl]),
                                r=[(dkey, h, sl.start // TB)], w=[("C." + dkey, cb, h)], dma=True)
                    chains = []
                    for h in range(NC):
                        for inst in range(2):
                            cp = (2 * h + inst) % NRT
                            chains.append(dict(
                                h=h, inst=inst, cp=cp, hs=slice(128 * h, 128 * (h + 1)),
                                gh=(g1c, g2c)[inst],
                                gk=("C." + ("G1", "G2")[inst], cb, h),
                                stk=("C.st", inst, h),
                                stv=st_[inst][:, 128 * h:128 * (h + 1)],
                                Gt=Gt_[cp], eq=eq_[cp], ek=ek_[cp], qls=qls_[cp], kx=kx_[cp],
                                kd=kd_[cp], kdtm=kdtm_[cp], Ssb=Ssb_[cp], ccol=ccol_[cp],
                                stbf=stbf_[cp],
                                psS=psS_[cp % 2], psO=psO_[cp % 2], psSD=psSD_[cp % 2],
                                KG=("C.Gt", cp), KEQ=("C.eq", cp), KEK=("C.ek", cp),
                                KQL=("C.qls", cp), KKX=("C.kx", cp), KKD=("C.kd", cp),
                                KKDT=("C.kdtm", cp), KSS=("C.Ssb", cp), KSB=("C.stbf", cp),
                                KEK2=("C.ek2", cp), ek2=ek2_[cp]))
                    for wv in range(4):
                      chains_w = chains[8 * wv:8 * wv + 8]
                      for ii, x in enumerate(chains_w):
                          p.op("vector", lambda v, x=x, cb=cb: v.tensor_tensor_scan(
                              x["Gt"][:], x["gh"][cb][:, x["hs"]], zeros512[:, 0:128], 0.0,
                              ALU.add, ALU.add),
                              r=[x["gk"], ("z5",)], w=[x["KG"]])
                          if x["inst"] == 0:
                              pst = (psT1, psT2)[x["h"] % 2]
                              p.op("tensor", lambda t, x=x, cb=cb, pst=pst: t.transpose(
                                  pst[:, 0:128], vc[cb][:, x["hs"]], ident[:]),
                                  r=[("C.VF", cb, x["h"]), ("ident",)], w=[pst.k])
                              p.op("scalar", lambda s, x=x, pst=pst: s.copy(vtm[:, x["hs"]], pst[:, 0:128]),
                                   r=[pst.k], w=[("C.vtm", x["hs"].start)])
                      for x in chains_w:
                          p.op("vector", lambda v, x=x: v.tensor_scalar(
                              x["ccol"][:, 0:1], x["Gt"][:, 63:64], -1.0, LNS, ALU.mult, ALU.add),
                              r=[x["KG"]], w=[("C.cc", x["cp"], 0)])
                      for x in chains:
                          p.op("scalar", lambda s, x=x: s.activation(
                              x["eq"][:], x["Gt"][:], AF.Exp, bias=x["ccol"][:, 0:1]),
                              r=[x["KG"], ("C.cc", x["cp"], 0)], w=[x["KEQ"]])
                          p.op("scalar", lambda s, x=x: s.activation(
                              x["ek"][:], x["Gt"][:], AF.Exp, scale=-1.0, bias=x["Gt"][:, 63:64]),
                              r=[x["KG"]], w=[x["KEK"]])
                          p.op("scalar", lambda s, x=x: s.activation(
                              x["ek2"][:], x["Gt"][:], AF.Exp, scale=-1.0, bias=x["Gt"][:, 127:128]),
                              r=[x["KG"]], w=[x["KEK2"]])
                          p.op("scalar", lambda s, x=x: s.activation(
                              x["ccol"][:, 1:3], x["Gt"][:, 63:128:64], AF.Exp),
                              r=[x["KG"]], w=[("C.cc", x["cp"], 1), ("C.cc", x["cp"], 2)])
                      for x in chains_w:
                          p.op("gpsimd", lambda g, x=x, cb=cb: g.tensor_mul(
                              x["qls"][:], qc[cb][:, x["hs"]], x["eq"][:]),
                              r=[("C.QF", cb, x["h"]), x["KEQ"]], w=[x["KQL"]])
                          p.op("gpsimd", lambda g, x=x, cb=cb: g.tensor_mul(
                              x["kx"][:], kc[cb][:, x["hs"]], x["ek"][:]),
                              r=[("C.KF", cb, x["h"]), x["KEK"]], w=[x["KKX"]])
                          p.op("gpsimd", lambda g, x=x, cb=cb: g.tensor_mul(
                              x["kd"][:], kc[cb][:, x["hs"]], x["ek2"][:]),
                              r=[("C.KF", cb, x["h"]), x["KEK2"]], w=[x["KKD"]])
                      for x in chains_w:
                          p.op("tensor", lambda t, x=x: t.matmul(
                              x["psS"][:, 0:128], x["kx"][:], x["qls"][:], start=True, stop=True),
                              r=[x["KKX"], x["KQL"]], w=[x["psS"].k])
                          p.op("vector", lambda v, x=x: v.scalar_tensor_tensor(
                              x["Ssb"][:], x["psS"][:, 0:128], 1.0, mask_ut[:], ALU.mult, ALU.mult),
                              r=[x["psS"].k, ("mut",)], w=[x["KSS"]])
                          p.op("vector", lambda v, x=x: v.tensor_scalar(
                              x["stbf"][:], x["stv"], x["ccol"][:, 1:2], None, ALU.mult),
                              r=[x["stk"], ("C.cc", x["cp"], 1)], w=[x["KSB"]])
                      for x in chains_w:
                          def fo(t, x=x):
                              t.matmul(x["psO"][:, 0:128], vtm[:, x["hs"]], x["Ssb"][:],
                                       start=True, stop=False)
                              return t.matmul(x["psO"][:, 0:128], x["stbf"][:], x["qls"][:],
                                              start=False, stop=True)
                          p.op("tensor", fo, r=[("C.vtm", x["hs"].start), x["KSS"], x["KSB"], x["KQL"]],
                               w=[x["psO"].k])
                          if x["inst"] == 0:
                              p.op("vector", lambda v, x=x: v.tensor_scalar(
                                  ocomb[:, x["hs"]], x["psO"][:, 0:128], a01[:, 0:1], None, ALU.mult),
                                  r=[x["psO"].k, ("C.a01",)], w=[("C.oc", x["hs"].start)])
                          else:
                              obp = x["h"] % 2
                              p.op("vector", lambda v, x=x, obp=obp: v.scalar_tensor_tensor(
                                  ob_[obp][:], x["psO"][:, 0:128], a01[:, 1:2], ocomb[:, x["hs"]],
                                  ALU.mult, ALU.add),
                                  r=[x["psO"].k, ("C.a01",), ("C.oc", x["hs"].start)], w=[("C.ob", obp)])
                              p.op("scalar", lambda e, x=x, sl=sl, obp=obp: e.dma_start(
                                  out=OIN[128 * x["h"]:128 * (x["h"] + 1), sl], in_=ob_[obp][:]),
                                  r=[("C.ob", obp)], w=[("C.attn", x["h"], sl.start)], dma=True)
                      for ii, x in enumerate(chains_w):
                          pst = (psT1, psT2)[ii % 2]
                          p.op("tensor", lambda t, x=x, pst=pst: t.transpose(
                              pst[:, 0:128], x["kd"][:], ident[:]),
                              r=[x["KKD"], ("ident",)], w=[pst.k])
                          p.op("vector", lambda v, x=x, pst=pst: v.tensor_copy(x["kdtm"][:], pst[:, 0:128]),
                               r=[pst.k], w=[x["KKDT"]])
                          p.op("tensor", lambda t, x=x: t.matmul(
                              x["psSD"][:, 0:128], x["kdtm"][:], vtm[:, x["hs"]], start=True, stop=True),
                              r=[x["KKDT"], ("C.vtm", x["hs"].start)], w=[x["psSD"].k])
                          p.op("vector", lambda v, x=x: v.scalar_tensor_tensor(
                              x["stv"], x["stv"], x["ccol"][:, 2:3], x["psSD"][:, 0:128],
                              ALU.mult, ALU.add),
                              r=[x["stk"], ("C.cc", x["cp"], 2), x["psSD"].k], w=[x["stk"]])

                wob = [sbp(f"C3w{i}", [128, TB], BF16) for i in range(NC)]
                atb = [sbp(f"C3a{i}", [128, TB], BF16) for i in range(NC)]
                rf = [sbp(f"C3r{i}", [128, TB], F32) for i in range(2)]
                ev3 = [sbp(f"C3e{i}", [128, TB], F32) for i in range(2)]
                for tb in (range(NTB) if 3 in c_sub else []):
                    for c in range(NC):
                        p.op("sync", lambda e, c=c, tb=tb: e.dma_start(
                            out=atb[c][:], in_=OIN[128 * c:128 * (c + 1), TB * tb:TB * (tb + 1)]),
                            r=[("C.attn", c, 128 * q) for q in range(4 * tb, 4 * tb + 4)],
                            w=[("C3.at", c)], dma=True)
                    for og in range(4):
                        par = og % 2
                        for c in range(NC):
                            p.op("gpsimd", lambda e, c=c, og=og: e.dma_start(
                                out=wob[c][:], in_=woT[128 * c:128 * (c + 1), TB * og:TB * (og + 1)]),
                                w=[("C3.wo", c)], dma=True)
                        for j in range(4):
                            m = 4 * og + j
                            pp = (psS_[0], psO_[0])[j % 2]
                            def fm3(t, pp=pp, par=par, j=j, tb=tb):
                                for c in range(NC):
                                    ins = t.matmul(pp[:, 0:TB], wob[c][:, 128 * j:128 * (j + 1)],
                                                   atb[c][:],
                                                   start=(c == 0), stop=(c == NC - 1))
                                return ins
                            p.op("tensor", fm3,
                                 r=[("C3.wo", c) for c in range(NC)]
                                   + [("C3.at", c) for c in range(NC)],
                                 w=[pp.k])
                            jp = j % 2
                            p.op("sync", lambda e, m=m, tb=tb, jp=jp: e.dma_start(
                                out=rf[jp][:], in_=R2[128 * m:128 * (m + 1), TB * tb:TB * (tb + 1)]),
                                r=[("RES_B", m, tb)], w=[("C3.rf", jp)], dma=True)
                            p.op("vector", lambda v, pp=pp, jp=jp: v.scalar_tensor_tensor(
                                ev3[jp][:], pp[:, 0:TB], 1.0, rf[jp][:], ALU.mult, ALU.add),
                                r=[pp.k, ("C3.rf", jp)], w=[("C3.ev", jp)])
                            p.op("scalar", lambda e, m=m, tb=tb, jp=jp: e.dma_start(
                                out=RC[128 * m:128 * (m + 1), TB * tb:TB * (tb + 1)], in_=ev3[jp][:]),
                                r=[("C3.ev", jp)], w=[("RES_C", m, tb)], dma=True)

        if "A" in phases:
            phase_A01(res_out["A"])
            phase_A2()
            phase_A3()
            phase_A4(res_out["A"])
        if "B" in phases or "D" in phases:
            drain_v_h0()
        if "B" in phases:
            ffn_phase("B", R1, lambda c, tb: ("RES_A", c, tb), res_out["B"], "RES_B")
        if "C" in phases:
            phase_C(res_out["C"])
        if "D" in phases:
            ffn_phase("Dp", R3, lambda c, tb: ("RES_C", c, tb), res_out["D"], "RES_D")

        lk = res_key[last_phase]
        final_keys = [(lk, m, tb) for m in range(NC) for tb in range(NTB)]
        final_keys = [k for k in final_keys if k in p.lastw]
        p.barrier()
        p.op("vector", lambda v: v.nop(), r=final_keys, w=[])

        mk, counts = p.finalize(sems)
        block = st.enter_context(nc.Block())
        block.sync(mk("sync"))
        block.tensor(mk("tensor"))
        block.vector(mk("vector"))
        block.scalar(mk("scalar"))
        block.gpsimd(mk("gpsimd"))
    return nc, counts


def host_inputs(inputs, core):
    d = {}
    d["xT"] = np.ascontiguousarray(inputs["L0_attn_a_x"][core].T)
    d["w_in"] = inputs["L0_attn_a_W_in"]
    cw = inputs["L0_attn_a_conv_w"]
    d["cw_h"] = np.ascontiguousarray(cw.transpose(2, 1, 0).reshape(4, D, NC, 128).transpose(2, 0, 1, 3))
    d["conv_b"] = np.ascontiguousarray(inputs["L0_attn_a_conv_b"].reshape(NC, 128).T)
    d["w_x"] = inputs["L0_attn_a_W_x"]
    d["w_dt"] = inputs["L0_attn_a_W_dt"]
    d["b_dt"] = np.ascontiguousarray(inputs["L0_attn_a_b_dt"].reshape(NC, 128).T)
    d["a_log"] = np.ascontiguousarray(inputs["L0_attn_a_A_log"].reshape(NC, 128, SSN).transpose(1, 0, 2).reshape(128, NC * SSN))
    d["d_vec"] = np.ascontiguousarray(inputs["L0_attn_a_D"].reshape(NC, 128).T)
    d["w_out"] = inputs["L0_attn_a_W_out"]
    d["wqT"] = np.ascontiguousarray(inputs["L0_attn_b_Wq"].T)
    d["wkT"] = np.ascontiguousarray(inputs["L0_attn_b_Wk"].T)
    d["wvT"] = np.ascontiguousarray(inputs["L0_attn_b_Wv"].T)
    d["woT"] = np.ascontiguousarray(inputs["L0_attn_b_Wo"].T)
    d["wg1T"] = np.ascontiguousarray(inputs["L0_attn_b_Wg1"].T)
    d["wg2T"] = np.ascontiguousarray(inputs["L0_attn_b_Wg2"].T)
    d["bg1"] = np.ascontiguousarray(inputs["L0_attn_b_bg1"].reshape(NC, 128).T)
    d["bg2"] = np.ascontiguousarray(inputs["L0_attn_b_bg2"].reshape(NC, 128).T)
    d["alpha"] = np.ascontiguousarray(inputs["L0_attn_b_alpha_list"].reshape(1, 2))
    d["lut"] = np.ascontiguousarray(inputs["L0_ffn_lut"].reshape(1, 16))
    d["gidxT"] = np.ascontiguousarray(inputs["L0_ffn_gate_idx"].T)
    d["uidxT"] = np.ascontiguousarray(inputs["L0_ffn_up_idx"].T)
    d["didxT"] = np.ascontiguousarray(inputs["L0_ffn_down_idx"].T)
    return d


_PHASES = ("A", "B", "C", "D")


def kernel(**inputs):
    nc, _ = build_nc(_PHASES)
    in_maps = [host_inputs(inputs, i) for i in range(8)]
    res = run_bass_kernel_spmd(nc, in_maps, core_ids=list(range(8)))
    out = np.stack([np.ascontiguousarray(r["outT"].T) for r in res.results])
    return out.astype(np.float32)


# version 50
# speedup vs baseline: 1.1567x; 1.1567x over previous
import contextlib
import numpy as np
import concourse.bass as bass
import concourse.mybir as mybir
from concourse.bass_utils import run_bass_kernel_spmd

F32 = mybir.dt.float32
BF16 = mybir.dt.bfloat16
I32 = mybir.dt.int32
AF = mybir.ActivationFunctionType
ALU = mybir.AluOpType

D = 2048
T = 2048
NC = D // 128
TB = 512
NTB = T // TB
SSN = 16
INTER = 5504
NO = 43

ENGS = ("sync", "tensor", "vector", "scalar", "gpsimd")
DMA_ENGS = ("sync", "scalar", "gpsimd", "vector")


class Prog:

    NSLOT = 8

    def __init__(self):
        self.ops = []
        self.lastw = {}
        self.readers = {}
        self.last_per_eng = {}
        self.last_dma_slot = {}
        self.dma_ord = {}
        self.bar = None

    def op(self, eng, fn, r=(), w=(), dma=False, qchain=False):
        ps_r = [k for k in r if isinstance(k, tuple) and len(k) > 0 and k[0] == "PS"]
        if ps_r:
            r = [k for k in r if k not in ps_r]
            w = list(w) + ps_r
        deps = set()
        for k in r:
            pr = self.lastw.get(k)
            if pr is not None:
                deps.add(pr)
        for k in w:
            pr = self.lastw.get(k)
            if pr is not None:
                deps.add(pr)
            deps.update(self.readers.get(k, ()))
        if self.bar is not None:
            deps.add(self.bar)
        i = len(self.ops)
        deps.discard(i)
        slot = None
        if dma:
            o = self.dma_ord.get(eng, 0)
            self.dma_ord[eng] = o + 1
            slot = o % self.NSLOT
            self.last_dma_slot[(eng, slot)] = i
        else:
            self.last_per_eng[eng] = i
        self.ops.append([eng, fn, dma, deps, False, slot, qchain])
        for k in r:
            self.readers.setdefault(k, set()).add(i)
        for k in w:
            self.lastw[k] = i
            self.readers[k] = set()
        return i

    def barrier(self):
        deps = set(self.last_per_eng.values()) | set(self.last_dma_slot.values())
        i = len(self.ops)
        self.ops.append(["vector", lambda v: v.nop(), False, deps, False, None, False])
        self.last_per_eng["vector"] = i
        self.bar = i

    def finalize(self, sems):
        ops = self.ops
        for o in ops:
            for d in o[3]:
                ops[d][4] = True
        for o in ops:
            if o[2]:
                o[4] = True
        cnt = {}
        val = [None] * len(ops)
        for i, (eng, fn, dma, deps, need, slot, qc) in enumerate(ops):
            if need:
                key = ("d", eng, slot) if dma else ("c", eng)
                cnt[key] = cnt.get(key, 0) + (16 if dma else 1)
                val[i] = (key, cnt[key])
        streams = {e: [] for e in ENGS}
        for i, (eng, fn, dma, deps, need, slot, qc) in enumerate(ops):
            waits = {}
            for d in deps:
                if eng == "tensor" and not dma and ops[d][0] == "tensor" and not ops[d][2]:
                    continue
                assert val[d] is not None
                k, v = val[d]
                if waits.get(k, -1) < v:
                    waits[k] = v
            streams[eng].append((i, fn, sorted(waits.items()), val[i]))
        counts = {e: (len(streams[e]), sum(len(w) for _, _, w, _ in streams[e]))
                  for e in ENGS}

        def mk(eng_name):
            def emit(eng):
                known = {}
                for i, fn, waits, v in streams[eng_name]:
                    for k, req in waits:
                        if known.get(k, -1) < req:
                            eng.wait_ge(sems[k], req)
                            known[k] = req
                    inst = fn(eng)
                    if v is not None:
                        assert inst is not None, f"op {i} on {eng_name} needs inc but emitted nothing"
                        inst.then_inc(sems[v[0]], 16 if v[0][0] == "d" else 1)
            return emit
        return mk, counts


class PS:
    def __init__(self, stack, nc, name, dt=F32):
        n = 512 if dt == F32 else 1024
        self.t = stack.enter_context(nc.psum_tensor(name, [128, n], dt))
        self.k = ("PS", name)

    def __getitem__(self, sl):
        return self.t[sl]


def build_nc(phases=("A", "B", "C", "D"), dbg=False, c_sub=(1, 2, 3)):
    nc = bass.Bass("TRN2", target_bir_lowering=False)
    p = Prog()

    def din(name, shape, dt=F32):
        return nc.dram_tensor(name, list(shape), dt, kind="ExternalInput").ap()

    xT = din("xT", [D, T])
    w_in = din("w_in", [D, 2 * D])
    cw_h = din("cw_h", [NC, 4, D, 128])
    conv_b = din("conv_b", [128, NC])
    w_x = din("w_x", [D, 160])
    w_dt = din("w_dt", [128, D])
    b_dt = din("b_dt", [128, NC])
    a_log = din("a_log", [128, NC * SSN])
    d_vec = din("d_vec", [128, NC])
    w_out = din("w_out", [D, D])
    wqT = din("wqT", [D, D])
    wkT = din("wkT", [D, D])
    wvT = din("wvT", [D, D])
    woT = din("woT", [D, D])
    wg1T = din("wg1T", [D, D])
    wg2T = din("wg2T", [D, D])
    bg1 = din("bg1", [128, NC])
    bg2 = din("bg2", [128, NC])
    alpha = din("alpha", [1, 2])
    lut = din("lut", [1, 16])
    gidxT = din("gidxT", [D, INTER], I32)
    uidxT = din("uidxT", [D, INTER], I32)
    didxT = din("didxT", [INTER, D], I32)

    outT = nc.dram_tensor("outT", [D, T], F32, kind="ExternalOutput").ap()

    def scr(name, shape, dt, addr_space="Local"):
        kind = "ExternalOutput" if dbg else "Internal"
        return nc.dram_tensor(name, list(shape), dt, kind=kind, addr_space=addr_space).ap()

    XC = scr("XC", [D, 3 + T], BF16)
    if dbg:
        DBG_DL = scr("DBG_DL", [D, TB], BF16)
        DBG_DU = scr("DBG_DU", [D, TB], BF16)
        DBG_B = scr("DBG_B", [128, TB], BF16)
        DBG_C = scr("DBG_C", [128, TB], F32)
        DBG_H = scr("DBG_H", [128, NC * TB], F32)
        DBG_Y = scr("DBG_Y", [D, TB], F32)
        DBG_U = scr("DBG_U", [D, TB], BF16)
        DBG_DP = scr("DBG_DP", [128, TB], BF16)
        DBG_BC = scr("DBG_BC", [16, TB], BF16)
        DBG_CC = scr("DBG_CC", [16, TB], BF16)
        DBG_X2 = scr("DBG_X2", [D, TB], BF16)
    RR = scr("RR", [D, T], BF16)
    XS = scr("XS", [D, T], BF16)
    OIN = scr("OIN", [D, T], BF16)
    R1 = scr("R1", [D, T], F32)
    R2 = scr("R2", [D, T], F32)
    R3 = scr("R3", [D, T], F32)
    WG = scr("WGd", [D, INTER], BF16)
    WU = scr("WUd", [D, INTER], BF16)
    WD = scr("WDd", [INTER, D], BF16)
    HB = scr("HBd", [INTER, T], BF16)
    QF = scr("QFd", [D, T], BF16)
    KF = scr("KFd", [D, T], BF16)
    VF = scr("VFd", [D, T], BF16)
    G1 = scr("G1d", [D, T], BF16)
    G2 = scr("G2d", [D, T], BF16)

    last_phase = phases[-1]
    res_out = {"A": R1, "B": R2, "C": R3, "D": outT}
    res_out[last_phase] = outT
    res_key = {"A": "RES_A", "B": "RES_B", "C": "RES_C", "D": "RES_D"}

    st = contextlib.ExitStack()
    with st:
        sems = {}
        for e in ENGS:
            sems[("c", e)] = st.enter_context(nc.semaphore(f"c_{e}"))
        for e in DMA_ENGS:
            for s_ in range(Prog.NSLOT):
                sems[("d", e, s_)] = st.enter_context(nc.semaphore(f"d_{e}{s_}"))

        sb = lambda name, shape, dt: st.enter_context(nc.sbuf_tensor(name, list(shape), dt))
        ident = sb("ident", [128, 128], BF16)
        ones_k1_f = sb("ones_k1_f", [1, 128], F32)
        ones_m1_bf = sb("ones_m1_bf", [128, 1], BF16)
        zeros512 = sb("zeros512", [128, 512], BF16)
        sel = sb("sel", [16, 16 * 128], BF16)
        onesrow = sb("onesrow", [1, 128], BF16)
        epscol = sb("epscol", [128, 1], F32)
        onecol = sb("onecol", [128, 1], F32)

        def g_init(g):
            g.memset(ident[:], 1.0)
            g.affine_select(out=ident[:], in_=ident[:], pattern=[[1, 128]],
                            compare_op=ALU.is_ge, fill=0.0, channel_multiplier=-1)
            return g.affine_select(out=ident[:], in_=ident[:], pattern=[[-1, 128]],
                                   compare_op=ALU.is_ge, fill=0.0, channel_multiplier=1)
        p.op("gpsimd", g_init, w=[("ident",)])
        p.op("gpsimd", lambda g: g.memset(ones_k1_f[:], 1.0), w=[("o2",)])
        p.op("gpsimd", lambda g: g.memset(ones_m1_bf[:], 1.0), w=[("o3",)])
        p.op("gpsimd", lambda g: g.memset(zeros512[:], 0.0), w=[("z5",)])
        p.op("gpsimd", lambda g: g.memset(sel[:], 0.0), w=[("sel",)])
        p.op("gpsimd", lambda g: g.memset(onesrow[:], 1.0), w=[("orow",)])
        p.op("gpsimd", lambda g: g.memset(epscol[:], 1e-5), w=[("eps",)])
        p.op("gpsimd", lambda g: g.memset(onecol[:], 1.0), w=[("onec",)])
        for n in range(16):
            p.op("gpsimd", lambda g, n=n: g.dma_start(out=sel[n:n + 1, 128 * n:128 * (n + 1)], in_=onesrow[:]),
                 r=[("orow",), ("sel",)], w=[("sel", n)], dma=True)

        def sel_n(n):
            return sel[0:16, 128 * n:128 * (n + 1)]

        mask_ut = sb("mask_ut", [128, 128], BF16)
        p.op("gpsimd", lambda g: g.memset(mask_ut[:], 1.0), w=[("mut",)])
        p.op("gpsimd", lambda g: g.affine_select(out=mask_ut[:], in_=mask_ut[:], pattern=[[1, 128]],
                                                 compare_op=ALU.is_ge, fill=0.0, channel_multiplier=-1),
             r=[("mut",)], w=[("mut",)])
        lutbc = sb("lutbc", [128, 16], F32)
        p.op("sync", lambda e: e.dma_start(out=lutbc[0:1, :], in_=lut), w=[("lut", 0)], dma=True)
        for dbl in range(7):
            lo, hi = 1 << dbl, 1 << (dbl + 1)
            p.op("sync", lambda e, lo=lo, hi=hi: e.dma_start(out=lutbc[lo:hi, :], in_=lutbc[0:lo, :]),
                 r=[("lut", dbl)], w=[("lut", dbl + 1)], dma=True)
        LUTK = ("lut", 7)

        def ln_phase(tag, src_dram, src_key, xn_pool, scope, psA, psB, psC, tick=None):
            sbp = lambda name, shape, dt: scope.enter_context(nc.sbuf_tensor(name, list(shape), dt))
            xf = [sbp(f"{tag}xf{i}", [128, TB], BF16) for i in range(4)]
            sq = [sbp(f"{tag}sq{i}", [128, TB], BF16) for i in range(2)]
            rows = [sbp(f"{tag}rw{i}", [1, TB], F32) for i in range(6)]
            bca = [sbp(f"{tag}bc{i}", [128, TB], F32) for i in range(2)]
            K = lambda *a: (tag,) + a
            for tb in range(NTB):
                for c in range(NC):
                    p.op("gpsimd", lambda e, c=c, tb=tb: e.dma_start(
                        out=xf[c % 4][:], in_=src_dram[128 * c:128 * (c + 1), TB * tb:TB * (tb + 1)]),
                        r=[src_key(c, tb)], w=[K("xf", c % 4)], dma=True)
                    if tick:
                        tick(1)
                    p.op("scalar", lambda s, c=c: s.square(sq[c % 2][:], xf[c % 4][:]),
                         r=[K("xf", c % 4)], w=[K("sq", c % 2)])
                    p.op("tensor", lambda t, c=c: t.matmul(psA[0:1, :], ones_m1_bf[:], xf[c % 4][:],
                                                           start=(c == 0), stop=(c == NC - 1)),
                         r=[K("xf", c % 4), ("o3",)], w=[psA.k])
                    p.op("tensor", lambda t, c=c: t.matmul(psB[0:1, :], ones_m1_bf[:], sq[c % 2][:],
                                                           start=(c == 0), stop=(c == NC - 1)),
                         r=[K("sq", c % 2), ("o3",)], w=[psB.k])
                p.op("scalar", lambda s: s.mul(rows[0][:], psA[0:1, :], 1.0 / D),
                     r=[psA.k], w=[K("r0")])
                p.op("scalar", lambda s: s.mul(rows[1][:], psB[0:1, :], 1.0 / D),
                     r=[psB.k], w=[K("r1")])
                p.op("vector", lambda v: v.tensor_mul(rows[2][:], rows[0][:], rows[0][:]),
                     r=[K("r0")], w=[K("r2")])
                p.op("vector", lambda v: v.tensor_sub(rows[3][:], rows[1][:], rows[2][:]),
                     r=[K("r1"), K("r2")], w=[K("r3")])
                p.op("scalar", lambda s: s.activation(rows[4][:], rows[3][:], AF.Ln, bias=epscol[0:1, 0:1]),
                     r=[K("r3"), ("eps",)], w=[K("r4")])
                p.op("scalar", lambda s: s.activation(rows[4][:], rows[4][:], AF.Exp, scale=-0.5),
                     r=[K("r4")], w=[K("r4")])
                p.op("vector", lambda v: v.tensor_mul(rows[5][:], rows[0][:], rows[4][:]),
                     r=[K("r0"), K("r4")], w=[K("r5")])
                p.op("tensor", lambda t: t.matmul(psC[:, 0:TB], ones_k1_f[:], rows[4][:], start=True, stop=True),
                     r=[K("r4"), ("o2",)], w=[psC.k])
                p.op("vector", lambda v: v.tensor_copy(bca[0][:], psC[:, 0:TB]),
                     r=[psC.k], w=[K("bc0")])
                p.op("tensor", lambda t: t.matmul(psC[:, 0:TB], ones_k1_f[:], rows[5][:], start=True, stop=True),
                     r=[K("r5"), ("o2",)], w=[psC.k])
                p.op("vector", lambda v: v.tensor_copy(bca[1][:], psC[:, 0:TB]),
                     r=[psC.k], w=[K("bc1")])
                for c in range(NC):
                    p.op("gpsimd", lambda e, c=c, tb=tb: e.dma_start(
                        out=xf[c % 4][:], in_=src_dram[128 * c:128 * (c + 1), TB * tb:TB * (tb + 1)]),
                        r=[src_key(c, tb)], w=[K("xf", c % 4)], dma=True)
                    p.op("vector", lambda v, c=c, tb=tb: v.scalar_tensor_tensor(
                        xn_pool[c][:, TB * tb:TB * (tb + 1)],
                        xf[c % 4][:], 1.0, bca[0][:], ALU.mult, ALU.mult),
                        r=[K("xf", c % 4), K("bc0")], w=[K("xnt", c)])
                    p.op("vector", lambda v, c=c, tb=tb: v.tensor_sub(
                        xn_pool[c][:, TB * tb:TB * (tb + 1)],
                        xn_pool[c][:, TB * tb:TB * (tb + 1)], bca[1][:]),
                        r=[K("xnt", c), K("bc1")], w=[K("xn", c, tb)])
            return lambda c, tb: K("xn", c, tb)

        def phase_A01(RA):
            with contextlib.ExitStack() as ph:
                sbp = lambda name, shape, dt: ph.enter_context(nc.sbuf_tensor(name, list(shape), dt))
                ps0 = PS(ph, nc, "pA0"); ps1 = PS(ph, nc, "pA1"); ps2 = PS(ph, nc, "pA2")
                xn = [sbp(f"Axn{i}", [128, T], BF16) for i in range(NC)]
                xnk = ln_phase("A0", xT, lambda c, tb: ("xTd", c, tb), xn, ph, ps0, ps1, ps2,
                               tick=lambda n: tick_v(1))
                wh = [sbp(f"Awh{i}", [128, D // 2], BF16) for i in range(NC)]
                ev = [sbp(f"Aev{i}", [128, TB], BF16) for i in range(2)]
                for hf in range(2):
                  for mh in range(2):
                    for c in range(NC):
                        p.op("gpsimd", lambda e, c=c, hf=hf, mh=mh: e.dma_start(
                            out=wh[c][:], in_=w_in[128 * c:128 * (c + 1),
                                                   D * hf + 1024 * mh:D * hf + 1024 * (mh + 1)]),
                            w=[("A.wh", c)], dma=True)
                    for m in range(8 * mh, 8 * mh + 8):
                        for tb in range(NTB):
                            pp = (ps0, ps1)[(m * NTB + tb) % 2]
                            def fm(t, m=m, tb=tb, pp=pp, mh=mh):
                                for c in range(NC):
                                    ins = t.matmul(pp[:, 0:TB],
                                                   wh[c][:, 128 * (m - 8 * mh):128 * (m - 8 * mh + 1)],
                                                   xn[c][:, TB * tb:TB * (tb + 1)],
                                                   start=(c == 0), stop=(c == NC - 1))
                                return ins
                            p.op("tensor", fm,
                                 r=[("A.wh", c) for c in range(NC)] + [xnk(c, tb) for c in range(NC)],
                                 w=[pp.k])
                            eb = ev[(m * NTB + tb) % 2]
                            ebk = ("A.ev", (m * NTB + tb) % 2)
                            p.op("scalar", lambda s, pp=pp, eb=eb: s.copy(eb[:], pp[:, 0:TB]),
                                 r=[pp.k], w=[ebk])
                            if hf == 0:
                                p.op("scalar", lambda e, m=m, tb=tb, eb=eb: e.dma_start(
                                    out=XC[128 * m:128 * (m + 1), 3 + TB * tb:3 + TB * (tb + 1)], in_=eb[:]),
                                    r=[ebk], w=[("XC", m, tb)], dma=True)
                            else:
                                p.op("scalar", lambda e, m=m, tb=tb, eb=eb: e.dma_start(
                                    out=RR[128 * m:128 * (m + 1), TB * tb:TB * (tb + 1)], in_=eb[:]),
                                    r=[ebk], w=[("RR", m, tb)], dma=True)
                            tick_v(4)
                zpad = sbp("Azpad", [128, 3], BF16)
                p.op("gpsimd", lambda g: g.memset(zpad[:], 0.0), w=[("A.zpad",)])
                for m in range(NC):
                    p.op("scalar", lambda e, m=m: e.dma_start(out=XC[128 * m:128 * (m + 1), 0:3], in_=zpad[:]),
                         r=[("A.zpad",)], w=[("XC", m, "pad")], dma=True)

        def phase_A2():
            with contextlib.ExitStack() as ph:
                p.barrier()
                sbp = lambda name, shape, dt: ph.enter_context(nc.sbuf_tensor(name, list(shape), dt))
                ps0 = PS(ph, nc, "pA20"); ps1 = PS(ph, nc, "pA21")
                xc = [sbp(f"A2xc{i}", [128, 3 + T], BF16) for i in range(NC)]
                cwb = [sbp(f"A2cw{i}", [128, D], BF16) for i in range(8)]
                cb16 = sbp("A2cb", [128, 16], F32)
                ev = [sbp(f"A2ev{i}", [128, TB], BF16) for i in range(2)]
                p.op("sync", lambda e: e.dma_start(out=cb16[:], in_=conv_b),
                     w=[("A.cb16",)], dma=True)
                for c in range(NC):
                    p.op("sync", lambda e, c=c: e.dma_start(out=xc[c][:], in_=XC[128 * c:128 * (c + 1), :]),
                         r=[("XC", c, "pad")] + [("XC", c, tb) for tb in range(NTB)],
                         w=[("A.xc", c)], dma=True)
                for m in range(NC):
                    par = m % 2
                    for k in range(4):
                        p.op("gpsimd", lambda e, m=m, k=k, par=par: e.dma_start(
                            out=cwb[4 * par + k][:].rearrange("p (c o) -> p c o", c=NC),
                            in_=cw_h[m, k].rearrange("(c p) o -> p c o", p=128)),
                            w=[("A.cwb", 4 * par + k)], dma=True)
                    for tb in range(NTB):
                        pp = (ps0, ps1)[tb % 2]
                        def fm(t, m=m, tb=tb, pp=pp, par=par):
                            for k in range(4):
                                for c in range(NC):
                                    ins = t.matmul(pp[:, 0:TB], cwb[4 * par + k][:, 128 * c:128 * (c + 1)],
                                                   xc[c][:, TB * tb + k:TB * tb + k + TB],
                                                   start=(k == 0 and c == 0), stop=(k == 3 and c == NC - 1))
                            return ins
                        p.op("tensor", fm,
                             r=[("A.cwb", 4 * par + k) for k in range(4)] + [("A.xc", c) for c in range(NC)],
                             w=[pp.k])
                        eb = ev[tb % 2]
                        ebk = ("A.ev2", tb % 2)
                        p.op("vector", lambda v, pp=pp, eb=eb, m=m: v.tensor_scalar(
                            eb[:], pp[:, 0:TB], cb16[:, m:m + 1], None, ALU.add),
                            r=[pp.k, ("A.cb16",)], w=[ebk])
                        p.op("scalar", lambda e, m=m, tb=tb, eb=eb: e.dma_start(
                            out=XS[128 * m:128 * (m + 1), TB * tb:TB * (tb + 1)], in_=eb[:]),
                            r=[ebk], w=[("XS", m, tb)], dma=True)
                        tick_v(10)

        def phase_A3():
            with contextlib.ExitStack() as ph:
                p.barrier()
                sbp = lambda name, shape, dt: ph.enter_context(nc.sbuf_tensor(name, list(shape), dt))
                ps0 = PS(ph, nc, "pA30"); ps1 = PS(ph, nc, "pA31")
                ps2 = PS(ph, nc, "pA32"); ps3 = PS(ph, nc, "pA33")
                wxr = [sbp(f"A3wx{i}", [128, 160], BF16) for i in range(NC)]
                wdtr = sbp("A3wdt", [128, D], BF16)
                nA = sbp("A3nA", [128, 256], F32)
                bdt16 = sbp("A3bdt", [128, 16], F32)
                dv16 = sbp("A3dv", [128, 16], F32)
                xs = [sbp(f"A3xs{i}", [128, TB], BF16) for i in range(2)]
                u = [sbp(f"A3u{i}", [128, TB], BF16) for i in range(8)]
                dpw = sbp("A3dpw", [128, T], BF16)
                Bw = sbp("A3Bw", [16, T], BF16)
                Cw = sbp("A3Cw", [16, T], BF16)
                uw = [sbp(f"A3uw{i}", [128, T], BF16) for i in range(2)]
                saw = sbp("A3saw", [128, T], BF16)
                sbw = sbp("A3sbw", [128, T], BF16)
                dlt = [sbp(f"A3dl{i}", [128, T], BF16) for i in range(2)]
                duw = [sbp(f"A3du{i}", [128, T], BF16) for i in range(2)]
                dAw = [sbp(f"A3dA{i}", [128, T], BF16) for i in range(2)]
                dBw = [sbp(f"A3dB{i}", [128, T], BF16) for i in range(2)]
                Hw = [sbp(f"A3H{i}", [128, T], BF16) for i in range(2)]
                tmpw = [sbp(f"A3tp{i}", [128, T], BF16) for i in range(4)]
                Bbw = [sbp(f"A3Bb{i}", [128, T], BF16) for i in range(2)]
                Cbw = [sbp(f"A3Cb{i}", [128, T], BF16) for i in range(2)]
                yaw = [sbp(f"A3ya{i}", [128, T], F32) for i in range(2)]
                rrw = [sbp(f"A3rr{i}", [128, T], BF16) for i in range(2)]
                oinw = [sbp(f"A3oi{i}", [128, T], BF16) for i in range(2)]

                for c in range(NC):
                    p.op("gpsimd", lambda e, c=c: e.dma_start(out=wxr[c][:], in_=w_x[128 * c:128 * (c + 1), :]),
                         w=[("A.wx", c)], dma=True)
                p.op("gpsimd", lambda e: e.dma_start(out=wdtr[:], in_=w_dt), w=[("A.wdt",)], dma=True)
                p.op("sync", lambda e: e.dma_start(out=nA[:], in_=a_log),
                     w=[("A.nA",)], dma=True)
                p.op("scalar", lambda s: s.activation(nA[:], nA[:], AF.Exp), r=[("A.nA",)], w=[("A.nA",)])
                p.op("vector", lambda v: v.tensor_scalar(nA[:], nA[:], -1.0, None, ALU.mult),
                     r=[("A.nA",)], w=[("A.nA",)])
                p.op("sync", lambda e: e.dma_start(out=bdt16[:], in_=b_dt),
                     w=[("A.bdt",)], dma=True)
                p.op("sync", lambda e: e.dma_start(out=dv16[:], in_=d_vec),
                     w=[("A.dv",)], dma=True)

                for tb in range(NTB):
                    for chf in range(2):
                        for c8 in range(8):
                            c = 8 * chf + c8
                            p.op("sync", lambda e, c=c, tb=tb: e.dma_start(
                                out=xs[c % 2][:], in_=XS[128 * c:128 * (c + 1), TB * tb:TB * (tb + 1)]),
                                r=[("XS", c, tb)], w=[("A.xs", c % 2)], dma=True)
                            p.op("scalar", lambda s, c=c, c8=c8: s.activation(u[c8][:], xs[c % 2][:], AF.Silu),
                                 r=[("A.xs", c % 2)], w=[("A.u", c8)])
                        uks = [("A.u", c8) for c8 in range(8)]
                        wxk = [("A.wx", 8 * chf + c8) for c8 in range(8)]

                        def f_dp(t, chf=chf):
                            for c8 in range(8):
                                ins = t.matmul(ps0[:, 0:TB], wxr[8 * chf + c8][:, 0:128], u[c8][:],
                                               start=(chf == 0 and c8 == 0), stop=(chf == 1 and c8 == 7))
                            return ins
                        p.op("tensor", f_dp, r=wxk + uks, w=[ps0.k])

                        def f_B(t, chf=chf):
                            for c8 in range(8):
                                ins = t.matmul(ps1[0:16, 0:TB], wxr[8 * chf + c8][:, 128:144], u[c8][:],
                                               start=(chf == 0 and c8 == 0), stop=(chf == 1 and c8 == 7))
                            return ins
                        p.op("tensor", f_B, r=wxk + uks, w=[ps1.k])

                        def f_C(t, chf=chf):
                            for c8 in range(8):
                                ins = t.matmul(ps2[0:16, 0:TB], wxr[8 * chf + c8][:, 144:160], u[c8][:],
                                               start=(chf == 0 and c8 == 0), stop=(chf == 1 and c8 == 7))
                            return ins
                        p.op("tensor", f_C, r=wxk + uks, w=[ps2.k])
                    p.op("scalar", lambda s, tb=tb: s.copy(dpw[:, TB * tb:TB * (tb + 1)], ps0[:, 0:TB]),
                         r=[ps0.k], w=[("A.dpw", tb)])
                    p.op("scalar", lambda s, tb=tb: s.copy(Bw[:, TB * tb:TB * (tb + 1)], ps1[0:16, 0:TB]),
                         r=[ps1.k], w=[("A.Bw", tb)])
                    p.op("scalar", lambda s, tb=tb: s.copy(Cw[:, TB * tb:TB * (tb + 1)], ps2[0:16, 0:TB]),
                         r=[ps2.k], w=[("A.Cw", tb)])
                dpk = [("A.dpw", tb) for tb in range(NTB)]
                Bwk = [("A.Bw", tb) for tb in range(NTB)]
                Cwk = [("A.Cw", tb) for tb in range(NTB)]

                for cg in range(8):
                    for ci in range(2):
                        c = 2 * cg + ci
                        kuw = ("A.uw", c % 2)
                        p.op("sync", lambda e, c=c: e.dma_start(
                            out=uw[c % 2][:], in_=XS[128 * c:128 * (c + 1), :]),
                            r=[("XS", c, tb) for tb in range(NTB)], w=[kuw], dma=True)
                        p.op("scalar", lambda s, c=c: s.activation(uw[c % 2][:], uw[c % 2][:], AF.Silu),
                             r=[kuw], w=[kuw])
                        for tb in range(NTB):
                            pp = (ps0, ps3)[tb % 2]
                            p.op("tensor", lambda t, c=c, tb=tb, pp=pp: t.matmul(
                                pp[:, 0:TB], wdtr[:, 128 * c:128 * (c + 1)], dpw[:, TB * tb:TB * (tb + 1)],
                                start=True, stop=True),
                                r=[("A.wdt",)] + dpk, w=[pp.k])
                            p.op("scalar", lambda s, pp=pp, tb=tb, c=c: s.activation(
                                saw[:, TB * tb:TB * (tb + 1)], pp[:, 0:TB], AF.Abs, bias=bdt16[:, c:c + 1]),
                                r=[pp.k, ("A.bdt",)], w=[("A.saw", tb)])
                            p.op("scalar", lambda s, pp=pp, tb=tb, c=c: s.activation(
                                sbw[:, TB * tb:TB * (tb + 1)], pp[:, 0:TB], AF.Relu, bias=bdt16[:, c:c + 1]),
                                r=[pp.k, ("A.bdt",)], w=[("A.sbw", tb)])
                        sawk = [("A.saw", tb) for tb in range(NTB)]
                        sbwk = [("A.sbw", tb) for tb in range(NTB)]
                        p.op("scalar", lambda s: s.activation(saw[:], saw[:], AF.Exp, scale=-1.0),
                             r=sawk, w=[("A.sawf",)])
                        p.op("scalar", lambda s: s.activation(saw[:], saw[:], AF.Ln, bias=onecol[:, 0:1]),
                             r=[("A.sawf",), ("onec",)], w=[("A.sawf",)])
                        p.op("vector", lambda v, ci=ci: v.tensor_add(dlt[ci][:], saw[:], sbw[:]),
                             r=[("A.sawf",)] + sbwk, w=[("A.dl", ci)])
                        p.op("vector", lambda v, ci=ci: v.tensor_mul(duw[ci][:], dlt[ci][:], uw[ci][:]),
                             r=[("A.dl", ci), ("A.uw", ci)], w=[("A.du", ci)])
                    for n in range(16):
                        npar = n % 2
                        Bb, Cb = Bbw[npar], Cbw[npar]
                        KB, KC = ("A.Bbw", npar), ("A.Cbw", npar)
                        for tb in range(NTB):
                            pp = (ps1, ps2)[tb % 2]
                            p.op("tensor", lambda t, n=n, tb=tb, pp=pp: t.matmul(
                                pp[:, 0:TB], sel_n(n), Bw[:, TB * tb:TB * (tb + 1)], start=True, stop=True),
                                r=Bwk + [("sel", n)], w=[pp.k])
                            p.op("scalar", lambda s, tb=tb, pp=pp, Bb=Bb: s.copy(
                                Bb[:, TB * tb:TB * (tb + 1)], pp[:, 0:TB]), r=[pp.k], w=[(KB, tb)])
                            p.op("tensor", lambda t, n=n, tb=tb, pp=pp: t.matmul(
                                pp[:, 0:TB], sel_n(n), Cw[:, TB * tb:TB * (tb + 1)], start=True, stop=True),
                                r=Cwk + [("sel", n), (KB, tb)], w=[pp.k])
                            p.op("scalar", lambda s, tb=tb, pp=pp, Cb=Cb: s.copy(
                                Cb[:, TB * tb:TB * (tb + 1)], pp[:, 0:TB]), r=[pp.k], w=[(KC, tb)])
                        KBf = [(KB, tb) for tb in range(NTB)]
                        KCf = [(KC, tb) for tb in range(NTB)]
                        for ci in range(2):
                            c = 2 * cg + ci
                            rot = (2 * n + ci) % 2
                            p.op("scalar", lambda s, c=c, n=n, rot=rot, ci=ci: s.activation(
                                dAw[rot][:], dlt[ci][:], AF.Exp, scale=nA[:, 16 * c + n:16 * c + n + 1]),
                                r=[("A.dl", ci), ("A.nA",)], w=[("A.dAw", rot)])
                            p.op("vector", lambda v, ci=ci, rot=rot, Bb=Bb: v.tensor_mul(
                                dBw[rot][:], duw[ci][:], Bb[:]),
                                r=[("A.du", ci)] + KBf, w=[("A.dBw", rot)])
                            p.op("vector", lambda v, rot=rot: v.tensor_tensor_scan(
                                Hw[rot][:], dAw[rot][:], dBw[rot][:], 0.0, ALU.mult, ALU.add),
                                r=[("A.dAw", rot), ("A.dBw", rot)], w=[("A.Hw", rot)])
                            tpi = (2 * n + ci) % 4
                            p.op("vector", lambda v, rot=rot, tpi=tpi, Cb=Cb: v.tensor_mul(
                                tmpw[tpi][:], Hw[rot][:], Cb[:]),
                                r=[("A.Hw", rot)] + KCf, w=[("A.tpw", tpi)])
                            if n == 0:
                                p.op("gpsimd", lambda g, ci=ci, tpi=tpi: g.dma_start(
                                    out=yaw[ci][:], in_=tmpw[tpi][:], max_dma_last_dim=2048),
                                    r=[("A.tpw", tpi)], w=[("A.yaw", ci)], dma=True, qchain=True)
                            else:
                                p.op("gpsimd", lambda g, ci=ci, tpi=tpi: g.dma_start(
                                    out=yaw[ci][:], in_=tmpw[tpi][:], accum_op=ALU.add,
                                    max_dma_last_dim=2048),
                                    r=[("A.tpw", tpi), ("A.yaw", ci)], w=[("A.yaw", ci)], dma=True, qchain=True)
                    for ci in range(2):
                        c = 2 * cg + ci
                        p.op("sync", lambda e, c=c, ci=ci: e.dma_start(
                            out=rrw[ci][:], in_=RR[128 * c:128 * (c + 1), :]),
                            r=[("RR", c, tb) for tb in range(NTB)], w=[("A.rrw", ci)], dma=True)
                        p.op("scalar", lambda s, ci=ci: s.activation(rrw[ci][:], rrw[ci][:], AF.Silu),
                             r=[("A.rrw", ci)], w=[("A.rrw", ci)])
                        p.op("vector", lambda v, c=c, ci=ci: v.scalar_tensor_tensor(
                            oinw[ci][:], uw[ci][:], dv16[:, c:c + 1], yaw[ci][:], ALU.mult, ALU.add),
                            r=[("A.uw", ci), ("A.dv",), ("A.yaw", ci)], w=[("A.oiw", ci)])
                        p.op("vector", lambda v, ci=ci: v.tensor_mul(oinw[ci][:], oinw[ci][:], rrw[ci][:]),
                             r=[("A.oiw", ci), ("A.rrw", ci)], w=[("A.oiw", ci)])
                        p.op("scalar", lambda e, c=c, ci=ci: e.dma_start(
                            out=OIN[128 * c:128 * (c + 1), :], in_=oinw[ci][:]),
                            r=[("A.oiw", ci)], w=[("OIN", c, 0), ("OIN", c, 1), ("OIN", c, 2), ("OIN", c, 3)],
                            dma=True)

        def phase_A4(RA):
            with contextlib.ExitStack() as ph:
                p.barrier()
                sbp = lambda name, shape, dt: ph.enter_context(nc.sbuf_tensor(name, list(shape), dt))
                ps0 = PS(ph, nc, "pA40"); ps1 = PS(ph, nc, "pA41")
                oin = [sbp(f"A4o{i}", [128, TB], BF16) for i in range(NC)]
                xs = [sbp(f"A4x{i}", [128, TB], BF16) for i in range(NC)]
                wob = [sbp(f"A4w{i}", [128, TB], BF16) for i in range(2 * NC)]
                xf = [sbp(f"A4f{i}", [128, TB], F32) for i in range(2)]
                rof = [sbp(f"A4r{i}", [128, TB], F32) for i in range(2)]
                for tb in range(NTB):
                    for c in range(NC):
                        p.op("sync", lambda e, c=c, tb=tb: e.dma_start(
                            out=oin[c][:], in_=OIN[128 * c:128 * (c + 1), TB * tb:TB * (tb + 1)]),
                            r=[("OIN", c, tb)], w=[("A4.oin", c)], dma=True)
                        p.op("sync", lambda e, c=c, tb=tb: e.dma_start(
                            out=xs[c][:], in_=XS[128 * c:128 * (c + 1), TB * tb:TB * (tb + 1)]),
                            r=[("XS", c, tb)], w=[("A4.xs", c)], dma=True)
                    for og in range(4):
                        par = og % 2
                        for c in range(NC):
                            p.op("gpsimd", lambda e, c=c, og=og, par=par: e.dma_start(
                                out=wob[NC * par + c][:], in_=w_out[128 * c:128 * (c + 1), TB * og:TB * (og + 1)]),
                                w=[("A4.wo", NC * par + c)], dma=True)
                        for j in range(4):
                            m = 4 * og + j
                            pp = (ps0, ps1)[j % 2]
                            def fm(t, pp=pp, par=par, j=j):
                                for c in range(NC):
                                    ins = t.matmul(pp[:, 0:TB], wob[NC * par + c][:, 128 * j:128 * (j + 1)],
                                                   oin[c][:], start=(c == 0), stop=(c == NC - 1))
                                return ins
                            p.op("tensor", fm,
                                 r=[("A4.wo", NC * par + c) for c in range(NC)] + [("A4.oin", c) for c in range(NC)],
                                 w=[pp.k])
                            jp = j % 2
                            p.op("sync", lambda e, m=m, tb=tb, jp=jp: e.dma_start(
                                out=xf[jp][:], in_=xT[128 * m:128 * (m + 1), TB * tb:TB * (tb + 1)]),
                                w=[("A4.xf", jp)], dma=True)
                            p.op("vector", lambda v, pp=pp, m=m, jp=jp: v.scalar_tensor_tensor(
                                rof[jp][:], pp[:, 0:TB], 1.0, xs[m][:], ALU.mult, ALU.add),
                                r=[pp.k, ("A4.xs", m)], w=[("A4.ro", jp)])
                            p.op("vector", lambda v, jp=jp: v.tensor_add(rof[jp][:], rof[jp][:], xf[jp][:]),
                                 r=[("A4.ro", jp), ("A4.xf", jp)], w=[("A4.ro", jp)])
                            p.op("scalar", lambda e, m=m, tb=tb, jp=jp: e.dma_start(
                                out=RA[128 * m:128 * (m + 1), TB * tb:TB * (tb + 1)], in_=rof[jp][:]),
                                r=[("A4.ro", jp)], w=[(res_key["A"], m, tb)], dma=True)
                            tick_v(8)

        dc_iv = [sb(f"DCiv{i}", [128, INTER // 2], BF16) for i in range(2)]
        dc_pv = sb("DCpv", [128, INTER // 2], BF16)
        dc_av = [sb(f"DCav{i}", [128, INTER // 2], BF16) for i in range(2)]
        dc_ig = [sb(f"DCig{i}", [128, D], BF16) for i in range(2)]
        dc_pg = sb("DCpg", [128, D], BF16)
        dc_ag = [sb(f"DCag{i}", [128, D], BF16) for i in range(2)]

        def decode_gen(eng, ld_eng, st_eng, idxb, prodb, accb, blocks):
            def gen():
                hb = 0
                for (srcv, dstv, F, wkey) in blocks:
                    par = hb % 2
                    hb += 1
                    ib, ac = idxb[par], accb[par]
                    ibk = ("DC", eng, par, "ib")
                    ack = ("DC", eng, par, "ac")
                    prk = ("DC", eng, "pr")
                    p.op(ld_eng, lambda e, srcv=srcv, ib=ib, F=F: e.dma_start(
                        out=ib[:, 0:F], in_=srcv),
                        w=[ibk], dma=True)
                    yield
                    p.op(eng, lambda v, ib=ib, ac=ac, F=F: v.tensor_scalar(
                        ac[:, 0:F], ib[:, 0:F], 0.0, lutbc[:, 0:1],
                        ALU.is_equal, ALU.mult),
                        r=[ibk, LUTK], w=[ack])
                    yield
                    for j in range(1, 16):
                        p.op(eng, lambda v, ib=ib, F=F, j=j: v.tensor_scalar(
                            prodb[:, 0:F], ib[:, 0:F], float(j), lutbc[:, j:j + 1],
                            ALU.is_equal, ALU.mult),
                            r=[ibk, LUTK], w=[prk])
                        yield
                        p.op(eng, lambda v, ac=ac, F=F: v.tensor_add(
                            ac[:, 0:F], ac[:, 0:F], prodb[:, 0:F]),
                            r=[ack, prk], w=[ack])
                        yield
                    p.op(st_eng, lambda e, dstv=dstv, ac=ac, F=F: e.dma_start(
                        out=dstv, in_=ac[:, 0:F]),
                        r=[ack], w=[wkey], dma=True)
                    yield
            return gen()

        HB0 = INTER // 2
        blk_gu = []
        for h in range(2):
            lo, hi = HB0 * h, HB0 * (h + 1)
            for c in range(NC):
                blk_gu.append((gidxT[128 * c:128 * (c + 1), lo:hi],
                               WG[128 * c:128 * (c + 1), lo:hi], HB0, ("WG", c, h)))
                blk_gu.append((uidxT[128 * c:128 * (c + 1), lo:hi],
                               WU[128 * c:128 * (c + 1), lo:hi], HB0, ("WU", c, h)))
        blk_d = [(didxT[128 * o:128 * (o + 1), :], WD[128 * o:128 * (o + 1), :],
                  D, ("WD", o)) for o in range(NO)]
        gen_v = decode_gen("vector", "gpsimd", "scalar", dc_iv, dc_pv, dc_av, blk_gu)
        gen_g = decode_gen("vector", "gpsimd", "sync", dc_ig, dc_pg, dc_ag, blk_d)
        n_v_left = [len(blk_gu) * 33]

        def tick_v(n):
            for _ in range(n):
                if next(gen_v, "end") == "end":
                    if next(gen_g, "end") == "end":
                        return
                else:
                    n_v_left[0] -= 1
        def drain_v_h0(n=10 ** 9):
            while n_v_left[0] > len(blk_gu) // 2 * 33:
                tick_v(1)
        def tick_g(n):
            for _ in range(n):
                if next(gen_g, "end") == "end":
                    return

        def ffn_phase(tag, src_dram, src_keyf, dst_dram, dst_keyn):
            with contextlib.ExitStack() as ph:
                p.barrier()
                sbp = lambda name, shape, dt: ph.enter_context(nc.sbuf_tensor(name, list(shape), dt))
                psA = PS(ph, nc, tag + "p0"); psB = PS(ph, nc, tag + "p1")
                psC = PS(ph, nc, tag + "p2"); psD = PS(ph, nc, tag + "p3")
                xn = [sbp(f"{tag}xn{i}", [128, T], BF16) for i in range(NC)]
                xnk = ln_phase(tag + "0", src_dram, src_keyf, xn, ph, psA, psB, psC,
                               tick=lambda n: tick_v(2))
                wgb = [sbp(f"{tag}wg{i}", [128, TB], BF16) for i in range(2 * NC)]
                wub = [sbp(f"{tag}wu{i}", [128, TB], BF16) for i in range(2 * NC)]
                hg = [sbp(f"{tag}hg{i}", [128, TB], BF16) for i in range(2)]
                hb = [sbp(f"{tag}hb{i}", [128, TB], BF16) for i in range(2)]
                psrot = [psA, psB, psC, psD]
                for og in range(11):
                    if og == 5:
                        tick_v(10 ** 9)
                    par = og % 2
                    ow = 512 if og < 10 else INTER - 5120
                    nj = 4 if og < 10 else 3
                    hks = [0] if og < 5 else ([0, 1] if og == 5 else [1])
                    for c in range(NC):
                        p.op("sync", lambda e, c=c, og=og, par=par, ow=ow: e.dma_start(
                            out=wgb[NC * par + c][:, 0:ow], in_=WG[128 * c:128 * (c + 1), 512 * og:512 * og + ow]),
                            r=[("WG", c, h) for h in hks], w=[(tag, "wg", NC * par + c)], dma=True)
                        p.op("sync", lambda e, c=c, og=og, par=par, ow=ow: e.dma_start(
                            out=wub[NC * par + c][:, 0:ow], in_=WU[128 * c:128 * (c + 1), 512 * og:512 * og + ow]),
                            r=[("WU", c, h) for h in hks], w=[(tag, "wu", NC * par + c)], dma=True)
                    for j in range(nj):
                        ot = 4 * og + j
                        for tb in range(NTB):
                            pg = psrot[(2 * (ot * NTB + tb)) % 4]
                            pu = psrot[(2 * (ot * NTB + tb) + 1) % 4]
                            def fg(t, par=par, j=j, tb=tb, pg=pg):
                                for c in range(NC):
                                    ins = t.matmul(pg[:, 0:TB], wgb[NC * par + c][:, 128 * j:128 * (j + 1)],
                                                   xn[c][:, TB * tb:TB * (tb + 1)],
                                                   start=(c == 0), stop=(c == NC - 1))
                                return ins
                            p.op("tensor", fg,
                                 r=[(tag, "wg", NC * par + c) for c in range(NC)] + [xnk(c, tb) for c in range(NC)],
                                 w=[pg.k])
                            def fu(t, par=par, j=j, tb=tb, pu=pu):
                                for c in range(NC):
                                    ins = t.matmul(pu[:, 0:TB], wub[NC * par + c][:, 128 * j:128 * (j + 1)],
                                                   xn[c][:, TB * tb:TB * (tb + 1)],
                                                   start=(c == 0), stop=(c == NC - 1))
                                return ins
                            p.op("tensor", fu,
                                 r=[(tag, "wu", NC * par + c) for c in range(NC)] + [xnk(c, tb) for c in range(NC)],
                                 w=[pu.k])
                            hpar = (ot * NTB + tb) % 2
                            p.op("scalar", lambda s, hpar=hpar, pg=pg: s.activation(hg[hpar][:], pg[:, 0:TB], AF.Silu),
                                 r=[pg.k], w=[(tag, "hg", hpar)])
                            p.op("vector", lambda v, hpar=hpar, pu=pu: v.tensor_mul(hb[hpar][:], hg[hpar][:], pu[:, 0:TB]),
                                 r=[(tag, "hg", hpar), pu.k], w=[(tag, "hb", hpar)])
                            p.op("scalar", lambda e, ot=ot, tb=tb, hpar=hpar: e.dma_start(
                                out=HB[128 * ot:128 * (ot + 1), TB * tb:TB * (tb + 1)], in_=hb[hpar][:]),
                                r=[(tag, "hb", hpar)], w=[(tag, "HB", ot, tb)], dma=True)
                            tick_v(8)
            tick_g(10 ** 9)
            with contextlib.ExitStack() as ph:
                p.barrier()
                sbp = lambda name, shape, dt: ph.enter_context(nc.sbuf_tensor(name, list(shape), dt))
                pso = [PS(ph, nc, f"{tag}q{i}") for i in range(8)]
                hcol = [sbp(f"{tag}hc{i}", [128, 1024], BF16) for i in range(NO)]
                r1t = [sbp(f"{tag}r1{i}", [128, TB], F32) for i in range(4)]
                wdb = [sbp(f"{tag}wd{i}", [128, TB], BF16) for i in range(2)]
                ev = [sbp(f"{tag}ev{i}", [128, TB], F32) for i in range(2)]
                for th in range(2):
                    for o in range(NO):
                        p.op("sync", lambda e, o=o, th=th: e.dma_start(
                            out=hcol[o][:], in_=HB[128 * o:128 * (o + 1), 1024 * th:1024 * (th + 1)]),
                            r=[(tag, "HB", o, 2 * th), (tag, "HB", o, 2 * th + 1)],
                            w=[(tag, "hc", o)], dma=True)
                    for dmg in range(4):
                        for o in range(NO):
                            par = o % 2
                            p.op("gpsimd", lambda e, o=o, dmg=dmg, par=par: e.dma_start(
                                out=wdb[par][:], in_=WD[128 * o:128 * (o + 1), TB * dmg:TB * (dmg + 1)]),
                                r=[("WD", o)], w=[(tag, "wd", par)], dma=True)
                            def fd(t, o=o, par=par):
                                for dj in range(4):
                                    for t2 in range(2):
                                        ins = t.matmul(pso[2 * dj + t2][:, 0:TB],
                                                       wdb[par][:, 128 * dj:128 * (dj + 1)],
                                                       hcol[o][:, TB * t2:TB * (t2 + 1)],
                                                       start=(o == 0), stop=(o == NO - 1))
                                return ins
                            p.op("tensor", fd,
                                 r=[(tag, "wd", par), (tag, "hc", o)],
                                 w=[pso[i].k for i in range(8)])
                        for dj in range(4):
                            m = 4 * dmg + dj
                            for t2 in range(2):
                                epar = t2
                                tb = 2 * th + t2
                                rb = (2 * dj + t2) % 4
                                p.op("sync", lambda e, m=m, tb=tb, rb=rb: e.dma_start(
                                    out=r1t[rb][:], in_=src_dram[128 * m:128 * (m + 1), TB * tb:TB * (tb + 1)]),
                                    r=[src_keyf(m, tb)], w=[(tag, "r1", rb)], dma=True)
                                p.op("vector", lambda v, dj=dj, t2=t2, m=m, epar=epar, rb=rb: v.scalar_tensor_tensor(
                                    ev[epar][:], pso[2 * dj + t2][:, 0:TB], 1.0,
                                    r1t[rb][:], ALU.mult, ALU.add),
                                    r=[pso[2 * dj + t2].k, (tag, "r1", rb)], w=[(tag, "ev", epar)])
                                p.op("scalar", lambda e, m=m, tb=tb, epar=epar: e.dma_start(
                                    out=dst_dram[128 * m:128 * (m + 1), TB * tb:TB * (tb + 1)], in_=ev[epar][:]),
                                    r=[(tag, "ev", epar)], w=[(dst_keyn, m, tb)], dma=True)

        def phase_C(RC):
            with contextlib.ExitStack() as ph:
                p.barrier()
                sbp = lambda name, shape, dt: ph.enter_context(nc.sbuf_tensor(name, list(shape), dt))
                psA = PS(ph, nc, "pC0"); psB = PS(ph, nc, "pC1"); psC = PS(ph, nc, "pC2")
                xn = [sbp(f"Cxn{i}", [128, T], BF16) for i in range(NC)]
                xnk = ln_phase("C0", R2, lambda c, tb: ("RES_B", c, tb), xn, ph, psA, psB, psC)
                wpb = [sbp(f"Cwp{i}", [128, TB], BF16) for i in range(2 * NC)]
                evb = [sbp(f"Cev{i}", [128, TB], BF16) for i in range(2)]
                za = [sbp(f"Cza{i}", [128, TB], F32) for i in range(2)]
                zm = [sbp(f"Czm{i}", [128, TB], F32) for i in range(2)]
                bgt = [sbp(f"Cbg{i}", [128, NC], F32) for i in range(2)]
                p.op("sync", lambda e: e.dma_start(out=bgt[0][:], in_=bg1), w=[("C.bg", 0)], dma=True)
                p.op("sync", lambda e: e.dma_start(out=bgt[1][:], in_=bg2), w=[("C.bg", 1)], dma=True)
                projs = [(wqT, QF, "QF", None), (wkT, KF, "KF", None), (wvT, VF, "VF", None),
                         (wg1T, G1, "G1", 0), (wg2T, G2, "G2", 1)]
                if 1 not in c_sub:
                    projs = []
                elif 4 in c_sub:
                    projs = projs[:3]
                elif 5 in c_sub:
                    projs = projs[3:4]
                elif 6 in c_sub:
                    projs = projs[:4]
                elif 7 in c_sub:
                    projs = projs[3:5]
                for wsrc, dst, dkey, gate_i in projs:
                    for og in range(4):
                        par = og % 2
                        for c in range(NC):
                            p.op("gpsimd", lambda e, c=c, og=og, par=par, wsrc=wsrc: e.dma_start(
                                out=wpb[NC * par + c][:], in_=wsrc[128 * c:128 * (c + 1), TB * og:TB * (og + 1)]),
                                w=[("C.wp", NC * par + c)], dma=True)
                        for j in range(4):
                            it = 4 * og + j
                            for tb in range(NTB):
                                pp = (psA, psB)[(j * NTB + tb) % 2]
                                def fp_(t, par=par, j=j, tb=tb, pp=pp):
                                    for c in range(NC):
                                        ins = t.matmul(pp[:, 0:TB], wpb[NC * par + c][:, 128 * j:128 * (j + 1)],
                                                       xn[c][:, TB * tb:TB * (tb + 1)],
                                                       start=(c == 0), stop=(c == NC - 1))
                                    return ins
                                p.op("tensor", fp_,
                                     r=[("C.wp", NC * par + c) for c in range(NC)] + [xnk(c, tb) for c in range(NC)],
                                     w=[pp.k])
                                ep = (j * NTB + tb) % 2
                                if gate_i is None:
                                    p.op("scalar", lambda s, pp=pp, ep=ep: s.copy(evb[ep][:], pp[:, 0:TB]),
                                         r=[pp.k], w=[("C.ev", ep)])
                                elif 8 in c_sub:
                                    gi = gate_i
                                    p.op("vector", lambda v, pp=pp, ep=ep, it=it, gi=gi: v.tensor_scalar(
                                        evb[ep][:], pp[:, 0:TB], bgt[gi][:, it:it + 1], 0.0, ALU.add, ALU.min),
                                        r=[pp.k, ("C.bg", gi)], w=[("C.ev", ep)])
                                elif 9 in c_sub:
                                    gi = gate_i
                                    p.op("scalar", lambda s, pp=pp, ep=ep, it=it, gi=gi: s.activation(
                                        za[ep][:], pp[:, 0:TB], AF.Abs, bias=bgt[gi][:, it:it + 1]),
                                        r=[pp.k, ("C.bg", gi)], w=[("C.za", ep)])
                                    p.op("vector", lambda v, ep=ep: v.tensor_copy(evb[ep][:], za[ep][:]),
                                         r=[("C.za", ep)], w=[("C.ev", ep)])
                                else:
                                    gi = gate_i
                                    p.op("scalar", lambda s, pp=pp, ep=ep, it=it, gi=gi: s.activation(
                                        za[ep][:], pp[:, 0:TB], AF.Abs, bias=bgt[gi][:, it:it + 1]),
                                        r=[pp.k, ("C.bg", gi)], w=[("C.za", ep)])
                                    p.op("scalar", lambda s, ep=ep: s.activation(za[ep][:], za[ep][:], AF.Exp, scale=-1.0),
                                         r=[("C.za", ep)], w=[("C.za", ep)])
                                    p.op("scalar", lambda s, ep=ep: s.activation(za[ep][:], za[ep][:], AF.Ln, bias=onecol[:, 0:1]),
                                         r=[("C.za", ep), ("onec",)], w=[("C.za", ep)])
                                    p.op("vector", lambda v, pp=pp, ep=ep, it=it, gi=gi: v.tensor_scalar(
                                        zm[ep][:], pp[:, 0:TB], bgt[gi][:, it:it + 1], 0.0, ALU.add, ALU.min),
                                        r=[pp.k, ("C.bg", gi)], w=[("C.zm", ep)])
                                    p.op("vector", lambda v, ep=ep: v.tensor_sub(zm[ep][:], zm[ep][:], za[ep][:]),
                                         r=[("C.zm", ep), ("C.za", ep)], w=[("C.zm", ep)])
                                    p.op("vector", lambda v, ep=ep: v.tensor_scalar(
                                        evb[ep][:], zm[ep][:], 1.0 / 16.0, -50.0, ALU.mult, ALU.max),
                                        r=[("C.zm", ep)], w=[("C.ev", ep)])
                                p.op("scalar", lambda e, dst=dst, it=it, tb=tb, ep=ep: e.dma_start(
                                    out=dst[128 * it:128 * (it + 1), TB * tb:TB * (tb + 1)], in_=evb[ep][:]),
                                    r=[("C.ev", ep)], w=[(dkey, it, tb)], dma=True)

            with contextlib.ExitStack() as ph:
                p.barrier()
                sbp = lambda name, shape, dt: ph.enter_context(nc.sbuf_tensor(name, list(shape), dt))
                psS_ = [PS(ph, nc, f"pS{i}") for i in range(2)]
                psO_ = [PS(ph, nc, f"pO{i}") for i in range(2)]
                psSD_ = [PS(ph, nc, f"pSD{i}") for i in range(2)]
                psT1 = PS(ph, nc, "pT1", BF16); psT2 = PS(ph, nc, "pT2", BF16)
                psAl = psS_[0]
                qc = [sbp(f"Cq{i}", [128, T], BF16) for i in range(2)]
                kc = [sbp(f"Ck{i}", [128, T], BF16) for i in range(2)]
                vc = [sbp(f"Cv{i}", [128, T], BF16) for i in range(2)]
                g1c = [sbp(f"Cg1{i}", [128, T], BF16) for i in range(2)]
                g2c = [sbp(f"Cg2{i}", [128, T], BF16) for i in range(2)]
                vtm = sbp("Cvtm", [128, T], BF16)
                ocomb = sbp("Coc", [128, T], BF16)
                st_ = [sbp(f"Cst{i}", [128, T], F32) for i in range(2)]
                NRT = 8
                Gt_ = [sbp(f"CG{i}", [128, 128], F32) for i in range(NRT)]
                eq_ = [sbp(f"Ceq{i}", [128, 128], BF16) for i in range(NRT)]
                ek_ = [sbp(f"Cek{i}", [128, 128], BF16) for i in range(NRT)]
                qls_ = [sbp(f"Cqls{i}", [128, 128], BF16) for i in range(NRT)]
                kx_ = [sbp(f"Ckx{i}", [128, 128], BF16) for i in range(NRT)]
                kd_ = [sbp(f"Ckd{i}", [128, 128], BF16) for i in range(NRT)]
                ek2_ = [sbp(f"Cek2{i}", [128, 128], BF16) for i in range(NRT)]
                kdtm_ = [sbp(f"Ckdt{i}", [128, 128], BF16) for i in range(NRT)]
                Ssb_ = [sbp(f"CSsb{i}", [128, 128], BF16) for i in range(NRT)]
                ccol_ = [sbp(f"Ccc{i}", [128, 4], F32) for i in range(NRT)]
                stbf_ = [sbp(f"Cstb{i}", [128, 128], BF16) for i in range(NRT)]
                ob_ = [sbp(f"Cob{i}", [128, 128], BF16) for i in range(2)]
                eqp = sbp("Ceqp", [128, 1024], F32)
                ekp = sbp("Cekp", [128, 1024], F32)
                eqb = sbp("Ceqb", [128, 1024], BF16)
                ekb = sbp("Cekb", [128, 1024], BF16)
                al = sbp("Cal", [1, 2], F32)
                al2 = sbp("Cal2", [1, 2], F32)
                mx1 = sbp("Cmx", [1, 1], F32)
                a01 = sbp("Ca01", [128, 2], F32)
                p.op("sync", lambda e: e.dma_start(out=al[:], in_=alpha), w=[("C.al",)], dma=True)
                p.op("vector", lambda v: v.reduce_max(mx1[:], al[:], axis=mybir.AxisListType.X),
                     r=[("C.al",)], w=[("C.mx",)])
                p.op("vector", lambda v: v.tensor_scalar(al2[:], al[:], mx1[0:1, 0:1], None, ALU.subtract),
                     r=[("C.al",), ("C.mx",)], w=[("C.al2",)])
                p.op("scalar", lambda s: s.activation(al2[:], al2[:], AF.Exp),
                     r=[("C.al2",)], w=[("C.al2",)])
                p.op("vector", lambda v: v.reduce_sum(mx1[:], al2[:], axis=mybir.AxisListType.X),
                     r=[("C.al2",)], w=[("C.mx",)])
                p.op("vector", lambda v: v.reciprocal(mx1[:], mx1[:]), r=[("C.mx",)], w=[("C.mx",)])
                p.op("vector", lambda v: v.tensor_scalar(al2[:], al2[:], mx1[0:1, 0:1], None, ALU.mult),
                     r=[("C.al2",), ("C.mx",)], w=[("C.al2",)])
                p.op("tensor", lambda t: t.matmul(psAl[:, 0:2], ones_k1_f[:], al2[:], start=True, stop=True),
                     r=[("C.al2",), ("o2",)], w=[psAl.k])
                p.op("scalar", lambda s: s.copy(a01[:], psAl[:, 0:2]), r=[psAl.k], w=[("C.a01",)])

                LNS = float(-0.5 * np.log(128.0))
                p.op("gpsimd", lambda g: g.memset(st_[0][:], 0.0), w=[("C.st", 0, h2) for h2 in range(NC)])
                p.op("gpsimd", lambda g: g.memset(st_[1][:], 0.0), w=[("C.st", 1, h2) for h2 in range(NC)])
                NCH = T // 128
                ld_engs = ("sync", "scalar", "gpsimd")
                for ch in (range(NCH) if 2 in c_sub else []):
                    cb = ch % 2
                    sl = slice(128 * ch, 128 * (ch + 1))
                    for ti, (buf, dsrc, dkey) in enumerate(((qc, QF, "QF"), (kc, KF, "KF"), (vc, VF, "VF"),
                                                            (g1c, G1, "G1"), (g2c, G2, "G2"))):
                        for h in range(NC):
                            eng = ld_engs[(ti * NC + h) % 3]
                            p.op(eng, lambda e, buf=buf, dsrc=dsrc, h=h, cb=cb, sl=sl: e.dma_start(
                                out=buf[cb][:, 128 * h:128 * (h + 1)], in_=dsrc[128 * h:128 * (h + 1), sl]),
                                r=[(dkey, h, sl.start // TB)], w=[("C." + dkey, cb, h)], dma=True)
                    chains = []
                    for h in range(NC):
                        for inst in range(2):
                            cp = (2 * h + inst) % NRT
                            chains.append(dict(
                                h=h, inst=inst, cp=cp, hs=slice(128 * h, 128 * (h + 1)),
                                gh=(g1c, g2c)[inst],
                                gk=("C." + ("G1", "G2")[inst], cb, h),
                                stk=("C.st", inst, h),
                                stv=st_[inst][:, 128 * h:128 * (h + 1)],
                                Gt=Gt_[cp], eq=eq_[cp], ek=ek_[cp], qls=qls_[cp], kx=kx_[cp],
                                kd=kd_[cp], kdtm=kdtm_[cp], Ssb=Ssb_[cp], ccol=ccol_[cp],
                                stbf=stbf_[cp],
                                psS=psS_[cp % 2], psO=psO_[cp % 2], psSD=psSD_[cp % 2],
                                KG=("C.Gt", cp), KEQ=("C.eq", cp), KEK=("C.ek", cp),
                                KQL=("C.qls", cp), KKX=("C.kx", cp), KKD=("C.kd", cp),
                                KKDT=("C.kdtm", cp), KSS=("C.Ssb", cp), KSB=("C.stbf", cp),
                                KEK2=("C.ek2", cp), ek2=ek2_[cp]))
                    for wv in range(4):
                      chains_w = chains[8 * wv:8 * wv + 8]
                      for ii, x in enumerate(chains_w):
                          p.op("vector", lambda v, x=x, cb=cb: v.tensor_tensor_scan(
                              x["Gt"][:], x["gh"][cb][:, x["hs"]], zeros512[:, 0:128], 0.0,
                              ALU.add, ALU.add),
                              r=[x["gk"], ("z5",)], w=[x["KG"]])
                          if x["inst"] == 0:
                              pst = (psT1, psT2)[x["h"] % 2]
                              p.op("tensor", lambda t, x=x, cb=cb, pst=pst: t.transpose(
                                  pst[:, 0:128], vc[cb][:, x["hs"]], ident[:]),
                                  r=[("C.VF", cb, x["h"]), ("ident",)], w=[pst.k])
                              p.op("vector", lambda v, x=x, pst=pst: v.tensor_copy(vtm[:, x["hs"]], pst[:, 0:128]),
                                   r=[pst.k], w=[("C.vtm", x["hs"].start)])
                      for ii, x in enumerate(chains_w):
                          ss = slice(128 * ii, 128 * (ii + 1))
                          p.op("gpsimd", lambda g, x=x, ss=ss: g.tensor_scalar(
                              eqp[:, ss], x["Gt"][:], x["Gt"][:, 63:64], LNS,
                              ALU.subtract, ALU.add),
                              r=[x["KG"]], w=[("C.eqp", ii)])
                          p.op("gpsimd", lambda g, x=x, ss=ss: g.tensor_scalar(
                              ekp[:, ss], x["Gt"][:], x["Gt"][:, 63:64], -1.0,
                              ALU.subtract, ALU.mult),
                              r=[x["KG"]], w=[("C.ekp", ii)])
                          p.op("vector", lambda v, x=x: v.tensor_sub(
                              x["ccol"][:, 3:4], x["Gt"][:, 127:128], x["Gt"][:, 63:64]),
                              r=[x["KG"]], w=[("C.cc", x["cp"], 3)])
                          p.op("scalar", lambda s, x=x: s.activation(
                              x["ccol"][:, 1:3], x["Gt"][:, 63:128:64], AF.Exp),
                              r=[x["KG"]], w=[("C.cc", x["cp"], 1), ("C.cc", x["cp"], 2)])
                          p.op("scalar", lambda s, x=x: s.activation(
                              x["ccol"][:, 3:4], x["ccol"][:, 3:4], AF.Exp),
                              r=[("C.cc", x["cp"], 3)], w=[("C.cc", x["cp"], 3)])
                      p.op("scalar", lambda s: s.activation(eqb[:], eqp[:], AF.Exp),
                           r=[("C.eqp", ii2) for ii2 in range(8)], w=[("C.eqb",)])
                      p.op("scalar", lambda s: s.activation(ekb[:], ekp[:], AF.Exp),
                           r=[("C.ekp", ii2) for ii2 in range(8)], w=[("C.ekb",)])
                      for ii, x in enumerate(chains_w):
                          ss = slice(128 * ii, 128 * (ii + 1))
                          p.op("vector", lambda v, x=x, ss=ss: v.tensor_scalar(
                              x["ek2"][:], ekb[:, ss], x["ccol"][:, 3:4], None, ALU.mult),
                              r=[("C.ekb",), ("C.cc", x["cp"], 3)], w=[x["KEK2"]])
                      for ii, x in enumerate(chains_w):
                          ss = slice(128 * ii, 128 * (ii + 1))
                          p.op("gpsimd", lambda g, x=x, cb=cb, ss=ss: g.tensor_mul(
                              x["qls"][:], qc[cb][:, x["hs"]], eqb[:, ss]),
                              r=[("C.QF", cb, x["h"]), ("C.eqb",)], w=[x["KQL"]])
                          p.op("gpsimd", lambda g, x=x, cb=cb, ss=ss: g.tensor_mul(
                              x["kx"][:], kc[cb][:, x["hs"]], ekb[:, ss]),
                              r=[("C.KF", cb, x["h"]), ("C.ekb",)], w=[x["KKX"]])
                          p.op("gpsimd", lambda g, x=x, cb=cb: g.tensor_mul(
                              x["kd"][:], kc[cb][:, x["hs"]], x["ek2"][:]),
                              r=[("C.KF", cb, x["h"]), x["KEK2"]], w=[x["KKD"]])
                      for x in chains_w:
                          p.op("tensor", lambda t, x=x: t.matmul(
                              x["psS"][:, 0:128], x["kx"][:], x["qls"][:], start=True, stop=True),
                              r=[x["KKX"], x["KQL"]], w=[x["psS"].k])
                          p.op("vector", lambda v, x=x: v.scalar_tensor_tensor(
                              x["Ssb"][:], x["psS"][:, 0:128], 1.0, mask_ut[:], ALU.mult, ALU.mult),
                              r=[x["psS"].k, ("mut",)], w=[x["KSS"]])
                          p.op("vector", lambda v, x=x: v.tensor_scalar(
                              x["stbf"][:], x["stv"], x["ccol"][:, 1:2], None, ALU.mult),
                              r=[x["stk"], ("C.cc", x["cp"], 1)], w=[x["KSB"]])
                      for x in chains_w:
                          def fo(t, x=x):
                              t.matmul(x["psO"][:, 0:128], vtm[:, x["hs"]], x["Ssb"][:],
                                       start=True, stop=False)
                              return t.matmul(x["psO"][:, 0:128], x["stbf"][:], x["qls"][:],
                                              start=False, stop=True)
                          p.op("tensor", fo, r=[("C.vtm", x["hs"].start), x["KSS"], x["KSB"], x["KQL"]],
                               w=[x["psO"].k])
                          if x["inst"] == 0:
                              p.op("vector", lambda v, x=x: v.tensor_scalar(
                                  ocomb[:, x["hs"]], x["psO"][:, 0:128], a01[:, 0:1], None, ALU.mult),
                                  r=[x["psO"].k, ("C.a01",)], w=[("C.oc", x["hs"].start)])
                          else:
                              obp = x["h"] % 2
                              p.op("vector", lambda v, x=x, obp=obp: v.scalar_tensor_tensor(
                                  ob_[obp][:], x["psO"][:, 0:128], a01[:, 1:2], ocomb[:, x["hs"]],
                                  ALU.mult, ALU.add),
                                  r=[x["psO"].k, ("C.a01",), ("C.oc", x["hs"].start)], w=[("C.ob", obp)])
                              p.op("sync", lambda e, x=x, sl=sl, obp=obp: e.dma_start(
                                  out=OIN[128 * x["h"]:128 * (x["h"] + 1), sl], in_=ob_[obp][:]),
                                  r=[("C.ob", obp)], w=[("C.attn", x["h"], sl.start)], dma=True)
                      for ii, x in enumerate(chains_w):
                          pst = (psT1, psT2)[ii % 2]
                          p.op("tensor", lambda t, x=x, pst=pst: t.transpose(
                              pst[:, 0:128], x["kd"][:], ident[:]),
                              r=[x["KKD"], ("ident",)], w=[pst.k])
                          p.op("vector", lambda v, x=x, pst=pst: v.tensor_copy(x["kdtm"][:], pst[:, 0:128]),
                               r=[pst.k], w=[x["KKDT"]])
                          p.op("tensor", lambda t, x=x: t.matmul(
                              x["psSD"][:, 0:128], x["kdtm"][:], vtm[:, x["hs"]], start=True, stop=True),
                              r=[x["KKDT"], ("C.vtm", x["hs"].start)], w=[x["psSD"].k])
                          p.op("vector", lambda v, x=x: v.scalar_tensor_tensor(
                              x["stv"], x["stv"], x["ccol"][:, 2:3], x["psSD"][:, 0:128],
                              ALU.mult, ALU.add),
                              r=[x["stk"], ("C.cc", x["cp"], 2), x["psSD"].k], w=[x["stk"]])

                wob = [sbp(f"C3w{i}", [128, TB], BF16) for i in range(NC)]
                atb = [sbp(f"C3a{i}", [128, TB], BF16) for i in range(NC)]
                rf = [sbp(f"C3r{i}", [128, TB], F32) for i in range(2)]
                ev3 = [sbp(f"C3e{i}", [128, TB], F32) for i in range(2)]
                for tb in (range(NTB) if 3 in c_sub else []):
                    for c in range(NC):
                        p.op("sync", lambda e, c=c, tb=tb: e.dma_start(
                            out=atb[c][:], in_=OIN[128 * c:128 * (c + 1), TB * tb:TB * (tb + 1)]),
                            r=[("C.attn", c, 128 * q) for q in range(4 * tb, 4 * tb + 4)],
                            w=[("C3.at", c)], dma=True)
                    for og in range(4):
                        par = og % 2
                        for c in range(NC):
                            p.op("gpsimd", lambda e, c=c, og=og: e.dma_start(
                                out=wob[c][:], in_=woT[128 * c:128 * (c + 1), TB * og:TB * (og + 1)]),
                                w=[("C3.wo", c)], dma=True)
                        for j in range(4):
                            m = 4 * og + j
                            pp = (psS_[0], psO_[0])[j % 2]
                            def fm3(t, pp=pp, par=par, j=j, tb=tb):
                                for c in range(NC):
                                    ins = t.matmul(pp[:, 0:TB], wob[c][:, 128 * j:128 * (j + 1)],
                                                   atb[c][:],
                                                   start=(c == 0), stop=(c == NC - 1))
                                return ins
                            p.op("tensor", fm3,
                                 r=[("C3.wo", c) for c in range(NC)]
                                   + [("C3.at", c) for c in range(NC)],
                                 w=[pp.k])
                            jp = j % 2
                            p.op("sync", lambda e, m=m, tb=tb, jp=jp: e.dma_start(
                                out=rf[jp][:], in_=R2[128 * m:128 * (m + 1), TB * tb:TB * (tb + 1)]),
                                r=[("RES_B", m, tb)], w=[("C3.rf", jp)], dma=True)
                            p.op("vector", lambda v, pp=pp, jp=jp: v.scalar_tensor_tensor(
                                ev3[jp][:], pp[:, 0:TB], 1.0, rf[jp][:], ALU.mult, ALU.add),
                                r=[pp.k, ("C3.rf", jp)], w=[("C3.ev", jp)])
                            p.op("scalar", lambda e, m=m, tb=tb, jp=jp: e.dma_start(
                                out=RC[128 * m:128 * (m + 1), TB * tb:TB * (tb + 1)], in_=ev3[jp][:]),
                                r=[("C3.ev", jp)], w=[("RES_C", m, tb)], dma=True)

        if "A" in phases:
            phase_A01(res_out["A"])
            phase_A2()
            phase_A3()
            phase_A4(res_out["A"])
        if "B" in phases or "D" in phases:
            drain_v_h0()
        if "B" in phases:
            ffn_phase("B", R1, lambda c, tb: ("RES_A", c, tb), res_out["B"], "RES_B")
        if "C" in phases:
            phase_C(res_out["C"])
        if "D" in phases:
            ffn_phase("Dp", R3, lambda c, tb: ("RES_C", c, tb), res_out["D"], "RES_D")

        lk = res_key[last_phase]
        final_keys = [(lk, m, tb) for m in range(NC) for tb in range(NTB)]
        final_keys = [k for k in final_keys if k in p.lastw]
        p.barrier()
        p.op("vector", lambda v: v.nop(), r=final_keys, w=[])

        mk, counts = p.finalize(sems)
        block = st.enter_context(nc.Block())
        block.sync(mk("sync"))
        block.tensor(mk("tensor"))
        block.vector(mk("vector"))
        block.scalar(mk("scalar"))
        block.gpsimd(mk("gpsimd"))
    return nc, counts


def host_inputs(inputs, core):
    d = {}
    d["xT"] = np.ascontiguousarray(inputs["L0_attn_a_x"][core].T)
    d["w_in"] = inputs["L0_attn_a_W_in"]
    cw = inputs["L0_attn_a_conv_w"]
    d["cw_h"] = np.ascontiguousarray(cw.transpose(2, 1, 0).reshape(4, D, NC, 128).transpose(2, 0, 1, 3))
    d["conv_b"] = np.ascontiguousarray(inputs["L0_attn_a_conv_b"].reshape(NC, 128).T)
    d["w_x"] = inputs["L0_attn_a_W_x"]
    d["w_dt"] = inputs["L0_attn_a_W_dt"]
    d["b_dt"] = np.ascontiguousarray(inputs["L0_attn_a_b_dt"].reshape(NC, 128).T)
    d["a_log"] = np.ascontiguousarray(inputs["L0_attn_a_A_log"].reshape(NC, 128, SSN).transpose(1, 0, 2).reshape(128, NC * SSN))
    d["d_vec"] = np.ascontiguousarray(inputs["L0_attn_a_D"].reshape(NC, 128).T)
    d["w_out"] = inputs["L0_attn_a_W_out"]
    d["wqT"] = np.ascontiguousarray(inputs["L0_attn_b_Wq"].T)
    d["wkT"] = np.ascontiguousarray(inputs["L0_attn_b_Wk"].T)
    d["wvT"] = np.ascontiguousarray(inputs["L0_attn_b_Wv"].T)
    d["woT"] = np.ascontiguousarray(inputs["L0_attn_b_Wo"].T)
    d["wg1T"] = np.ascontiguousarray(inputs["L0_attn_b_Wg1"].T)
    d["wg2T"] = np.ascontiguousarray(inputs["L0_attn_b_Wg2"].T)
    d["bg1"] = np.ascontiguousarray(inputs["L0_attn_b_bg1"].reshape(NC, 128).T)
    d["bg2"] = np.ascontiguousarray(inputs["L0_attn_b_bg2"].reshape(NC, 128).T)
    d["alpha"] = np.ascontiguousarray(inputs["L0_attn_b_alpha_list"].reshape(1, 2))
    d["lut"] = np.ascontiguousarray(inputs["L0_ffn_lut"].reshape(1, 16))
    d["gidxT"] = np.ascontiguousarray(inputs["L0_ffn_gate_idx"].T)
    d["uidxT"] = np.ascontiguousarray(inputs["L0_ffn_up_idx"].T)
    d["didxT"] = np.ascontiguousarray(inputs["L0_ffn_down_idx"].T)
    return d


_PHASES = ("A", "B", "C", "D")


def kernel(**inputs):
    nc, _ = build_nc(_PHASES)
    in_maps = [host_inputs(inputs, i) for i in range(8)]
    res = run_bass_kernel_spmd(nc, in_maps, core_ids=list(range(8)))
    out = np.stack([np.ascontiguousarray(r["outT"].T) for r in res.results])
    return out.astype(np.float32)


# version 52
# speedup vs baseline: 1.2295x; 1.0630x over previous
import contextlib
import numpy as np
import concourse.bass as bass
import concourse.mybir as mybir
from concourse.bass_utils import run_bass_kernel_spmd

F32 = mybir.dt.float32
BF16 = mybir.dt.bfloat16
I32 = mybir.dt.int32
AF = mybir.ActivationFunctionType
ALU = mybir.AluOpType

D = 2048
T = 2048
NC = D // 128
TB = 512
NTB = T // TB
SSN = 16
INTER = 5504
NO = 43

ENGS = ("sync", "tensor", "vector", "scalar", "gpsimd")
DMA_ENGS = ("sync", "scalar", "gpsimd", "vector")


class Prog:

    NSLOT = 8

    def __init__(self):
        self.ops = []
        self.lastw = {}
        self.readers = {}
        self.last_per_eng = {}
        self.last_dma_slot = {}
        self.dma_ord = {}
        self.bar = None

    def op(self, eng, fn, r=(), w=(), dma=False, qchain=False):
        ps_r = [k for k in r if isinstance(k, tuple) and len(k) > 0 and k[0] == "PS"]
        if ps_r:
            r = [k for k in r if k not in ps_r]
            w = list(w) + ps_r
        deps = set()
        for k in r:
            pr = self.lastw.get(k)
            if pr is not None:
                deps.add(pr)
        for k in w:
            pr = self.lastw.get(k)
            if pr is not None:
                deps.add(pr)
            deps.update(self.readers.get(k, ()))
        if self.bar is not None:
            deps.add(self.bar)
        i = len(self.ops)
        deps.discard(i)
        slot = None
        if dma:
            o = self.dma_ord.get(eng, 0)
            self.dma_ord[eng] = o + 1
            slot = o % self.NSLOT
            self.last_dma_slot[(eng, slot)] = i
        else:
            self.last_per_eng[eng] = i
        self.ops.append([eng, fn, dma, deps, False, slot, qchain])
        for k in r:
            self.readers.setdefault(k, set()).add(i)
        for k in w:
            self.lastw[k] = i
            self.readers[k] = set()
        return i

    def barrier(self):
        deps = set(self.last_per_eng.values()) | set(self.last_dma_slot.values())
        i = len(self.ops)
        self.ops.append(["vector", lambda v: v.nop(), False, deps, False, None, False])
        self.last_per_eng["vector"] = i
        self.bar = i

    def finalize(self, sems):
        ops = self.ops
        for o in ops:
            for d in o[3]:
                ops[d][4] = True
        for o in ops:
            if o[2]:
                o[4] = True
        cnt = {}
        val = [None] * len(ops)
        for i, (eng, fn, dma, deps, need, slot, qc) in enumerate(ops):
            if need:
                key = ("d", eng, slot) if dma else ("c", eng)
                cnt[key] = cnt.get(key, 0) + (16 if dma else 1)
                val[i] = (key, cnt[key])
        streams = {e: [] for e in ENGS}
        for i, (eng, fn, dma, deps, need, slot, qc) in enumerate(ops):
            waits = {}
            for d in deps:
                if eng == "tensor" and not dma and ops[d][0] == "tensor" and not ops[d][2]:
                    continue
                assert val[d] is not None
                k, v = val[d]
                if waits.get(k, -1) < v:
                    waits[k] = v
            streams[eng].append((i, fn, sorted(waits.items()), val[i]))
        counts = {e: (len(streams[e]), sum(len(w) for _, _, w, _ in streams[e]))
                  for e in ENGS}

        def mk(eng_name):
            def emit(eng):
                known = {}
                for i, fn, waits, v in streams[eng_name]:
                    for k, req in waits:
                        if known.get(k, -1) < req:
                            eng.wait_ge(sems[k], req)
                            known[k] = req
                    inst = fn(eng)
                    if v is not None:
                        assert inst is not None, f"op {i} on {eng_name} needs inc but emitted nothing"
                        inst.then_inc(sems[v[0]], 16 if v[0][0] == "d" else 1)
            return emit
        return mk, counts


class PS:
    def __init__(self, stack, nc, name, dt=F32):
        n = 512 if dt == F32 else 1024
        self.t = stack.enter_context(nc.psum_tensor(name, [128, n], dt))
        self.k = ("PS", name)

    def __getitem__(self, sl):
        return self.t[sl]


def build_nc(phases=("A", "B", "C", "D"), dbg=False, c_sub=(1, 2, 3)):
    nc = bass.Bass("TRN2", target_bir_lowering=False)
    p = Prog()

    def din(name, shape, dt=F32):
        return nc.dram_tensor(name, list(shape), dt, kind="ExternalInput").ap()

    xT = din("xT", [D, T])
    w_in = din("w_in", [D, 2 * D])
    cw_h = din("cw_h", [NC, 4, D, 128])
    conv_b = din("conv_b", [128, NC])
    w_x = din("w_x", [D, 160])
    w_dt = din("w_dt", [128, D])
    b_dt = din("b_dt", [128, NC])
    a_log = din("a_log", [128, NC * SSN])
    d_vec = din("d_vec", [128, NC])
    w_out = din("w_out", [D, D])
    wqT = din("wqT", [D, D])
    wkT = din("wkT", [D, D])
    wvT = din("wvT", [D, D])
    woT = din("woT", [D, D])
    wg1T = din("wg1T", [D, D])
    wg2T = din("wg2T", [D, D])
    bg1 = din("bg1", [128, NC])
    bg2 = din("bg2", [128, NC])
    alpha = din("alpha", [1, 2])
    lut = din("lut", [1, 16])
    gidxT = din("gidxT", [D, INTER], I32)
    uidxT = din("uidxT", [D, INTER], I32)
    didxT = din("didxT", [INTER, D], I32)

    outT = nc.dram_tensor("outT", [D, T], F32, kind="ExternalOutput").ap()

    def scr(name, shape, dt, addr_space="Local"):
        kind = "ExternalOutput" if dbg else "Internal"
        return nc.dram_tensor(name, list(shape), dt, kind=kind, addr_space=addr_space).ap()

    XC = scr("XC", [D, 3 + T], BF16)
    if dbg:
        DBG_DL = scr("DBG_DL", [D, TB], BF16)
        DBG_DU = scr("DBG_DU", [D, TB], BF16)
        DBG_B = scr("DBG_B", [128, TB], BF16)
        DBG_C = scr("DBG_C", [128, TB], F32)
        DBG_H = scr("DBG_H", [128, NC * TB], F32)
        DBG_Y = scr("DBG_Y", [D, TB], F32)
        DBG_U = scr("DBG_U", [D, TB], BF16)
        DBG_DP = scr("DBG_DP", [128, TB], BF16)
        DBG_BC = scr("DBG_BC", [16, TB], BF16)
        DBG_CC = scr("DBG_CC", [16, TB], BF16)
        DBG_X2 = scr("DBG_X2", [D, TB], BF16)
    RR = scr("RR", [D, T], BF16)
    XS = scr("XS", [D, T], BF16)
    OIN = scr("OIN", [D, T], BF16)
    R1 = scr("R1", [D, T], F32)
    R2 = scr("R2", [D, T], F32)
    R3 = scr("R3", [D, T], F32)
    WG = scr("WGd", [D, INTER], BF16)
    WU = scr("WUd", [D, INTER], BF16)
    WD = scr("WDd", [INTER, D], BF16)
    HB = scr("HBd", [INTER, T], BF16)
    QF = scr("QFd", [D, T], BF16)
    KF = scr("KFd", [D, T], BF16)
    VF = scr("VFd", [D, T], BF16)
    G1 = scr("G1d", [D, T], BF16)
    G2 = scr("G2d", [D, T], BF16)

    last_phase = phases[-1]
    res_out = {"A": R1, "B": R2, "C": R3, "D": outT}
    res_out[last_phase] = outT
    res_key = {"A": "RES_A", "B": "RES_B", "C": "RES_C", "D": "RES_D"}

    st = contextlib.ExitStack()
    with st:
        sems = {}
        for e in ENGS:
            sems[("c", e)] = st.enter_context(nc.semaphore(f"c_{e}"))
        for e in DMA_ENGS:
            for s_ in range(Prog.NSLOT):
                sems[("d", e, s_)] = st.enter_context(nc.semaphore(f"d_{e}{s_}"))

        sb = lambda name, shape, dt: st.enter_context(nc.sbuf_tensor(name, list(shape), dt))
        ident = sb("ident", [128, 128], BF16)
        ones_k1_f = sb("ones_k1_f", [1, 128], F32)
        ones_m1_bf = sb("ones_m1_bf", [128, 1], BF16)
        zeros512 = sb("zeros512", [128, 512], BF16)
        sel = sb("sel", [16, 16 * 128], BF16)
        onesrow = sb("onesrow", [1, 128], BF16)
        epscol = sb("epscol", [128, 1], F32)
        onecol = sb("onecol", [128, 1], F32)

        def g_init(g):
            g.memset(ident[:], 1.0)
            g.affine_select(out=ident[:], in_=ident[:], pattern=[[1, 128]],
                            compare_op=ALU.is_ge, fill=0.0, channel_multiplier=-1)
            return g.affine_select(out=ident[:], in_=ident[:], pattern=[[-1, 128]],
                                   compare_op=ALU.is_ge, fill=0.0, channel_multiplier=1)
        p.op("gpsimd", g_init, w=[("ident",)])
        p.op("gpsimd", lambda g: g.memset(ones_k1_f[:], 1.0), w=[("o2",)])
        p.op("gpsimd", lambda g: g.memset(ones_m1_bf[:], 1.0), w=[("o3",)])
        p.op("gpsimd", lambda g: g.memset(zeros512[:], 0.0), w=[("z5",)])
        p.op("gpsimd", lambda g: g.memset(sel[:], 0.0), w=[("sel",)])
        p.op("gpsimd", lambda g: g.memset(onesrow[:], 1.0), w=[("orow",)])
        p.op("gpsimd", lambda g: g.memset(epscol[:], 1e-5), w=[("eps",)])
        p.op("gpsimd", lambda g: g.memset(onecol[:], 1.0), w=[("onec",)])
        for n in range(16):
            p.op("gpsimd", lambda g, n=n: g.dma_start(out=sel[n:n + 1, 128 * n:128 * (n + 1)], in_=onesrow[:]),
                 r=[("orow",), ("sel",)], w=[("sel", n)], dma=True)

        def sel_n(n):
            return sel[0:16, 128 * n:128 * (n + 1)]

        mask_ut = sb("mask_ut", [128, 128], BF16)
        p.op("gpsimd", lambda g: g.memset(mask_ut[:], 1.0), w=[("mut",)])
        p.op("gpsimd", lambda g: g.affine_select(out=mask_ut[:], in_=mask_ut[:], pattern=[[1, 128]],
                                                 compare_op=ALU.is_ge, fill=0.0, channel_multiplier=-1),
             r=[("mut",)], w=[("mut",)])
        lutbc = sb("lutbc", [128, 16], F32)
        p.op("sync", lambda e: e.dma_start(out=lutbc[0:1, :], in_=lut), w=[("lut", 0)], dma=True)
        for dbl in range(7):
            lo, hi = 1 << dbl, 1 << (dbl + 1)
            p.op("sync", lambda e, lo=lo, hi=hi: e.dma_start(out=lutbc[lo:hi, :], in_=lutbc[0:lo, :]),
                 r=[("lut", dbl)], w=[("lut", dbl + 1)], dma=True)
        LUTK = ("lut", 7)

        def ln_phase(tag, src_dram, src_key, xn_pool, scope, psA, psB, psC, tick=None):
            sbp = lambda name, shape, dt: scope.enter_context(nc.sbuf_tensor(name, list(shape), dt))
            xf = [sbp(f"{tag}xf{i}", [128, TB], BF16) for i in range(4)]
            sq = [sbp(f"{tag}sq{i}", [128, TB], BF16) for i in range(2)]
            rows = [sbp(f"{tag}rw{i}", [1, TB], F32) for i in range(6)]
            bca = [sbp(f"{tag}bc{i}", [128, TB], F32) for i in range(2)]
            K = lambda *a: (tag,) + a
            for tb in range(NTB):
                for c in range(NC):
                    p.op("gpsimd", lambda e, c=c, tb=tb: e.dma_start(
                        out=xf[c % 4][:], in_=src_dram[128 * c:128 * (c + 1), TB * tb:TB * (tb + 1)]),
                        r=[src_key(c, tb)], w=[K("xf", c % 4)], dma=True)
                    if tick:
                        tick(1)
                    p.op("scalar", lambda s, c=c: s.square(sq[c % 2][:], xf[c % 4][:]),
                         r=[K("xf", c % 4)], w=[K("sq", c % 2)])
                    p.op("tensor", lambda t, c=c: t.matmul(psA[0:1, :], ones_m1_bf[:], xf[c % 4][:],
                                                           start=(c == 0), stop=(c == NC - 1)),
                         r=[K("xf", c % 4), ("o3",)], w=[psA.k])
                    p.op("tensor", lambda t, c=c: t.matmul(psB[0:1, :], ones_m1_bf[:], sq[c % 2][:],
                                                           start=(c == 0), stop=(c == NC - 1)),
                         r=[K("sq", c % 2), ("o3",)], w=[psB.k])
                p.op("scalar", lambda s: s.mul(rows[0][:], psA[0:1, :], 1.0 / D),
                     r=[psA.k], w=[K("r0")])
                p.op("scalar", lambda s: s.mul(rows[1][:], psB[0:1, :], 1.0 / D),
                     r=[psB.k], w=[K("r1")])
                p.op("vector", lambda v: v.tensor_mul(rows[2][:], rows[0][:], rows[0][:]),
                     r=[K("r0")], w=[K("r2")])
                p.op("vector", lambda v: v.tensor_sub(rows[3][:], rows[1][:], rows[2][:]),
                     r=[K("r1"), K("r2")], w=[K("r3")])
                p.op("scalar", lambda s: s.activation(rows[4][:], rows[3][:], AF.Ln, bias=epscol[0:1, 0:1]),
                     r=[K("r3"), ("eps",)], w=[K("r4")])
                p.op("scalar", lambda s: s.activation(rows[4][:], rows[4][:], AF.Exp, scale=-0.5),
                     r=[K("r4")], w=[K("r4")])
                p.op("vector", lambda v: v.tensor_mul(rows[5][:], rows[0][:], rows[4][:]),
                     r=[K("r0"), K("r4")], w=[K("r5")])
                p.op("tensor", lambda t: t.matmul(psC[:, 0:TB], ones_k1_f[:], rows[4][:], start=True, stop=True),
                     r=[K("r4"), ("o2",)], w=[psC.k])
                p.op("vector", lambda v: v.tensor_copy(bca[0][:], psC[:, 0:TB]),
                     r=[psC.k], w=[K("bc0")])
                p.op("tensor", lambda t: t.matmul(psC[:, 0:TB], ones_k1_f[:], rows[5][:], start=True, stop=True),
                     r=[K("r5"), ("o2",)], w=[psC.k])
                p.op("vector", lambda v: v.tensor_copy(bca[1][:], psC[:, 0:TB]),
                     r=[psC.k], w=[K("bc1")])
                for c in range(NC):
                    p.op("gpsimd", lambda e, c=c, tb=tb: e.dma_start(
                        out=xf[c % 4][:], in_=src_dram[128 * c:128 * (c + 1), TB * tb:TB * (tb + 1)]),
                        r=[src_key(c, tb)], w=[K("xf", c % 4)], dma=True)
                    p.op("vector", lambda v, c=c, tb=tb: v.scalar_tensor_tensor(
                        xn_pool[c][:, TB * tb:TB * (tb + 1)],
                        xf[c % 4][:], 1.0, bca[0][:], ALU.mult, ALU.mult),
                        r=[K("xf", c % 4), K("bc0")], w=[K("xnt", c)])
                    p.op("vector", lambda v, c=c, tb=tb: v.tensor_sub(
                        xn_pool[c][:, TB * tb:TB * (tb + 1)],
                        xn_pool[c][:, TB * tb:TB * (tb + 1)], bca[1][:]),
                        r=[K("xnt", c), K("bc1")], w=[K("xn", c, tb)])
            return lambda c, tb: K("xn", c, tb)

        def phase_A01(RA):
            with contextlib.ExitStack() as ph:
                sbp = lambda name, shape, dt: ph.enter_context(nc.sbuf_tensor(name, list(shape), dt))
                ps0 = PS(ph, nc, "pA0"); ps1 = PS(ph, nc, "pA1"); ps2 = PS(ph, nc, "pA2")
                xn = [sbp(f"Axn{i}", [128, T], BF16) for i in range(NC)]
                xnk = ln_phase("A0", xT, lambda c, tb: ("xTd", c, tb), xn, ph, ps0, ps1, ps2,
                               tick=lambda n: tick_v(1))
                wh = [sbp(f"Awh{i}", [128, D // 2], BF16) for i in range(NC)]
                ev = [sbp(f"Aev{i}", [128, TB], BF16) for i in range(2)]
                for hf in range(2):
                  for mh in range(2):
                    for c in range(NC):
                        p.op("gpsimd", lambda e, c=c, hf=hf, mh=mh: e.dma_start(
                            out=wh[c][:], in_=w_in[128 * c:128 * (c + 1),
                                                   D * hf + 1024 * mh:D * hf + 1024 * (mh + 1)]),
                            w=[("A.wh", c)], dma=True)
                    for m in range(8 * mh, 8 * mh + 8):
                        for tb in range(NTB):
                            pp = (ps0, ps1)[(m * NTB + tb) % 2]
                            def fm(t, m=m, tb=tb, pp=pp, mh=mh):
                                for c in range(NC):
                                    ins = t.matmul(pp[:, 0:TB],
                                                   wh[c][:, 128 * (m - 8 * mh):128 * (m - 8 * mh + 1)],
                                                   xn[c][:, TB * tb:TB * (tb + 1)],
                                                   start=(c == 0), stop=(c == NC - 1))
                                return ins
                            p.op("tensor", fm,
                                 r=[("A.wh", c) for c in range(NC)] + [xnk(c, tb) for c in range(NC)],
                                 w=[pp.k])
                            eb = ev[(m * NTB + tb) % 2]
                            ebk = ("A.ev", (m * NTB + tb) % 2)
                            p.op("scalar", lambda s, pp=pp, eb=eb: s.copy(eb[:], pp[:, 0:TB]),
                                 r=[pp.k], w=[ebk])
                            if hf == 0:
                                p.op("scalar", lambda e, m=m, tb=tb, eb=eb: e.dma_start(
                                    out=XC[128 * m:128 * (m + 1), 3 + TB * tb:3 + TB * (tb + 1)], in_=eb[:]),
                                    r=[ebk], w=[("XC", m, tb)], dma=True)
                            else:
                                p.op("scalar", lambda e, m=m, tb=tb, eb=eb: e.dma_start(
                                    out=RR[128 * m:128 * (m + 1), TB * tb:TB * (tb + 1)], in_=eb[:]),
                                    r=[ebk], w=[("RR", m, tb)], dma=True)
                            tick_v(4)
                zpad = sbp("Azpad", [128, 3], BF16)
                p.op("gpsimd", lambda g: g.memset(zpad[:], 0.0), w=[("A.zpad",)])
                for m in range(NC):
                    p.op("scalar", lambda e, m=m: e.dma_start(out=XC[128 * m:128 * (m + 1), 0:3], in_=zpad[:]),
                         r=[("A.zpad",)], w=[("XC", m, "pad")], dma=True)

        def phase_A2():
            with contextlib.ExitStack() as ph:
                p.barrier()
                sbp = lambda name, shape, dt: ph.enter_context(nc.sbuf_tensor(name, list(shape), dt))
                ps0 = PS(ph, nc, "pA20"); ps1 = PS(ph, nc, "pA21")
                xc = [sbp(f"A2xc{i}", [128, 3 + T], BF16) for i in range(NC)]
                cwb = [sbp(f"A2cw{i}", [128, D], BF16) for i in range(8)]
                cb16 = sbp("A2cb", [128, 16], F32)
                ev = [sbp(f"A2ev{i}", [128, TB], BF16) for i in range(2)]
                p.op("sync", lambda e: e.dma_start(out=cb16[:], in_=conv_b),
                     w=[("A.cb16",)], dma=True)
                for c in range(NC):
                    p.op("sync", lambda e, c=c: e.dma_start(out=xc[c][:], in_=XC[128 * c:128 * (c + 1), :]),
                         r=[("XC", c, "pad")] + [("XC", c, tb) for tb in range(NTB)],
                         w=[("A.xc", c)], dma=True)
                for m in range(NC):
                    par = m % 2
                    for k in range(4):
                        p.op("gpsimd", lambda e, m=m, k=k, par=par: e.dma_start(
                            out=cwb[4 * par + k][:].rearrange("p (c o) -> p c o", c=NC),
                            in_=cw_h[m, k].rearrange("(c p) o -> p c o", p=128)),
                            w=[("A.cwb", 4 * par + k)], dma=True)
                    for tb in range(NTB):
                        pp = (ps0, ps1)[tb % 2]
                        def fm(t, m=m, tb=tb, pp=pp, par=par):
                            for k in range(4):
                                for c in range(NC):
                                    ins = t.matmul(pp[:, 0:TB], cwb[4 * par + k][:, 128 * c:128 * (c + 1)],
                                                   xc[c][:, TB * tb + k:TB * tb + k + TB],
                                                   start=(k == 0 and c == 0), stop=(k == 3 and c == NC - 1))
                            return ins
                        p.op("tensor", fm,
                             r=[("A.cwb", 4 * par + k) for k in range(4)] + [("A.xc", c) for c in range(NC)],
                             w=[pp.k])
                        eb = ev[tb % 2]
                        ebk = ("A.ev2", tb % 2)
                        p.op("vector", lambda v, pp=pp, eb=eb, m=m: v.tensor_scalar(
                            eb[:], pp[:, 0:TB], cb16[:, m:m + 1], None, ALU.add),
                            r=[pp.k, ("A.cb16",)], w=[ebk])
                        p.op("scalar", lambda e, m=m, tb=tb, eb=eb: e.dma_start(
                            out=XS[128 * m:128 * (m + 1), TB * tb:TB * (tb + 1)], in_=eb[:]),
                            r=[ebk], w=[("XS", m, tb)], dma=True)
                        tick_v(10)

        def phase_A3():
            with contextlib.ExitStack() as ph:
                p.barrier()
                sbp = lambda name, shape, dt: ph.enter_context(nc.sbuf_tensor(name, list(shape), dt))
                ps0 = PS(ph, nc, "pA30"); ps1 = PS(ph, nc, "pA31")
                ps2 = PS(ph, nc, "pA32"); ps3 = PS(ph, nc, "pA33")
                wxr = [sbp(f"A3wx{i}", [128, 160], BF16) for i in range(NC)]
                wdtr = sbp("A3wdt", [128, D], BF16)
                nA = sbp("A3nA", [128, 256], F32)
                bdt16 = sbp("A3bdt", [128, 16], F32)
                dv16 = sbp("A3dv", [128, 16], F32)
                xs = [sbp(f"A3xs{i}", [128, TB], BF16) for i in range(2)]
                u = [sbp(f"A3u{i}", [128, TB], BF16) for i in range(8)]
                dpw = sbp("A3dpw", [128, T], BF16)
                Bw = sbp("A3Bw", [16, T], BF16)
                Cw = sbp("A3Cw", [16, T], BF16)
                uw = [sbp(f"A3uw{i}", [128, T], BF16) for i in range(2)]
                saw = sbp("A3saw", [128, T], BF16)
                sbw = sbp("A3sbw", [128, T], BF16)
                dlt = [sbp(f"A3dl{i}", [128, T], BF16) for i in range(2)]
                duw = [sbp(f"A3du{i}", [128, T], BF16) for i in range(2)]
                dAw = [sbp(f"A3dA{i}", [128, T], BF16) for i in range(2)]
                dBw = [sbp(f"A3dB{i}", [128, T], BF16) for i in range(2)]
                Hw = [sbp(f"A3H{i}", [128, T], BF16) for i in range(2)]
                tmpw = [sbp(f"A3tp{i}", [128, T], BF16) for i in range(4)]
                Bbw = [sbp(f"A3Bb{i}", [128, T], BF16) for i in range(2)]
                Cbw = [sbp(f"A3Cb{i}", [128, T], BF16) for i in range(2)]
                yaw = [sbp(f"A3ya{i}", [128, T], F32) for i in range(2)]
                rrw = [sbp(f"A3rr{i}", [128, T], BF16) for i in range(2)]
                oinw = [sbp(f"A3oi{i}", [128, T], BF16) for i in range(2)]

                for c in range(NC):
                    p.op("gpsimd", lambda e, c=c: e.dma_start(out=wxr[c][:], in_=w_x[128 * c:128 * (c + 1), :]),
                         w=[("A.wx", c)], dma=True)
                p.op("gpsimd", lambda e: e.dma_start(out=wdtr[:], in_=w_dt), w=[("A.wdt",)], dma=True)
                p.op("sync", lambda e: e.dma_start(out=nA[:], in_=a_log),
                     w=[("A.nA",)], dma=True)
                p.op("scalar", lambda s: s.activation(nA[:], nA[:], AF.Exp), r=[("A.nA",)], w=[("A.nA",)])
                p.op("vector", lambda v: v.tensor_scalar(nA[:], nA[:], -1.0, None, ALU.mult),
                     r=[("A.nA",)], w=[("A.nA",)])
                p.op("sync", lambda e: e.dma_start(out=bdt16[:], in_=b_dt),
                     w=[("A.bdt",)], dma=True)
                p.op("sync", lambda e: e.dma_start(out=dv16[:], in_=d_vec),
                     w=[("A.dv",)], dma=True)

                for tb in range(NTB):
                    for chf in range(2):
                        for c8 in range(8):
                            c = 8 * chf + c8
                            p.op("sync", lambda e, c=c, tb=tb: e.dma_start(
                                out=xs[c % 2][:], in_=XS[128 * c:128 * (c + 1), TB * tb:TB * (tb + 1)]),
                                r=[("XS", c, tb)], w=[("A.xs", c % 2)], dma=True)
                            p.op("scalar", lambda s, c=c, c8=c8: s.activation(u[c8][:], xs[c % 2][:], AF.Silu),
                                 r=[("A.xs", c % 2)], w=[("A.u", c8)])
                        uks = [("A.u", c8) for c8 in range(8)]
                        wxk = [("A.wx", 8 * chf + c8) for c8 in range(8)]

                        def f_dp(t, chf=chf):
                            for c8 in range(8):
                                ins = t.matmul(ps0[:, 0:TB], wxr[8 * chf + c8][:, 0:128], u[c8][:],
                                               start=(chf == 0 and c8 == 0), stop=(chf == 1 and c8 == 7))
                            return ins
                        p.op("tensor", f_dp, r=wxk + uks, w=[ps0.k])

                        def f_B(t, chf=chf):
                            for c8 in range(8):
                                ins = t.matmul(ps1[0:16, 0:TB], wxr[8 * chf + c8][:, 128:144], u[c8][:],
                                               start=(chf == 0 and c8 == 0), stop=(chf == 1 and c8 == 7))
                            return ins
                        p.op("tensor", f_B, r=wxk + uks, w=[ps1.k])

                        def f_C(t, chf=chf):
                            for c8 in range(8):
                                ins = t.matmul(ps2[0:16, 0:TB], wxr[8 * chf + c8][:, 144:160], u[c8][:],
                                               start=(chf == 0 and c8 == 0), stop=(chf == 1 and c8 == 7))
                            return ins
                        p.op("tensor", f_C, r=wxk + uks, w=[ps2.k])
                    p.op("scalar", lambda s, tb=tb: s.copy(dpw[:, TB * tb:TB * (tb + 1)], ps0[:, 0:TB]),
                         r=[ps0.k], w=[("A.dpw", tb)])
                    p.op("scalar", lambda s, tb=tb: s.copy(Bw[:, TB * tb:TB * (tb + 1)], ps1[0:16, 0:TB]),
                         r=[ps1.k], w=[("A.Bw", tb)])
                    p.op("scalar", lambda s, tb=tb: s.copy(Cw[:, TB * tb:TB * (tb + 1)], ps2[0:16, 0:TB]),
                         r=[ps2.k], w=[("A.Cw", tb)])
                dpk = [("A.dpw", tb) for tb in range(NTB)]
                Bwk = [("A.Bw", tb) for tb in range(NTB)]
                Cwk = [("A.Cw", tb) for tb in range(NTB)]

                for cg in range(8):
                    for ci in range(2):
                        c = 2 * cg + ci
                        kuw = ("A.uw", c % 2)
                        p.op("sync", lambda e, c=c: e.dma_start(
                            out=uw[c % 2][:], in_=XS[128 * c:128 * (c + 1), :]),
                            r=[("XS", c, tb) for tb in range(NTB)], w=[kuw], dma=True)
                        p.op("scalar", lambda s, c=c: s.activation(uw[c % 2][:], uw[c % 2][:], AF.Silu),
                             r=[kuw], w=[kuw])
                        for tb in range(NTB):
                            pp = (ps0, ps3)[tb % 2]
                            p.op("tensor", lambda t, c=c, tb=tb, pp=pp: t.matmul(
                                pp[:, 0:TB], wdtr[:, 128 * c:128 * (c + 1)], dpw[:, TB * tb:TB * (tb + 1)],
                                start=True, stop=True),
                                r=[("A.wdt",)] + dpk, w=[pp.k])
                            p.op("scalar", lambda s, pp=pp, tb=tb, c=c: s.activation(
                                saw[:, TB * tb:TB * (tb + 1)], pp[:, 0:TB], AF.Abs, bias=bdt16[:, c:c + 1]),
                                r=[pp.k, ("A.bdt",)], w=[("A.saw", tb)])
                            p.op("scalar", lambda s, pp=pp, tb=tb, c=c: s.activation(
                                sbw[:, TB * tb:TB * (tb + 1)], pp[:, 0:TB], AF.Relu, bias=bdt16[:, c:c + 1]),
                                r=[pp.k, ("A.bdt",)], w=[("A.sbw", tb)])
                        sawk = [("A.saw", tb) for tb in range(NTB)]
                        sbwk = [("A.sbw", tb) for tb in range(NTB)]
                        p.op("scalar", lambda s: s.activation(saw[:], saw[:], AF.Exp, scale=-1.0),
                             r=sawk, w=[("A.sawf",)])
                        p.op("scalar", lambda s: s.activation(saw[:], saw[:], AF.Ln, bias=onecol[:, 0:1]),
                             r=[("A.sawf",), ("onec",)], w=[("A.sawf",)])
                        p.op("vector", lambda v, ci=ci: v.tensor_add(dlt[ci][:], saw[:], sbw[:]),
                             r=[("A.sawf",)] + sbwk, w=[("A.dl", ci)])
                        p.op("vector", lambda v, ci=ci: v.tensor_mul(duw[ci][:], dlt[ci][:], uw[ci][:]),
                             r=[("A.dl", ci), ("A.uw", ci)], w=[("A.du", ci)])
                    for n in range(16):
                        npar = n % 2
                        Bb, Cb = Bbw[npar], Cbw[npar]
                        KB, KC = ("A.Bbw", npar), ("A.Cbw", npar)
                        for tb in range(NTB):
                            pp = (ps1, ps2)[tb % 2]
                            p.op("tensor", lambda t, n=n, tb=tb, pp=pp: t.matmul(
                                pp[:, 0:TB], sel_n(n), Bw[:, TB * tb:TB * (tb + 1)], start=True, stop=True),
                                r=Bwk + [("sel", n)], w=[pp.k])
                            p.op("scalar", lambda s, tb=tb, pp=pp, Bb=Bb: s.copy(
                                Bb[:, TB * tb:TB * (tb + 1)], pp[:, 0:TB]), r=[pp.k], w=[(KB, tb)])
                            p.op("tensor", lambda t, n=n, tb=tb, pp=pp: t.matmul(
                                pp[:, 0:TB], sel_n(n), Cw[:, TB * tb:TB * (tb + 1)], start=True, stop=True),
                                r=Cwk + [("sel", n), (KB, tb)], w=[pp.k])
                            p.op("scalar", lambda s, tb=tb, pp=pp, Cb=Cb: s.copy(
                                Cb[:, TB * tb:TB * (tb + 1)], pp[:, 0:TB]), r=[pp.k], w=[(KC, tb)])
                        KBf = [(KB, tb) for tb in range(NTB)]
                        KCf = [(KC, tb) for tb in range(NTB)]
                        for ci in range(2):
                            c = 2 * cg + ci
                            rot = (2 * n + ci) % 2
                            p.op("scalar", lambda s, c=c, n=n, rot=rot, ci=ci: s.activation(
                                dAw[rot][:], dlt[ci][:], AF.Exp, scale=nA[:, 16 * c + n:16 * c + n + 1]),
                                r=[("A.dl", ci), ("A.nA",)], w=[("A.dAw", rot)])
                            p.op("vector", lambda v, ci=ci, rot=rot, Bb=Bb: v.tensor_mul(
                                dBw[rot][:], duw[ci][:], Bb[:]),
                                r=[("A.du", ci)] + KBf, w=[("A.dBw", rot)])
                            p.op("vector", lambda v, rot=rot: v.tensor_tensor_scan(
                                Hw[rot][:], dAw[rot][:], dBw[rot][:], 0.0, ALU.mult, ALU.add),
                                r=[("A.dAw", rot), ("A.dBw", rot)], w=[("A.Hw", rot)])
                            tpi = (2 * n + ci) % 4
                            p.op("vector", lambda v, rot=rot, tpi=tpi, Cb=Cb: v.tensor_mul(
                                tmpw[tpi][:], Hw[rot][:], Cb[:]),
                                r=[("A.Hw", rot)] + KCf, w=[("A.tpw", tpi)])
                            if n == 0:
                                p.op("gpsimd", lambda g, ci=ci, tpi=tpi: g.dma_start(
                                    out=yaw[ci][:], in_=tmpw[tpi][:], max_dma_last_dim=2048),
                                    r=[("A.tpw", tpi)], w=[("A.yaw", ci)], dma=True, qchain=True)
                            else:
                                p.op("gpsimd", lambda g, ci=ci, tpi=tpi: g.dma_start(
                                    out=yaw[ci][:], in_=tmpw[tpi][:], accum_op=ALU.add,
                                    max_dma_last_dim=2048),
                                    r=[("A.tpw", tpi), ("A.yaw", ci)], w=[("A.yaw", ci)], dma=True, qchain=True)
                    for ci in range(2):
                        c = 2 * cg + ci
                        p.op("sync", lambda e, c=c, ci=ci: e.dma_start(
                            out=rrw[ci][:], in_=RR[128 * c:128 * (c + 1), :]),
                            r=[("RR", c, tb) for tb in range(NTB)], w=[("A.rrw", ci)], dma=True)
                        p.op("scalar", lambda s, ci=ci: s.activation(rrw[ci][:], rrw[ci][:], AF.Silu),
                             r=[("A.rrw", ci)], w=[("A.rrw", ci)])
                        p.op("vector", lambda v, c=c, ci=ci: v.scalar_tensor_tensor(
                            oinw[ci][:], uw[ci][:], dv16[:, c:c + 1], yaw[ci][:], ALU.mult, ALU.add),
                            r=[("A.uw", ci), ("A.dv",), ("A.yaw", ci)], w=[("A.oiw", ci)])
                        p.op("vector", lambda v, ci=ci: v.tensor_mul(oinw[ci][:], oinw[ci][:], rrw[ci][:]),
                             r=[("A.oiw", ci), ("A.rrw", ci)], w=[("A.oiw", ci)])
                        p.op("scalar", lambda e, c=c, ci=ci: e.dma_start(
                            out=OIN[128 * c:128 * (c + 1), :], in_=oinw[ci][:]),
                            r=[("A.oiw", ci)], w=[("OIN", c, 0), ("OIN", c, 1), ("OIN", c, 2), ("OIN", c, 3)],
                            dma=True)

        def phase_A4(RA):
            with contextlib.ExitStack() as ph:
                p.barrier()
                sbp = lambda name, shape, dt: ph.enter_context(nc.sbuf_tensor(name, list(shape), dt))
                ps0 = PS(ph, nc, "pA40"); ps1 = PS(ph, nc, "pA41")
                oin = [sbp(f"A4o{i}", [128, TB], BF16) for i in range(NC)]
                xs = [sbp(f"A4x{i}", [128, TB], BF16) for i in range(NC)]
                wob = [sbp(f"A4w{i}", [128, TB], BF16) for i in range(2 * NC)]
                xf = [sbp(f"A4f{i}", [128, TB], F32) for i in range(2)]
                rof = [sbp(f"A4r{i}", [128, TB], F32) for i in range(2)]
                for tb in range(NTB):
                    for c in range(NC):
                        p.op("sync", lambda e, c=c, tb=tb: e.dma_start(
                            out=oin[c][:], in_=OIN[128 * c:128 * (c + 1), TB * tb:TB * (tb + 1)]),
                            r=[("OIN", c, tb)], w=[("A4.oin", c)], dma=True)
                        p.op("sync", lambda e, c=c, tb=tb: e.dma_start(
                            out=xs[c][:], in_=XS[128 * c:128 * (c + 1), TB * tb:TB * (tb + 1)]),
                            r=[("XS", c, tb)], w=[("A4.xs", c)], dma=True)
                    for og in range(4):
                        par = og % 2
                        for c in range(NC):
                            p.op("gpsimd", lambda e, c=c, og=og, par=par: e.dma_start(
                                out=wob[NC * par + c][:], in_=w_out[128 * c:128 * (c + 1), TB * og:TB * (og + 1)]),
                                w=[("A4.wo", NC * par + c)], dma=True)
                        for j in range(4):
                            m = 4 * og + j
                            pp = (ps0, ps1)[j % 2]
                            def fm(t, pp=pp, par=par, j=j):
                                for c in range(NC):
                                    ins = t.matmul(pp[:, 0:TB], wob[NC * par + c][:, 128 * j:128 * (j + 1)],
                                                   oin[c][:], start=(c == 0), stop=(c == NC - 1))
                                return ins
                            p.op("tensor", fm,
                                 r=[("A4.wo", NC * par + c) for c in range(NC)] + [("A4.oin", c) for c in range(NC)],
                                 w=[pp.k])
                            jp = j % 2
                            p.op("sync", lambda e, m=m, tb=tb, jp=jp: e.dma_start(
                                out=xf[jp][:], in_=xT[128 * m:128 * (m + 1), TB * tb:TB * (tb + 1)]),
                                w=[("A4.xf", jp)], dma=True)
                            p.op("vector", lambda v, pp=pp, m=m, jp=jp: v.scalar_tensor_tensor(
                                rof[jp][:], pp[:, 0:TB], 1.0, xs[m][:], ALU.mult, ALU.add),
                                r=[pp.k, ("A4.xs", m)], w=[("A4.ro", jp)])
                            p.op("vector", lambda v, jp=jp: v.tensor_add(rof[jp][:], rof[jp][:], xf[jp][:]),
                                 r=[("A4.ro", jp), ("A4.xf", jp)], w=[("A4.ro", jp)])
                            p.op("scalar", lambda e, m=m, tb=tb, jp=jp: e.dma_start(
                                out=RA[128 * m:128 * (m + 1), TB * tb:TB * (tb + 1)], in_=rof[jp][:]),
                                r=[("A4.ro", jp)], w=[(res_key["A"], m, tb)], dma=True)
                            tick_v(8)

        dc_iv = [sb(f"DCiv{i}", [128, INTER // 2], BF16) for i in range(2)]
        dc_pv = sb("DCpv", [128, INTER // 2], BF16)
        dc_av = [sb(f"DCav{i}", [128, INTER // 2], BF16) for i in range(2)]
        dc_ig = [sb(f"DCig{i}", [128, D], BF16) for i in range(2)]
        dc_pg = sb("DCpg", [128, D], BF16)
        dc_ag = [sb(f"DCag{i}", [128, D], BF16) for i in range(2)]

        def decode_gen(eng, ld_eng, st_eng, idxb, prodb, accb, blocks):
            def gen():
                hb = 0
                for (srcv, dstv, F, wkey) in blocks:
                    par = hb % 2
                    hb += 1
                    ib, ac = idxb[par], accb[par]
                    ibk = ("DC", eng, par, "ib")
                    ack = ("DC", eng, par, "ac")
                    prk = ("DC", eng, "pr")
                    p.op(ld_eng, lambda e, srcv=srcv, ib=ib, F=F: e.dma_start(
                        out=ib[:, 0:F], in_=srcv),
                        w=[ibk], dma=True)
                    yield
                    p.op(eng, lambda v, ib=ib, ac=ac, F=F: v.tensor_scalar(
                        ac[:, 0:F], ib[:, 0:F], 0.0, lutbc[:, 0:1],
                        ALU.is_equal, ALU.mult),
                        r=[ibk, LUTK], w=[ack])
                    yield
                    for j in range(1, 16):
                        p.op(eng, lambda v, ib=ib, F=F, j=j: v.tensor_scalar(
                            prodb[:, 0:F], ib[:, 0:F], float(j), lutbc[:, j:j + 1],
                            ALU.is_equal, ALU.mult),
                            r=[ibk, LUTK], w=[prk])
                        yield
                        p.op(eng, lambda v, ac=ac, F=F: v.tensor_add(
                            ac[:, 0:F], ac[:, 0:F], prodb[:, 0:F]),
                            r=[ack, prk], w=[ack])
                        yield
                    p.op(st_eng, lambda e, dstv=dstv, ac=ac, F=F: e.dma_start(
                        out=dstv, in_=ac[:, 0:F]),
                        r=[ack], w=[wkey], dma=True)
                    yield
            return gen()

        HB0 = INTER // 2
        blk_gu = []
        for h in range(2):
            lo, hi = HB0 * h, HB0 * (h + 1)
            for c in range(NC):
                blk_gu.append((gidxT[128 * c:128 * (c + 1), lo:hi],
                               WG[128 * c:128 * (c + 1), lo:hi], HB0, ("WG", c, h)))
                blk_gu.append((uidxT[128 * c:128 * (c + 1), lo:hi],
                               WU[128 * c:128 * (c + 1), lo:hi], HB0, ("WU", c, h)))
        blk_d = [(didxT[128 * o:128 * (o + 1), :], WD[128 * o:128 * (o + 1), :],
                  D, ("WD", o)) for o in range(NO)]
        gen_v = decode_gen("vector", "gpsimd", "scalar", dc_iv, dc_pv, dc_av, blk_gu)
        gen_g = decode_gen("vector", "gpsimd", "sync", dc_ig, dc_pg, dc_ag, blk_d)
        n_v_left = [len(blk_gu) * 33]

        def tick_v(n):
            for _ in range(n):
                if next(gen_v, "end") == "end":
                    if next(gen_g, "end") == "end":
                        return
                else:
                    n_v_left[0] -= 1
        def drain_v_h0(n=10 ** 9):
            while n_v_left[0] > len(blk_gu) // 2 * 33:
                tick_v(1)
        def tick_g(n):
            for _ in range(n):
                if next(gen_g, "end") == "end":
                    return

        def ffn_phase(tag, src_dram, src_keyf, dst_dram, dst_keyn):
            with contextlib.ExitStack() as ph:
                p.barrier()
                sbp = lambda name, shape, dt: ph.enter_context(nc.sbuf_tensor(name, list(shape), dt))
                psA = PS(ph, nc, tag + "p0"); psB = PS(ph, nc, tag + "p1")
                psC = PS(ph, nc, tag + "p2"); psD = PS(ph, nc, tag + "p3")
                xn = [sbp(f"{tag}xn{i}", [128, T], BF16) for i in range(NC)]
                xnk = ln_phase(tag + "0", src_dram, src_keyf, xn, ph, psA, psB, psC,
                               tick=lambda n: tick_v(2))
                wgb = [sbp(f"{tag}wg{i}", [128, TB], BF16) for i in range(2 * NC)]
                wub = [sbp(f"{tag}wu{i}", [128, TB], BF16) for i in range(2 * NC)]
                hg = [sbp(f"{tag}hg{i}", [128, TB], BF16) for i in range(2)]
                hb = [sbp(f"{tag}hb{i}", [128, TB], BF16) for i in range(2)]
                psrot = [psA, psB, psC, psD]
                for og in range(11):
                    if og == 5:
                        tick_v(10 ** 9)
                    par = og % 2
                    ow = 512 if og < 10 else INTER - 5120
                    nj = 4 if og < 10 else 3
                    hks = [0] if og < 5 else ([0, 1] if og == 5 else [1])
                    for c in range(NC):
                        p.op("sync", lambda e, c=c, og=og, par=par, ow=ow: e.dma_start(
                            out=wgb[NC * par + c][:, 0:ow], in_=WG[128 * c:128 * (c + 1), 512 * og:512 * og + ow]),
                            r=[("WG", c, h) for h in hks], w=[(tag, "wg", NC * par + c)], dma=True)
                        p.op("sync", lambda e, c=c, og=og, par=par, ow=ow: e.dma_start(
                            out=wub[NC * par + c][:, 0:ow], in_=WU[128 * c:128 * (c + 1), 512 * og:512 * og + ow]),
                            r=[("WU", c, h) for h in hks], w=[(tag, "wu", NC * par + c)], dma=True)
                    for j in range(nj):
                        ot = 4 * og + j
                        for tb in range(NTB):
                            pg = psrot[(2 * (ot * NTB + tb)) % 4]
                            pu = psrot[(2 * (ot * NTB + tb) + 1) % 4]
                            def fg(t, par=par, j=j, tb=tb, pg=pg):
                                for c in range(NC):
                                    ins = t.matmul(pg[:, 0:TB], wgb[NC * par + c][:, 128 * j:128 * (j + 1)],
                                                   xn[c][:, TB * tb:TB * (tb + 1)],
                                                   start=(c == 0), stop=(c == NC - 1))
                                return ins
                            p.op("tensor", fg,
                                 r=[(tag, "wg", NC * par + c) for c in range(NC)] + [xnk(c, tb) for c in range(NC)],
                                 w=[pg.k])
                            def fu(t, par=par, j=j, tb=tb, pu=pu):
                                for c in range(NC):
                                    ins = t.matmul(pu[:, 0:TB], wub[NC * par + c][:, 128 * j:128 * (j + 1)],
                                                   xn[c][:, TB * tb:TB * (tb + 1)],
                                                   start=(c == 0), stop=(c == NC - 1))
                                return ins
                            p.op("tensor", fu,
                                 r=[(tag, "wu", NC * par + c) for c in range(NC)] + [xnk(c, tb) for c in range(NC)],
                                 w=[pu.k])
                            hpar = (ot * NTB + tb) % 2
                            p.op("scalar", lambda s, hpar=hpar, pg=pg: s.activation(hg[hpar][:], pg[:, 0:TB], AF.Silu),
                                 r=[pg.k], w=[(tag, "hg", hpar)])
                            p.op("vector", lambda v, hpar=hpar, pu=pu: v.tensor_mul(hb[hpar][:], hg[hpar][:], pu[:, 0:TB]),
                                 r=[(tag, "hg", hpar), pu.k], w=[(tag, "hb", hpar)])
                            p.op("scalar", lambda e, ot=ot, tb=tb, hpar=hpar: e.dma_start(
                                out=HB[128 * ot:128 * (ot + 1), TB * tb:TB * (tb + 1)], in_=hb[hpar][:]),
                                r=[(tag, "hb", hpar)], w=[(tag, "HB", ot, tb)], dma=True)
                            tick_v(8)
            tick_g(10 ** 9)
            with contextlib.ExitStack() as ph:
                p.barrier()
                sbp = lambda name, shape, dt: ph.enter_context(nc.sbuf_tensor(name, list(shape), dt))
                pso = [PS(ph, nc, f"{tag}q{i}") for i in range(8)]
                hcol = [sbp(f"{tag}hc{i}", [128, 1024], BF16) for i in range(NO)]
                r1t = [sbp(f"{tag}r1{i}", [128, TB], F32) for i in range(4)]
                wdb = [sbp(f"{tag}wd{i}", [128, TB], BF16) for i in range(2)]
                ev = [sbp(f"{tag}ev{i}", [128, TB], F32) for i in range(2)]
                for th in range(2):
                    for o in range(NO):
                        p.op("sync", lambda e, o=o, th=th: e.dma_start(
                            out=hcol[o][:], in_=HB[128 * o:128 * (o + 1), 1024 * th:1024 * (th + 1)]),
                            r=[(tag, "HB", o, 2 * th), (tag, "HB", o, 2 * th + 1)],
                            w=[(tag, "hc", o)], dma=True)
                    for dmg in range(4):
                        for o in range(NO):
                            par = o % 2
                            p.op("gpsimd", lambda e, o=o, dmg=dmg, par=par: e.dma_start(
                                out=wdb[par][:], in_=WD[128 * o:128 * (o + 1), TB * dmg:TB * (dmg + 1)]),
                                r=[("WD", o)], w=[(tag, "wd", par)], dma=True)
                            def fd(t, o=o, par=par):
                                for dj in range(4):
                                    for t2 in range(2):
                                        ins = t.matmul(pso[2 * dj + t2][:, 0:TB],
                                                       wdb[par][:, 128 * dj:128 * (dj + 1)],
                                                       hcol[o][:, TB * t2:TB * (t2 + 1)],
                                                       start=(o == 0), stop=(o == NO - 1))
                                return ins
                            p.op("tensor", fd,
                                 r=[(tag, "wd", par), (tag, "hc", o)],
                                 w=[pso[i].k for i in range(8)])
                        for dj in range(4):
                            m = 4 * dmg + dj
                            for t2 in range(2):
                                epar = t2
                                tb = 2 * th + t2
                                rb = (2 * dj + t2) % 4
                                p.op("sync", lambda e, m=m, tb=tb, rb=rb: e.dma_start(
                                    out=r1t[rb][:], in_=src_dram[128 * m:128 * (m + 1), TB * tb:TB * (tb + 1)]),
                                    r=[src_keyf(m, tb)], w=[(tag, "r1", rb)], dma=True)
                                p.op("vector", lambda v, dj=dj, t2=t2, m=m, epar=epar, rb=rb: v.scalar_tensor_tensor(
                                    ev[epar][:], pso[2 * dj + t2][:, 0:TB], 1.0,
                                    r1t[rb][:], ALU.mult, ALU.add),
                                    r=[pso[2 * dj + t2].k, (tag, "r1", rb)], w=[(tag, "ev", epar)])
                                p.op("scalar", lambda e, m=m, tb=tb, epar=epar: e.dma_start(
                                    out=dst_dram[128 * m:128 * (m + 1), TB * tb:TB * (tb + 1)], in_=ev[epar][:]),
                                    r=[(tag, "ev", epar)], w=[(dst_keyn, m, tb)], dma=True)

        def phase_C(RC):
            with contextlib.ExitStack() as ph:
                p.barrier()
                sbp = lambda name, shape, dt: ph.enter_context(nc.sbuf_tensor(name, list(shape), dt))
                psA = PS(ph, nc, "pC0"); psB = PS(ph, nc, "pC1"); psC = PS(ph, nc, "pC2")
                xn = [sbp(f"Cxn{i}", [128, T], BF16) for i in range(NC)]
                xnk = ln_phase("C0", R2, lambda c, tb: ("RES_B", c, tb), xn, ph, psA, psB, psC)
                wpb = [sbp(f"Cwp{i}", [128, TB], BF16) for i in range(2 * NC)]
                evb = [sbp(f"Cev{i}", [128, TB], BF16) for i in range(2)]
                za = [sbp(f"Cza{i}", [128, TB], F32) for i in range(2)]
                zm = [sbp(f"Czm{i}", [128, TB], F32) for i in range(2)]
                bgt = [sbp(f"Cbg{i}", [128, NC], F32) for i in range(2)]
                p.op("sync", lambda e: e.dma_start(out=bgt[0][:], in_=bg1), w=[("C.bg", 0)], dma=True)
                p.op("sync", lambda e: e.dma_start(out=bgt[1][:], in_=bg2), w=[("C.bg", 1)], dma=True)
                projs = [(wqT, QF, "QF", None), (wkT, KF, "KF", None), (wvT, VF, "VF", None),
                         (wg1T, G1, "G1", 0), (wg2T, G2, "G2", 1)]
                if 1 not in c_sub:
                    projs = []
                elif 4 in c_sub:
                    projs = projs[:3]
                elif 5 in c_sub:
                    projs = projs[3:4]
                elif 6 in c_sub:
                    projs = projs[:4]
                elif 7 in c_sub:
                    projs = projs[3:5]
                for wsrc, dst, dkey, gate_i in projs:
                    for og in range(4):
                        par = og % 2
                        for c in range(NC):
                            p.op("gpsimd", lambda e, c=c, og=og, par=par, wsrc=wsrc: e.dma_start(
                                out=wpb[NC * par + c][:], in_=wsrc[128 * c:128 * (c + 1), TB * og:TB * (og + 1)]),
                                w=[("C.wp", NC * par + c)], dma=True)
                        for j in range(4):
                            it = 4 * og + j
                            for tb in range(NTB):
                                pp = (psA, psB)[(j * NTB + tb) % 2]
                                def fp_(t, par=par, j=j, tb=tb, pp=pp):
                                    for c in range(NC):
                                        ins = t.matmul(pp[:, 0:TB], wpb[NC * par + c][:, 128 * j:128 * (j + 1)],
                                                       xn[c][:, TB * tb:TB * (tb + 1)],
                                                       start=(c == 0), stop=(c == NC - 1))
                                    return ins
                                p.op("tensor", fp_,
                                     r=[("C.wp", NC * par + c) for c in range(NC)] + [xnk(c, tb) for c in range(NC)],
                                     w=[pp.k])
                                ep = (j * NTB + tb) % 2
                                if gate_i is None:
                                    p.op("scalar", lambda s, pp=pp, ep=ep: s.copy(evb[ep][:], pp[:, 0:TB]),
                                         r=[pp.k], w=[("C.ev", ep)])
                                elif 8 in c_sub:
                                    gi = gate_i
                                    p.op("vector", lambda v, pp=pp, ep=ep, it=it, gi=gi: v.tensor_scalar(
                                        evb[ep][:], pp[:, 0:TB], bgt[gi][:, it:it + 1], 0.0, ALU.add, ALU.min),
                                        r=[pp.k, ("C.bg", gi)], w=[("C.ev", ep)])
                                elif 9 in c_sub:
                                    gi = gate_i
                                    p.op("scalar", lambda s, pp=pp, ep=ep, it=it, gi=gi: s.activation(
                                        za[ep][:], pp[:, 0:TB], AF.Abs, bias=bgt[gi][:, it:it + 1]),
                                        r=[pp.k, ("C.bg", gi)], w=[("C.za", ep)])
                                    p.op("vector", lambda v, ep=ep: v.tensor_copy(evb[ep][:], za[ep][:]),
                                         r=[("C.za", ep)], w=[("C.ev", ep)])
                                else:
                                    gi = gate_i
                                    p.op("scalar", lambda s, pp=pp, ep=ep, it=it, gi=gi: s.activation(
                                        za[ep][:], pp[:, 0:TB], AF.Abs, bias=bgt[gi][:, it:it + 1]),
                                        r=[pp.k, ("C.bg", gi)], w=[("C.za", ep)])
                                    p.op("scalar", lambda s, ep=ep: s.activation(za[ep][:], za[ep][:], AF.Exp, scale=-1.0),
                                         r=[("C.za", ep)], w=[("C.za", ep)])
                                    p.op("scalar", lambda s, ep=ep: s.activation(za[ep][:], za[ep][:], AF.Ln, bias=onecol[:, 0:1]),
                                         r=[("C.za", ep), ("onec",)], w=[("C.za", ep)])
                                    p.op("vector", lambda v, pp=pp, ep=ep, it=it, gi=gi: v.tensor_scalar(
                                        zm[ep][:], pp[:, 0:TB], bgt[gi][:, it:it + 1], 0.0, ALU.add, ALU.min),
                                        r=[pp.k, ("C.bg", gi)], w=[("C.zm", ep)])
                                    p.op("vector", lambda v, ep=ep: v.tensor_sub(zm[ep][:], zm[ep][:], za[ep][:]),
                                         r=[("C.zm", ep), ("C.za", ep)], w=[("C.zm", ep)])
                                    p.op("vector", lambda v, ep=ep: v.tensor_scalar(
                                        evb[ep][:], zm[ep][:], 1.0 / 16.0, -50.0, ALU.mult, ALU.max),
                                        r=[("C.zm", ep)], w=[("C.ev", ep)])
                                p.op("scalar", lambda e, dst=dst, it=it, tb=tb, ep=ep: e.dma_start(
                                    out=dst[128 * it:128 * (it + 1), TB * tb:TB * (tb + 1)], in_=evb[ep][:]),
                                    r=[("C.ev", ep)], w=[(dkey, it, tb)], dma=True)

            with contextlib.ExitStack() as ph:
                p.barrier()
                sbp = lambda name, shape, dt: ph.enter_context(nc.sbuf_tensor(name, list(shape), dt))
                psS_ = [PS(ph, nc, f"pS{i}") for i in range(2)]
                psO_ = [PS(ph, nc, f"pO{i}") for i in range(2)]
                psSD_ = [PS(ph, nc, f"pSD{i}") for i in range(2)]
                psT1 = PS(ph, nc, "pT1", BF16); psT2 = PS(ph, nc, "pT2", BF16)
                psAl = psS_[0]
                qc = [sbp(f"Cq{i}", [128, T], BF16) for i in range(2)]
                kc = [sbp(f"Ck{i}", [128, T], BF16) for i in range(2)]
                vc = [sbp(f"Cv{i}", [128, T], BF16) for i in range(2)]
                g1c = [sbp(f"Cg1{i}", [128, T], BF16) for i in range(2)]
                g2c = [sbp(f"Cg2{i}", [128, T], BF16) for i in range(2)]
                vtm = sbp("Cvtm", [128, T], BF16)
                ocomb = sbp("Coc", [128, T], BF16)
                st_ = [sbp(f"Cst{i}", [128, T], F32) for i in range(2)]
                NRT = 8
                Gt_ = [sbp(f"CG{i}", [128, 128], F32) for i in range(NRT)]
                eq_ = [sbp(f"Ceq{i}", [128, 128], BF16) for i in range(NRT)]
                ek_ = [sbp(f"Cek{i}", [128, 128], BF16) for i in range(NRT)]
                qls_ = [sbp(f"Cqls{i}", [128, 128], BF16) for i in range(NRT)]
                kx_ = [sbp(f"Ckx{i}", [128, 128], BF16) for i in range(NRT)]
                kd_ = [sbp(f"Ckd{i}", [128, 128], BF16) for i in range(NRT)]
                ek2_ = [sbp(f"Cek2{i}", [128, 128], BF16) for i in range(NRT)]
                kdtm_ = [sbp(f"Ckdt{i}", [128, 128], BF16) for i in range(NRT)]
                Ssb_ = [sbp(f"CSsb{i}", [128, 128], BF16) for i in range(NRT)]
                ccol_ = [sbp(f"Ccc{i}", [128, 4], F32) for i in range(NRT)]
                stbf_ = [sbp(f"Cstb{i}", [128, 128], BF16) for i in range(NRT)]
                ob_ = [sbp(f"Cob{i}", [128, 128], BF16) for i in range(2)]
                eqp = sbp("Ceqp", [128, 1024], F32)
                ekp = sbp("Cekp", [128, 1024], F32)
                eqb = sbp("Ceqb", [128, 1024], BF16)
                ekb = sbp("Cekb", [128, 1024], BF16)
                ek2b = sbp("Cek2b", [128, 1024], BF16)
                qlsb = sbp("Cqlsb", [128, 1024], BF16)
                kxb = sbp("Ckxb", [128, 1024], BF16)
                kdb = sbp("Ckdb", [128, 1024], BF16)
                al = sbp("Cal", [1, 2], F32)
                al2 = sbp("Cal2", [1, 2], F32)
                mx1 = sbp("Cmx", [1, 1], F32)
                a01 = sbp("Ca01", [128, 2], F32)
                p.op("sync", lambda e: e.dma_start(out=al[:], in_=alpha), w=[("C.al",)], dma=True)
                p.op("vector", lambda v: v.reduce_max(mx1[:], al[:], axis=mybir.AxisListType.X),
                     r=[("C.al",)], w=[("C.mx",)])
                p.op("vector", lambda v: v.tensor_scalar(al2[:], al[:], mx1[0:1, 0:1], None, ALU.subtract),
                     r=[("C.al",), ("C.mx",)], w=[("C.al2",)])
                p.op("scalar", lambda s: s.activation(al2[:], al2[:], AF.Exp),
                     r=[("C.al2",)], w=[("C.al2",)])
                p.op("vector", lambda v: v.reduce_sum(mx1[:], al2[:], axis=mybir.AxisListType.X),
                     r=[("C.al2",)], w=[("C.mx",)])
                p.op("vector", lambda v: v.reciprocal(mx1[:], mx1[:]), r=[("C.mx",)], w=[("C.mx",)])
                p.op("vector", lambda v: v.tensor_scalar(al2[:], al2[:], mx1[0:1, 0:1], None, ALU.mult),
                     r=[("C.al2",), ("C.mx",)], w=[("C.al2",)])
                p.op("tensor", lambda t: t.matmul(psAl[:, 0:2], ones_k1_f[:], al2[:], start=True, stop=True),
                     r=[("C.al2",), ("o2",)], w=[psAl.k])
                p.op("scalar", lambda s: s.copy(a01[:], psAl[:, 0:2]), r=[psAl.k], w=[("C.a01",)])

                LNS = float(-0.5 * np.log(128.0))
                p.op("gpsimd", lambda g: g.memset(st_[0][:], 0.0), w=[("C.st", 0, h2) for h2 in range(NC)])
                p.op("gpsimd", lambda g: g.memset(st_[1][:], 0.0), w=[("C.st", 1, h2) for h2 in range(NC)])
                NCH = T // 128
                ld_engs = ("sync", "scalar", "gpsimd")
                for ch in (range(NCH) if 2 in c_sub else []):
                    cb = ch % 2
                    sl = slice(128 * ch, 128 * (ch + 1))
                    for ti, (buf, dsrc, dkey) in enumerate(((qc, QF, "QF"), (kc, KF, "KF"), (vc, VF, "VF"),
                                                            (g1c, G1, "G1"), (g2c, G2, "G2"))):
                        for h in range(NC):
                            eng = ld_engs[(ti * NC + h) % 3]
                            p.op(eng, lambda e, buf=buf, dsrc=dsrc, h=h, cb=cb, sl=sl: e.dma_start(
                                out=buf[cb][:, 128 * h:128 * (h + 1)], in_=dsrc[128 * h:128 * (h + 1), sl]),
                                r=[(dkey, h, sl.start // TB)], w=[("C." + dkey, cb, h)], dma=True)
                    chains = []
                    for h in range(NC):
                        for inst in range(2):
                            cp = (2 * h + inst) % NRT
                            chains.append(dict(
                                h=h, inst=inst, cp=cp, hs=slice(128 * h, 128 * (h + 1)),
                                gh=(g1c, g2c)[inst],
                                gk=("C." + ("G1", "G2")[inst], cb, h),
                                stk=("C.st", inst, h),
                                stv=st_[inst][:, 128 * h:128 * (h + 1)],
                                Gt=Gt_[cp], eq=eq_[cp], ek=ek_[cp], qls=qls_[cp], kx=kx_[cp],
                                kd=kd_[cp], kdtm=kdtm_[cp], Ssb=Ssb_[cp], ccol=ccol_[cp],
                                stbf=stbf_[cp],
                                psS=psS_[cp % 2], psO=psO_[cp % 2], psSD=psSD_[cp % 2],
                                KG=("C.Gt", cp), KEQ=("C.eq", cp), KEK=("C.ek", cp),
                                KQL=("C.qls", cp), KKX=("C.kx", cp), KKD=("C.kd", cp),
                                KKDT=("C.kdtm", cp), KSS=("C.Ssb", cp), KSB=("C.stbf", cp),
                                KEK2=("C.ek2", cp), ek2=ek2_[cp]))
                    for wv in range(4):
                      chains_w = chains[8 * wv:8 * wv + 8]
                      for ii, x in enumerate(chains_w):
                          p.op("vector", lambda v, x=x, cb=cb: v.tensor_tensor_scan(
                              x["Gt"][:], x["gh"][cb][:, x["hs"]], zeros512[:, 0:128], 0.0,
                              ALU.add, ALU.add),
                              r=[x["gk"], ("z5",)], w=[x["KG"]])
                          if x["inst"] == 0:
                              pst = (psT1, psT2)[x["h"] % 2]
                              p.op("tensor", lambda t, x=x, cb=cb, pst=pst: t.transpose(
                                  pst[:, 0:128], vc[cb][:, x["hs"]], ident[:]),
                                  r=[("C.VF", cb, x["h"]), ("ident",)], w=[pst.k])
                              p.op("vector", lambda v, x=x, pst=pst: v.tensor_copy(vtm[:, x["hs"]], pst[:, 0:128]),
                                   r=[pst.k], w=[("C.vtm", x["hs"].start)])
                      for ii, x in enumerate(chains_w):
                          ss = slice(512 * x["inst"] + 128 * (x["h"] - 4 * wv),
                                     512 * x["inst"] + 128 * (x["h"] - 4 * wv) + 128)
                          x["ss"] = ss
                          p.op("vector", lambda v, x=x, ss=ss: v.tensor_scalar(
                              eqp[:, ss], x["Gt"][:], x["Gt"][:, 63:64], LNS,
                              ALU.subtract, ALU.add),
                              r=[x["KG"]], w=[("C.eqp", ii)])
                          p.op("vector", lambda v, x=x, ss=ss: v.tensor_scalar(
                              ekp[:, ss], x["Gt"][:], x["Gt"][:, 63:64], -1.0,
                              ALU.subtract, ALU.mult),
                              r=[x["KG"]], w=[("C.ekp", ii)])
                          p.op("vector", lambda v, x=x: v.tensor_sub(
                              x["ccol"][:, 3:4], x["Gt"][:, 127:128], x["Gt"][:, 63:64]),
                              r=[x["KG"]], w=[("C.cc", x["cp"], 3)])
                          p.op("scalar", lambda s, x=x: s.activation(
                              x["ccol"][:, 1:3], x["Gt"][:, 63:128:64], AF.Exp),
                              r=[x["KG"]], w=[("C.cc", x["cp"], 1), ("C.cc", x["cp"], 2)])
                          p.op("scalar", lambda s, x=x: s.activation(
                              x["ccol"][:, 3:4], x["ccol"][:, 3:4], AF.Exp),
                              r=[("C.cc", x["cp"], 3)], w=[("C.cc", x["cp"], 3)])
                      p.op("scalar", lambda s: s.activation(eqb[:], eqp[:], AF.Exp),
                           r=[("C.eqp", ii2) for ii2 in range(8)], w=[("C.eqb",)])
                      p.op("scalar", lambda s: s.activation(ekb[:], ekp[:], AF.Exp),
                           r=[("C.ekp", ii2) for ii2 in range(8)], w=[("C.ekb",)])
                      for ii, x in enumerate(chains_w):
                          p.op("vector", lambda v, x=x: v.tensor_scalar(
                              ek2b[:, x["ss"]], ekb[:, x["ss"]], x["ccol"][:, 3:4], None, ALU.mult),
                              r=[("C.ekb",), ("C.cc", x["cp"], 3)], w=[("C.e2", ii)])
                      wsl = slice(512 * wv, 512 * (wv + 1))
                      for inst in range(2):
                          isl = slice(512 * inst, 512 * (inst + 1))
                          p.op("vector", lambda v, cb=cb, isl=isl, wsl=wsl: v.tensor_mul(
                              qlsb[:, isl], qc[cb][:, wsl], eqb[:, isl]),
                              r=[("C.eqb",)] + [("C.QF", cb, 4 * wv + hh) for hh in range(4)],
                              w=[("C.qlsb", inst)])
                          p.op("vector", lambda v, cb=cb, isl=isl, wsl=wsl: v.tensor_mul(
                              kxb[:, isl], kc[cb][:, wsl], ekb[:, isl]),
                              r=[("C.ekb",)] + [("C.KF", cb, 4 * wv + hh) for hh in range(4)],
                              w=[("C.kxb", inst)])
                          p.op("vector", lambda v, cb=cb, isl=isl, wsl=wsl: v.tensor_mul(
                              kdb[:, isl], kc[cb][:, wsl], ek2b[:, isl]),
                              r=[("C.e2", 2 * hh + inst2) for hh in range(4) for inst2 in ((inst,) if True else ())]
                                + [("C.KF", cb, 4 * wv + hh) for hh in range(4)],
                              w=[("C.kdb", inst)])
                      for ii, x in enumerate(chains_w):
                          p.op("tensor", lambda t, x=x: t.matmul(
                              x["psS"][:, 0:128], kxb[:, x["ss"]], qlsb[:, x["ss"]],
                              start=True, stop=True),
                              r=[("C.kxb", x["inst"]), ("C.qlsb", x["inst"])], w=[x["psS"].k])
                          p.op("vector", lambda v, x=x: v.scalar_tensor_tensor(
                              x["Ssb"][:], x["psS"][:, 0:128], 1.0, mask_ut[:], ALU.mult, ALU.mult),
                              r=[x["psS"].k, ("mut",)], w=[x["KSS"]])
                          p.op("gpsimd", lambda g, x=x: g.tensor_scalar(
                              x["stbf"][:], x["stv"], x["ccol"][:, 1:2], None, ALU.mult),
                              r=[x["stk"], ("C.cc", x["cp"], 1)], w=[x["KSB"]])
                      for ii, x in enumerate(chains_w):
                          def fo(t, x=x):
                              t.matmul(x["psO"][:, 0:128], vtm[:, x["hs"]], x["Ssb"][:],
                                       start=True, stop=False)
                              return t.matmul(x["psO"][:, 0:128], x["stbf"][:], qlsb[:, x["ss"]],
                                              start=False, stop=True)
                          p.op("tensor", fo, r=[("C.vtm", x["hs"].start), x["KSS"], x["KSB"],
                                                ("C.qlsb", x["inst"])],
                               w=[x["psO"].k])
                          if x["inst"] == 0:
                              p.op("vector", lambda v, x=x: v.tensor_scalar(
                                  ocomb[:, x["hs"]], x["psO"][:, 0:128], a01[:, 0:1], None, ALU.mult),
                                  r=[x["psO"].k, ("C.a01",)], w=[("C.oc", x["hs"].start)])
                          else:
                              obp = x["h"] % 2
                              p.op("vector", lambda v, x=x, obp=obp: v.scalar_tensor_tensor(
                                  ob_[obp][:], x["psO"][:, 0:128], a01[:, 1:2], ocomb[:, x["hs"]],
                                  ALU.mult, ALU.add),
                                  r=[x["psO"].k, ("C.a01",), ("C.oc", x["hs"].start)], w=[("C.ob", obp)])
                              p.op("sync", lambda e, x=x, sl=sl, obp=obp: e.dma_start(
                                  out=OIN[128 * x["h"]:128 * (x["h"] + 1), sl], in_=ob_[obp][:]),
                                  r=[("C.ob", obp)], w=[("C.attn", x["h"], sl.start)], dma=True)
                      for ii, x in enumerate(chains_w):
                          pst = (psT1, psT2)[ii % 2]
                          p.op("tensor", lambda t, x=x, pst=pst: t.transpose(
                              pst[:, 0:128], kdb[:, x["ss"]], ident[:]),
                              r=[("C.kdb", x["inst"]), ("ident",)], w=[pst.k])
                          p.op("scalar", lambda s, x=x, pst=pst: s.copy(x["kdtm"][:], pst[:, 0:128]),
                               r=[pst.k], w=[x["KKDT"]])
                          p.op("tensor", lambda t, x=x: t.matmul(
                              x["psSD"][:, 0:128], x["kdtm"][:], vtm[:, x["hs"]], start=True, stop=True),
                              r=[x["KKDT"], ("C.vtm", x["hs"].start)], w=[x["psSD"].k])
                          p.op("vector", lambda v, x=x: v.scalar_tensor_tensor(
                              x["stv"], x["stv"], x["ccol"][:, 2:3], x["psSD"][:, 0:128],
                              ALU.mult, ALU.add),
                              r=[x["stk"], ("C.cc", x["cp"], 2), x["psSD"].k], w=[x["stk"]])

                wob = [sbp(f"C3w{i}", [128, TB], BF16) for i in range(NC)]
                atb = [sbp(f"C3a{i}", [128, TB], BF16) for i in range(NC)]
                rf = [sbp(f"C3r{i}", [128, TB], F32) for i in range(2)]
                ev3 = [sbp(f"C3e{i}", [128, TB], F32) for i in range(2)]
                for tb in (range(NTB) if 3 in c_sub else []):
                    for c in range(NC):
                        p.op("sync", lambda e, c=c, tb=tb: e.dma_start(
                            out=atb[c][:], in_=OIN[128 * c:128 * (c + 1), TB * tb:TB * (tb + 1)]),
                            r=[("C.attn", c, 128 * q) for q in range(4 * tb, 4 * tb + 4)],
                            w=[("C3.at", c)], dma=True)
                    for og in range(4):
                        par = og % 2
                        for c in range(NC):
                            p.op("gpsimd", lambda e, c=c, og=og: e.dma_start(
                                out=wob[c][:], in_=woT[128 * c:128 * (c + 1), TB * og:TB * (og + 1)]),
                                w=[("C3.wo", c)], dma=True)
                        for j in range(4):
                            m = 4 * og + j
                            pp = (psS_[0], psO_[0])[j % 2]
                            def fm3(t, pp=pp, par=par, j=j, tb=tb):
                                for c in range(NC):
                                    ins = t.matmul(pp[:, 0:TB], wob[c][:, 128 * j:128 * (j + 1)],
                                                   atb[c][:],
                                                   start=(c == 0), stop=(c == NC - 1))
                                return ins
                            p.op("tensor", fm3,
                                 r=[("C3.wo", c) for c in range(NC)]
                                   + [("C3.at", c) for c in range(NC)],
                                 w=[pp.k])
                            jp = j % 2
                            p.op("sync", lambda e, m=m, tb=tb, jp=jp: e.dma_start(
                                out=rf[jp][:], in_=R2[128 * m:128 * (m + 1), TB * tb:TB * (tb + 1)]),
                                r=[("RES_B", m, tb)], w=[("C3.rf", jp)], dma=True)
                            p.op("vector", lambda v, pp=pp, jp=jp: v.scalar_tensor_tensor(
                                ev3[jp][:], pp[:, 0:TB], 1.0, rf[jp][:], ALU.mult, ALU.add),
                                r=[pp.k, ("C3.rf", jp)], w=[("C3.ev", jp)])
                            p.op("scalar", lambda e, m=m, tb=tb, jp=jp: e.dma_start(
                                out=RC[128 * m:128 * (m + 1), TB * tb:TB * (tb + 1)], in_=ev3[jp][:]),
                                r=[("C3.ev", jp)], w=[("RES_C", m, tb)], dma=True)

        if "A" in phases:
            phase_A01(res_out["A"])
            phase_A2()
            phase_A3()
            phase_A4(res_out["A"])
        if "B" in phases or "D" in phases:
            drain_v_h0()
        if "B" in phases:
            ffn_phase("B", R1, lambda c, tb: ("RES_A", c, tb), res_out["B"], "RES_B")
        if "C" in phases:
            phase_C(res_out["C"])
        if "D" in phases:
            ffn_phase("Dp", R3, lambda c, tb: ("RES_C", c, tb), res_out["D"], "RES_D")

        lk = res_key[last_phase]
        final_keys = [(lk, m, tb) for m in range(NC) for tb in range(NTB)]
        final_keys = [k for k in final_keys if k in p.lastw]
        p.barrier()
        p.op("vector", lambda v: v.nop(), r=final_keys, w=[])

        mk, counts = p.finalize(sems)
        block = st.enter_context(nc.Block())
        block.sync(mk("sync"))
        block.tensor(mk("tensor"))
        block.vector(mk("vector"))
        block.scalar(mk("scalar"))
        block.gpsimd(mk("gpsimd"))
    return nc, counts


def host_inputs(inputs, core):
    d = {}
    d["xT"] = np.ascontiguousarray(inputs["L0_attn_a_x"][core].T)
    d["w_in"] = inputs["L0_attn_a_W_in"]
    cw = inputs["L0_attn_a_conv_w"]
    d["cw_h"] = np.ascontiguousarray(cw.transpose(2, 1, 0).reshape(4, D, NC, 128).transpose(2, 0, 1, 3))
    d["conv_b"] = np.ascontiguousarray(inputs["L0_attn_a_conv_b"].reshape(NC, 128).T)
    d["w_x"] = inputs["L0_attn_a_W_x"]
    d["w_dt"] = inputs["L0_attn_a_W_dt"]
    d["b_dt"] = np.ascontiguousarray(inputs["L0_attn_a_b_dt"].reshape(NC, 128).T)
    d["a_log"] = np.ascontiguousarray(inputs["L0_attn_a_A_log"].reshape(NC, 128, SSN).transpose(1, 0, 2).reshape(128, NC * SSN))
    d["d_vec"] = np.ascontiguousarray(inputs["L0_attn_a_D"].reshape(NC, 128).T)
    d["w_out"] = inputs["L0_attn_a_W_out"]
    d["wqT"] = np.ascontiguousarray(inputs["L0_attn_b_Wq"].T)
    d["wkT"] = np.ascontiguousarray(inputs["L0_attn_b_Wk"].T)
    d["wvT"] = np.ascontiguousarray(inputs["L0_attn_b_Wv"].T)
    d["woT"] = np.ascontiguousarray(inputs["L0_attn_b_Wo"].T)
    d["wg1T"] = np.ascontiguousarray(inputs["L0_attn_b_Wg1"].T)
    d["wg2T"] = np.ascontiguousarray(inputs["L0_attn_b_Wg2"].T)
    d["bg1"] = np.ascontiguousarray(inputs["L0_attn_b_bg1"].reshape(NC, 128).T)
    d["bg2"] = np.ascontiguousarray(inputs["L0_attn_b_bg2"].reshape(NC, 128).T)
    d["alpha"] = np.ascontiguousarray(inputs["L0_attn_b_alpha_list"].reshape(1, 2))
    d["lut"] = np.ascontiguousarray(inputs["L0_ffn_lut"].reshape(1, 16))
    d["gidxT"] = np.ascontiguousarray(inputs["L0_ffn_gate_idx"].T)
    d["uidxT"] = np.ascontiguousarray(inputs["L0_ffn_up_idx"].T)
    d["didxT"] = np.ascontiguousarray(inputs["L0_ffn_down_idx"].T)
    return d


_PHASES = ("A", "B", "C", "D")


def kernel(**inputs):
    nc, _ = build_nc(_PHASES)
    in_maps = [host_inputs(inputs, i) for i in range(8)]
    res = run_bass_kernel_spmd(nc, in_maps, core_ids=list(range(8)))
    out = np.stack([np.ascontiguousarray(r["outT"].T) for r in res.results])
    return out.astype(np.float32)


# version 53
# speedup vs baseline: 1.2505x; 1.0170x over previous
import contextlib
import numpy as np
import concourse.bass as bass
import concourse.mybir as mybir
from concourse.bass_utils import run_bass_kernel_spmd

F32 = mybir.dt.float32
BF16 = mybir.dt.bfloat16
I32 = mybir.dt.int32
AF = mybir.ActivationFunctionType
ALU = mybir.AluOpType

D = 2048
T = 2048
NC = D // 128
TB = 512
NTB = T // TB
SSN = 16
INTER = 5504
NO = 43

ENGS = ("sync", "tensor", "vector", "scalar", "gpsimd")
DMA_ENGS = ("sync", "scalar", "gpsimd", "vector")


class Prog:

    NSLOT = 8

    def __init__(self):
        self.ops = []
        self.lastw = {}
        self.readers = {}
        self.last_per_eng = {}
        self.last_dma_slot = {}
        self.dma_ord = {}
        self.bar = None

    def op(self, eng, fn, r=(), w=(), dma=False, qchain=False):
        ps_r = [k for k in r if isinstance(k, tuple) and len(k) > 0 and k[0] == "PS"]
        if ps_r:
            r = [k for k in r if k not in ps_r]
            w = list(w) + ps_r
        deps = set()
        for k in r:
            pr = self.lastw.get(k)
            if pr is not None:
                deps.add(pr)
        for k in w:
            pr = self.lastw.get(k)
            if pr is not None:
                deps.add(pr)
            deps.update(self.readers.get(k, ()))
        if self.bar is not None:
            deps.add(self.bar)
        i = len(self.ops)
        deps.discard(i)
        slot = None
        if dma:
            o = self.dma_ord.get(eng, 0)
            self.dma_ord[eng] = o + 1
            slot = o % self.NSLOT
            self.last_dma_slot[(eng, slot)] = i
        else:
            self.last_per_eng[eng] = i
        self.ops.append([eng, fn, dma, deps, False, slot, qchain])
        for k in r:
            self.readers.setdefault(k, set()).add(i)
        for k in w:
            self.lastw[k] = i
            self.readers[k] = set()
        return i

    def barrier(self):
        deps = set(self.last_per_eng.values()) | set(self.last_dma_slot.values())
        i = len(self.ops)
        self.ops.append(["vector", lambda v: v.nop(), False, deps, False, None, False])
        self.last_per_eng["vector"] = i
        self.bar = i

    def finalize(self, sems):
        ops = self.ops
        for o in ops:
            for d in o[3]:
                ops[d][4] = True
        for o in ops:
            if o[2]:
                o[4] = True
        cnt = {}
        val = [None] * len(ops)
        for i, (eng, fn, dma, deps, need, slot, qc) in enumerate(ops):
            if need:
                key = ("d", eng, slot) if dma else ("c", eng)
                cnt[key] = cnt.get(key, 0) + (16 if dma else 1)
                val[i] = (key, cnt[key])
        streams = {e: [] for e in ENGS}
        for i, (eng, fn, dma, deps, need, slot, qc) in enumerate(ops):
            waits = {}
            for d in deps:
                if eng == "tensor" and not dma and ops[d][0] == "tensor" and not ops[d][2]:
                    continue
                assert val[d] is not None
                k, v = val[d]
                if waits.get(k, -1) < v:
                    waits[k] = v
            streams[eng].append((i, fn, sorted(waits.items()), val[i]))
        counts = {e: (len(streams[e]), sum(len(w) for _, _, w, _ in streams[e]))
                  for e in ENGS}

        def mk(eng_name):
            def emit(eng):
                known = {}
                for i, fn, waits, v in streams[eng_name]:
                    for k, req in waits:
                        if known.get(k, -1) < req:
                            eng.wait_ge(sems[k], req)
                            known[k] = req
                    inst = fn(eng)
                    if v is not None:
                        assert inst is not None, f"op {i} on {eng_name} needs inc but emitted nothing"
                        inst.then_inc(sems[v[0]], 16 if v[0][0] == "d" else 1)
            return emit
        return mk, counts


class PS:
    def __init__(self, stack, nc, name, dt=F32):
        n = 512 if dt == F32 else 1024
        self.t = stack.enter_context(nc.psum_tensor(name, [128, n], dt))
        self.k = ("PS", name)

    def __getitem__(self, sl):
        return self.t[sl]


def build_nc(phases=("A", "B", "C", "D"), dbg=False, c_sub=(1, 2, 3)):
    nc = bass.Bass("TRN2", target_bir_lowering=False)
    p = Prog()

    def din(name, shape, dt=F32):
        return nc.dram_tensor(name, list(shape), dt, kind="ExternalInput").ap()

    xT = din("xT", [D, T])
    w_in = din("w_in", [D, 2 * D])
    cw_h = din("cw_h", [NC, 4, D, 128])
    conv_b = din("conv_b", [128, NC])
    w_x = din("w_x", [D, 160])
    w_dt = din("w_dt", [128, D])
    b_dt = din("b_dt", [128, NC])
    a_log = din("a_log", [128, NC * SSN])
    d_vec = din("d_vec", [128, NC])
    w_out = din("w_out", [D, D])
    wqT = din("wqT", [D, D])
    wkT = din("wkT", [D, D])
    wvT = din("wvT", [D, D])
    woT = din("woT", [D, D])
    wg1T = din("wg1T", [D, D])
    wg2T = din("wg2T", [D, D])
    bg1 = din("bg1", [128, NC])
    bg2 = din("bg2", [128, NC])
    alpha = din("alpha", [1, 2])
    lut = din("lut", [1, 16])
    gidxT = din("gidxT", [D, INTER], I32)
    uidxT = din("uidxT", [D, INTER], I32)
    didxT = din("didxT", [INTER, D], I32)

    outT = nc.dram_tensor("outT", [D, T], F32, kind="ExternalOutput").ap()

    def scr(name, shape, dt, addr_space="Local"):
        kind = "ExternalOutput" if dbg else "Internal"
        return nc.dram_tensor(name, list(shape), dt, kind=kind, addr_space=addr_space).ap()

    XC = scr("XC", [D, 3 + T], BF16)
    if dbg:
        DBG_DL = scr("DBG_DL", [D, TB], BF16)
        DBG_DU = scr("DBG_DU", [D, TB], BF16)
        DBG_B = scr("DBG_B", [128, TB], BF16)
        DBG_C = scr("DBG_C", [128, TB], F32)
        DBG_H = scr("DBG_H", [128, NC * TB], F32)
        DBG_Y = scr("DBG_Y", [D, TB], F32)
        DBG_U = scr("DBG_U", [D, TB], BF16)
        DBG_DP = scr("DBG_DP", [128, TB], BF16)
        DBG_BC = scr("DBG_BC", [16, TB], BF16)
        DBG_CC = scr("DBG_CC", [16, TB], BF16)
        DBG_X2 = scr("DBG_X2", [D, TB], BF16)
    RR = scr("RR", [D, T], BF16)
    XS = scr("XS", [D, T], BF16)
    OIN = scr("OIN", [D, T], BF16)
    R1 = scr("R1", [D, T], F32)
    R2 = scr("R2", [D, T], F32)
    R3 = scr("R3", [D, T], F32)
    WG = scr("WGd", [D, INTER], BF16)
    WU = scr("WUd", [D, INTER], BF16)
    WD = scr("WDd", [INTER, D], BF16)
    HB = scr("HBd", [INTER, T], BF16)
    QF = scr("QFd", [D, T], BF16)
    KF = scr("KFd", [D, T], BF16)
    VF = scr("VFd", [D, T], BF16)
    G1 = scr("G1d", [D, T], BF16)
    G2 = scr("G2d", [D, T], BF16)

    last_phase = phases[-1]
    res_out = {"A": R1, "B": R2, "C": R3, "D": outT}
    res_out[last_phase] = outT
    res_key = {"A": "RES_A", "B": "RES_B", "C": "RES_C", "D": "RES_D"}

    st = contextlib.ExitStack()
    with st:
        sems = {}
        for e in ENGS:
            sems[("c", e)] = st.enter_context(nc.semaphore(f"c_{e}"))
        for e in DMA_ENGS:
            for s_ in range(Prog.NSLOT):
                sems[("d", e, s_)] = st.enter_context(nc.semaphore(f"d_{e}{s_}"))

        sb = lambda name, shape, dt: st.enter_context(nc.sbuf_tensor(name, list(shape), dt))
        ident = sb("ident", [128, 128], BF16)
        ones_k1_f = sb("ones_k1_f", [1, 128], F32)
        ones_m1_bf = sb("ones_m1_bf", [128, 1], BF16)
        zeros512 = sb("zeros512", [128, 512], BF16)
        sel = sb("sel", [16, 16 * 128], BF16)
        onesrow = sb("onesrow", [1, 128], BF16)
        epscol = sb("epscol", [128, 1], F32)
        onecol = sb("onecol", [128, 1], F32)

        def g_init(g):
            g.memset(ident[:], 1.0)
            g.affine_select(out=ident[:], in_=ident[:], pattern=[[1, 128]],
                            compare_op=ALU.is_ge, fill=0.0, channel_multiplier=-1)
            return g.affine_select(out=ident[:], in_=ident[:], pattern=[[-1, 128]],
                                   compare_op=ALU.is_ge, fill=0.0, channel_multiplier=1)
        p.op("gpsimd", g_init, w=[("ident",)])
        p.op("gpsimd", lambda g: g.memset(ones_k1_f[:], 1.0), w=[("o2",)])
        p.op("gpsimd", lambda g: g.memset(ones_m1_bf[:], 1.0), w=[("o3",)])
        p.op("gpsimd", lambda g: g.memset(zeros512[:], 0.0), w=[("z5",)])
        p.op("gpsimd", lambda g: g.memset(sel[:], 0.0), w=[("sel",)])
        p.op("gpsimd", lambda g: g.memset(onesrow[:], 1.0), w=[("orow",)])
        p.op("gpsimd", lambda g: g.memset(epscol[:], 1e-5), w=[("eps",)])
        p.op("gpsimd", lambda g: g.memset(onecol[:], 1.0), w=[("onec",)])
        for n in range(16):
            p.op("gpsimd", lambda g, n=n: g.dma_start(out=sel[n:n + 1, 128 * n:128 * (n + 1)], in_=onesrow[:]),
                 r=[("orow",), ("sel",)], w=[("sel", n)], dma=True)

        def sel_n(n):
            return sel[0:16, 128 * n:128 * (n + 1)]

        mask_ut = sb("mask_ut", [128, 128], BF16)
        p.op("gpsimd", lambda g: g.memset(mask_ut[:], 1.0), w=[("mut",)])
        p.op("gpsimd", lambda g: g.affine_select(out=mask_ut[:], in_=mask_ut[:], pattern=[[1, 128]],
                                                 compare_op=ALU.is_ge, fill=0.0, channel_multiplier=-1),
             r=[("mut",)], w=[("mut",)])
        lutbc = sb("lutbc", [128, 16], F32)
        p.op("sync", lambda e: e.dma_start(out=lutbc[0:1, :], in_=lut), w=[("lut", 0)], dma=True)
        for dbl in range(7):
            lo, hi = 1 << dbl, 1 << (dbl + 1)
            p.op("sync", lambda e, lo=lo, hi=hi: e.dma_start(out=lutbc[lo:hi, :], in_=lutbc[0:lo, :]),
                 r=[("lut", dbl)], w=[("lut", dbl + 1)], dma=True)
        LUTK = ("lut", 7)

        def ln_phase(tag, src_dram, src_key, xn_pool, scope, psA, psB, psC, tick=None):
            sbp = lambda name, shape, dt: scope.enter_context(nc.sbuf_tensor(name, list(shape), dt))
            xf = [sbp(f"{tag}xf{i}", [128, TB], BF16) for i in range(4)]
            sq = [sbp(f"{tag}sq{i}", [128, TB], BF16) for i in range(2)]
            rows = [sbp(f"{tag}rw{i}", [1, TB], F32) for i in range(6)]
            bca = [sbp(f"{tag}bc{i}", [128, TB], F32) for i in range(2)]
            K = lambda *a: (tag,) + a
            for tb in range(NTB):
                for c in range(NC):
                    p.op("gpsimd", lambda e, c=c, tb=tb: e.dma_start(
                        out=xf[c % 4][:], in_=src_dram[128 * c:128 * (c + 1), TB * tb:TB * (tb + 1)]),
                        r=[src_key(c, tb)], w=[K("xf", c % 4)], dma=True)
                    if tick:
                        tick(1)
                    p.op("scalar", lambda s, c=c: s.square(sq[c % 2][:], xf[c % 4][:]),
                         r=[K("xf", c % 4)], w=[K("sq", c % 2)])
                    p.op("tensor", lambda t, c=c: t.matmul(psA[0:1, :], ones_m1_bf[:], xf[c % 4][:],
                                                           start=(c == 0), stop=(c == NC - 1)),
                         r=[K("xf", c % 4), ("o3",)], w=[psA.k])
                    p.op("tensor", lambda t, c=c: t.matmul(psB[0:1, :], ones_m1_bf[:], sq[c % 2][:],
                                                           start=(c == 0), stop=(c == NC - 1)),
                         r=[K("sq", c % 2), ("o3",)], w=[psB.k])
                p.op("scalar", lambda s: s.mul(rows[0][:], psA[0:1, :], 1.0 / D),
                     r=[psA.k], w=[K("r0")])
                p.op("scalar", lambda s: s.mul(rows[1][:], psB[0:1, :], 1.0 / D),
                     r=[psB.k], w=[K("r1")])
                p.op("vector", lambda v: v.tensor_mul(rows[2][:], rows[0][:], rows[0][:]),
                     r=[K("r0")], w=[K("r2")])
                p.op("vector", lambda v: v.tensor_sub(rows[3][:], rows[1][:], rows[2][:]),
                     r=[K("r1"), K("r2")], w=[K("r3")])
                p.op("scalar", lambda s: s.activation(rows[4][:], rows[3][:], AF.Ln, bias=epscol[0:1, 0:1]),
                     r=[K("r3"), ("eps",)], w=[K("r4")])
                p.op("scalar", lambda s: s.activation(rows[4][:], rows[4][:], AF.Exp, scale=-0.5),
                     r=[K("r4")], w=[K("r4")])
                p.op("vector", lambda v: v.tensor_mul(rows[5][:], rows[0][:], rows[4][:]),
                     r=[K("r0"), K("r4")], w=[K("r5")])
                p.op("tensor", lambda t: t.matmul(psC[:, 0:TB], ones_k1_f[:], rows[4][:], start=True, stop=True),
                     r=[K("r4"), ("o2",)], w=[psC.k])
                p.op("vector", lambda v: v.tensor_copy(bca[0][:], psC[:, 0:TB]),
                     r=[psC.k], w=[K("bc0")])
                p.op("tensor", lambda t: t.matmul(psC[:, 0:TB], ones_k1_f[:], rows[5][:], start=True, stop=True),
                     r=[K("r5"), ("o2",)], w=[psC.k])
                p.op("vector", lambda v: v.tensor_copy(bca[1][:], psC[:, 0:TB]),
                     r=[psC.k], w=[K("bc1")])
                for c in range(NC):
                    p.op("gpsimd", lambda e, c=c, tb=tb: e.dma_start(
                        out=xf[c % 4][:], in_=src_dram[128 * c:128 * (c + 1), TB * tb:TB * (tb + 1)]),
                        r=[src_key(c, tb)], w=[K("xf", c % 4)], dma=True)
                    p.op("vector", lambda v, c=c, tb=tb: v.scalar_tensor_tensor(
                        xn_pool[c][:, TB * tb:TB * (tb + 1)],
                        xf[c % 4][:], 1.0, bca[0][:], ALU.mult, ALU.mult),
                        r=[K("xf", c % 4), K("bc0")], w=[K("xnt", c)])
                    p.op("vector", lambda v, c=c, tb=tb: v.tensor_sub(
                        xn_pool[c][:, TB * tb:TB * (tb + 1)],
                        xn_pool[c][:, TB * tb:TB * (tb + 1)], bca[1][:]),
                        r=[K("xnt", c), K("bc1")], w=[K("xn", c, tb)])
            return lambda c, tb: K("xn", c, tb)

        def phase_A01(RA):
            with contextlib.ExitStack() as ph:
                sbp = lambda name, shape, dt: ph.enter_context(nc.sbuf_tensor(name, list(shape), dt))
                ps0 = PS(ph, nc, "pA0"); ps1 = PS(ph, nc, "pA1"); ps2 = PS(ph, nc, "pA2")
                xn = [sbp(f"Axn{i}", [128, T], BF16) for i in range(NC)]
                xnk = ln_phase("A0", xT, lambda c, tb: ("xTd", c, tb), xn, ph, ps0, ps1, ps2,
                               tick=lambda n: tick_v(1))
                wh = [sbp(f"Awh{i}", [128, D // 2], BF16) for i in range(NC)]
                ev = [sbp(f"Aev{i}", [128, TB], BF16) for i in range(2)]
                for hf in range(2):
                  for mh in range(2):
                    for c in range(NC):
                        p.op("gpsimd", lambda e, c=c, hf=hf, mh=mh: e.dma_start(
                            out=wh[c][:], in_=w_in[128 * c:128 * (c + 1),
                                                   D * hf + 1024 * mh:D * hf + 1024 * (mh + 1)]),
                            w=[("A.wh", c)], dma=True)
                    for m in range(8 * mh, 8 * mh + 8):
                        for tb in range(NTB):
                            pp = (ps0, ps1)[(m * NTB + tb) % 2]
                            def fm(t, m=m, tb=tb, pp=pp, mh=mh):
                                for c in range(NC):
                                    ins = t.matmul(pp[:, 0:TB],
                                                   wh[c][:, 128 * (m - 8 * mh):128 * (m - 8 * mh + 1)],
                                                   xn[c][:, TB * tb:TB * (tb + 1)],
                                                   start=(c == 0), stop=(c == NC - 1))
                                return ins
                            p.op("tensor", fm,
                                 r=[("A.wh", c) for c in range(NC)] + [xnk(c, tb) for c in range(NC)],
                                 w=[pp.k])
                            eb = ev[(m * NTB + tb) % 2]
                            ebk = ("A.ev", (m * NTB + tb) % 2)
                            p.op("scalar", lambda s, pp=pp, eb=eb: s.copy(eb[:], pp[:, 0:TB]),
                                 r=[pp.k], w=[ebk])
                            if hf == 0:
                                p.op("scalar", lambda e, m=m, tb=tb, eb=eb: e.dma_start(
                                    out=XC[128 * m:128 * (m + 1), 3 + TB * tb:3 + TB * (tb + 1)], in_=eb[:]),
                                    r=[ebk], w=[("XC", m, tb)], dma=True)
                            else:
                                p.op("scalar", lambda e, m=m, tb=tb, eb=eb: e.dma_start(
                                    out=RR[128 * m:128 * (m + 1), TB * tb:TB * (tb + 1)], in_=eb[:]),
                                    r=[ebk], w=[("RR", m, tb)], dma=True)
                            tick_v(4)
                zpad = sbp("Azpad", [128, 3], BF16)
                p.op("gpsimd", lambda g: g.memset(zpad[:], 0.0), w=[("A.zpad",)])
                for m in range(NC):
                    p.op("scalar", lambda e, m=m: e.dma_start(out=XC[128 * m:128 * (m + 1), 0:3], in_=zpad[:]),
                         r=[("A.zpad",)], w=[("XC", m, "pad")], dma=True)

        def phase_A2():
            with contextlib.ExitStack() as ph:
                p.barrier()
                sbp = lambda name, shape, dt: ph.enter_context(nc.sbuf_tensor(name, list(shape), dt))
                ps0 = PS(ph, nc, "pA20"); ps1 = PS(ph, nc, "pA21")
                xc = [sbp(f"A2xc{i}", [128, 3 + T], BF16) for i in range(NC)]
                cwb = [sbp(f"A2cw{i}", [128, D], BF16) for i in range(8)]
                cb16 = sbp("A2cb", [128, 16], F32)
                ev = [sbp(f"A2ev{i}", [128, TB], BF16) for i in range(2)]
                p.op("sync", lambda e: e.dma_start(out=cb16[:], in_=conv_b),
                     w=[("A.cb16",)], dma=True)
                for c in range(NC):
                    p.op("sync", lambda e, c=c: e.dma_start(out=xc[c][:], in_=XC[128 * c:128 * (c + 1), :]),
                         r=[("XC", c, "pad")] + [("XC", c, tb) for tb in range(NTB)],
                         w=[("A.xc", c)], dma=True)
                for m in range(NC):
                    par = m % 2
                    for k in range(4):
                        p.op("gpsimd", lambda e, m=m, k=k, par=par: e.dma_start(
                            out=cwb[4 * par + k][:].rearrange("p (c o) -> p c o", c=NC),
                            in_=cw_h[m, k].rearrange("(c p) o -> p c o", p=128)),
                            w=[("A.cwb", 4 * par + k)], dma=True)
                    for tb in range(NTB):
                        pp = (ps0, ps1)[tb % 2]
                        def fm(t, m=m, tb=tb, pp=pp, par=par):
                            for k in range(4):
                                for c in range(NC):
                                    ins = t.matmul(pp[:, 0:TB], cwb[4 * par + k][:, 128 * c:128 * (c + 1)],
                                                   xc[c][:, TB * tb + k:TB * tb + k + TB],
                                                   start=(k == 0 and c == 0), stop=(k == 3 and c == NC - 1))
                            return ins
                        p.op("tensor", fm,
                             r=[("A.cwb", 4 * par + k) for k in range(4)] + [("A.xc", c) for c in range(NC)],
                             w=[pp.k])
                        eb = ev[tb % 2]
                        ebk = ("A.ev2", tb % 2)
                        p.op("vector", lambda v, pp=pp, eb=eb, m=m: v.tensor_scalar(
                            eb[:], pp[:, 0:TB], cb16[:, m:m + 1], None, ALU.add),
                            r=[pp.k, ("A.cb16",)], w=[ebk])
                        p.op("scalar", lambda e, m=m, tb=tb, eb=eb: e.dma_start(
                            out=XS[128 * m:128 * (m + 1), TB * tb:TB * (tb + 1)], in_=eb[:]),
                            r=[ebk], w=[("XS", m, tb)], dma=True)
                        tick_v(10)

        def phase_A3():
            with contextlib.ExitStack() as ph:
                p.barrier()
                sbp = lambda name, shape, dt: ph.enter_context(nc.sbuf_tensor(name, list(shape), dt))
                ps0 = PS(ph, nc, "pA30"); ps1 = PS(ph, nc, "pA31")
                ps2 = PS(ph, nc, "pA32"); ps3 = PS(ph, nc, "pA33")
                wxr = [sbp(f"A3wx{i}", [128, 160], BF16) for i in range(NC)]
                wdtr = sbp("A3wdt", [128, D], BF16)
                nA = sbp("A3nA", [128, 256], F32)
                bdt16 = sbp("A3bdt", [128, 16], F32)
                dv16 = sbp("A3dv", [128, 16], F32)
                xs = [sbp(f"A3xs{i}", [128, TB], BF16) for i in range(2)]
                u = [sbp(f"A3u{i}", [128, TB], BF16) for i in range(8)]
                dpw = sbp("A3dpw", [128, T], BF16)
                Bw = sbp("A3Bw", [16, T], BF16)
                Cw = sbp("A3Cw", [16, T], BF16)
                uw = [sbp(f"A3uw{i}", [128, T], BF16) for i in range(2)]
                saw = sbp("A3saw", [128, T], BF16)
                sbw = sbp("A3sbw", [128, T], BF16)
                dlt = [sbp(f"A3dl{i}", [128, T], BF16) for i in range(2)]
                duw = [sbp(f"A3du{i}", [128, T], BF16) for i in range(2)]
                dAw = [sbp(f"A3dA{i}", [128, T], BF16) for i in range(2)]
                dBw = [sbp(f"A3dB{i}", [128, T], BF16) for i in range(2)]
                Hw = [sbp(f"A3H{i}", [128, T], BF16) for i in range(2)]
                tmpw = [sbp(f"A3tp{i}", [128, T], BF16) for i in range(4)]
                Bbw = [sbp(f"A3Bb{i}", [128, T], BF16) for i in range(2)]
                Cbw = [sbp(f"A3Cb{i}", [128, T], BF16) for i in range(2)]
                yaw = [sbp(f"A3ya{i}", [128, T], F32) for i in range(2)]
                rrw = [sbp(f"A3rr{i}", [128, T], BF16) for i in range(2)]
                oinw = [sbp(f"A3oi{i}", [128, T], BF16) for i in range(2)]

                for c in range(NC):
                    p.op("gpsimd", lambda e, c=c: e.dma_start(out=wxr[c][:], in_=w_x[128 * c:128 * (c + 1), :]),
                         w=[("A.wx", c)], dma=True)
                p.op("gpsimd", lambda e: e.dma_start(out=wdtr[:], in_=w_dt), w=[("A.wdt",)], dma=True)
                p.op("sync", lambda e: e.dma_start(out=nA[:], in_=a_log),
                     w=[("A.nA",)], dma=True)
                p.op("scalar", lambda s: s.activation(nA[:], nA[:], AF.Exp), r=[("A.nA",)], w=[("A.nA",)])
                p.op("vector", lambda v: v.tensor_scalar(nA[:], nA[:], -1.0, None, ALU.mult),
                     r=[("A.nA",)], w=[("A.nA",)])
                p.op("sync", lambda e: e.dma_start(out=bdt16[:], in_=b_dt),
                     w=[("A.bdt",)], dma=True)
                p.op("sync", lambda e: e.dma_start(out=dv16[:], in_=d_vec),
                     w=[("A.dv",)], dma=True)

                for tb in range(NTB):
                    for chf in range(2):
                        for c8 in range(8):
                            c = 8 * chf + c8
                            p.op("sync", lambda e, c=c, tb=tb: e.dma_start(
                                out=xs[c % 2][:], in_=XS[128 * c:128 * (c + 1), TB * tb:TB * (tb + 1)]),
                                r=[("XS", c, tb)], w=[("A.xs", c % 2)], dma=True)
                            p.op("scalar", lambda s, c=c, c8=c8: s.activation(u[c8][:], xs[c % 2][:], AF.Silu),
                                 r=[("A.xs", c % 2)], w=[("A.u", c8)])
                        uks = [("A.u", c8) for c8 in range(8)]
                        wxk = [("A.wx", 8 * chf + c8) for c8 in range(8)]

                        def f_dp(t, chf=chf):
                            for c8 in range(8):
                                ins = t.matmul(ps0[:, 0:TB], wxr[8 * chf + c8][:, 0:128], u[c8][:],
                                               start=(chf == 0 and c8 == 0), stop=(chf == 1 and c8 == 7))
                            return ins
                        p.op("tensor", f_dp, r=wxk + uks, w=[ps0.k])

                        def f_B(t, chf=chf):
                            for c8 in range(8):
                                ins = t.matmul(ps1[0:16, 0:TB], wxr[8 * chf + c8][:, 128:144], u[c8][:],
                                               start=(chf == 0 and c8 == 0), stop=(chf == 1 and c8 == 7))
                            return ins
                        p.op("tensor", f_B, r=wxk + uks, w=[ps1.k])

                        def f_C(t, chf=chf):
                            for c8 in range(8):
                                ins = t.matmul(ps2[0:16, 0:TB], wxr[8 * chf + c8][:, 144:160], u[c8][:],
                                               start=(chf == 0 and c8 == 0), stop=(chf == 1 and c8 == 7))
                            return ins
                        p.op("tensor", f_C, r=wxk + uks, w=[ps2.k])
                    p.op("scalar", lambda s, tb=tb: s.copy(dpw[:, TB * tb:TB * (tb + 1)], ps0[:, 0:TB]),
                         r=[ps0.k], w=[("A.dpw", tb)])
                    p.op("scalar", lambda s, tb=tb: s.copy(Bw[:, TB * tb:TB * (tb + 1)], ps1[0:16, 0:TB]),
                         r=[ps1.k], w=[("A.Bw", tb)])
                    p.op("scalar", lambda s, tb=tb: s.copy(Cw[:, TB * tb:TB * (tb + 1)], ps2[0:16, 0:TB]),
                         r=[ps2.k], w=[("A.Cw", tb)])
                dpk = [("A.dpw", tb) for tb in range(NTB)]
                Bwk = [("A.Bw", tb) for tb in range(NTB)]
                Cwk = [("A.Cw", tb) for tb in range(NTB)]

                for cg in range(8):
                    for ci in range(2):
                        c = 2 * cg + ci
                        kuw = ("A.uw", c % 2)
                        p.op("sync", lambda e, c=c: e.dma_start(
                            out=uw[c % 2][:], in_=XS[128 * c:128 * (c + 1), :]),
                            r=[("XS", c, tb) for tb in range(NTB)], w=[kuw], dma=True)
                        p.op("scalar", lambda s, c=c: s.activation(uw[c % 2][:], uw[c % 2][:], AF.Silu),
                             r=[kuw], w=[kuw])
                        for tb in range(NTB):
                            pp = (ps0, ps3)[tb % 2]
                            p.op("tensor", lambda t, c=c, tb=tb, pp=pp: t.matmul(
                                pp[:, 0:TB], wdtr[:, 128 * c:128 * (c + 1)], dpw[:, TB * tb:TB * (tb + 1)],
                                start=True, stop=True),
                                r=[("A.wdt",)] + dpk, w=[pp.k])
                            p.op("scalar", lambda s, pp=pp, tb=tb, c=c: s.activation(
                                saw[:, TB * tb:TB * (tb + 1)], pp[:, 0:TB], AF.Abs, bias=bdt16[:, c:c + 1]),
                                r=[pp.k, ("A.bdt",)], w=[("A.saw", tb)])
                            p.op("scalar", lambda s, pp=pp, tb=tb, c=c: s.activation(
                                sbw[:, TB * tb:TB * (tb + 1)], pp[:, 0:TB], AF.Relu, bias=bdt16[:, c:c + 1]),
                                r=[pp.k, ("A.bdt",)], w=[("A.sbw", tb)])
                        sawk = [("A.saw", tb) for tb in range(NTB)]
                        sbwk = [("A.sbw", tb) for tb in range(NTB)]
                        p.op("scalar", lambda s: s.activation(saw[:], saw[:], AF.Exp, scale=-1.0),
                             r=sawk, w=[("A.sawf",)])
                        p.op("scalar", lambda s: s.activation(saw[:], saw[:], AF.Ln, bias=onecol[:, 0:1]),
                             r=[("A.sawf",), ("onec",)], w=[("A.sawf",)])
                        p.op("vector", lambda v, ci=ci: v.tensor_add(dlt[ci][:], saw[:], sbw[:]),
                             r=[("A.sawf",)] + sbwk, w=[("A.dl", ci)])
                        p.op("vector", lambda v, ci=ci: v.tensor_mul(duw[ci][:], dlt[ci][:], uw[ci][:]),
                             r=[("A.dl", ci), ("A.uw", ci)], w=[("A.du", ci)])
                    for n in range(16):
                        tick_v(1)
                        npar = n % 2
                        Bb, Cb = Bbw[npar], Cbw[npar]
                        KB, KC = ("A.Bbw", npar), ("A.Cbw", npar)
                        for tb in range(NTB):
                            pp = (ps1, ps2)[tb % 2]
                            p.op("tensor", lambda t, n=n, tb=tb, pp=pp: t.matmul(
                                pp[:, 0:TB], sel_n(n), Bw[:, TB * tb:TB * (tb + 1)], start=True, stop=True),
                                r=Bwk + [("sel", n)], w=[pp.k])
                            p.op("scalar", lambda s, tb=tb, pp=pp, Bb=Bb: s.copy(
                                Bb[:, TB * tb:TB * (tb + 1)], pp[:, 0:TB]), r=[pp.k], w=[(KB, tb)])
                            p.op("tensor", lambda t, n=n, tb=tb, pp=pp: t.matmul(
                                pp[:, 0:TB], sel_n(n), Cw[:, TB * tb:TB * (tb + 1)], start=True, stop=True),
                                r=Cwk + [("sel", n), (KB, tb)], w=[pp.k])
                            p.op("scalar", lambda s, tb=tb, pp=pp, Cb=Cb: s.copy(
                                Cb[:, TB * tb:TB * (tb + 1)], pp[:, 0:TB]), r=[pp.k], w=[(KC, tb)])
                        KBf = [(KB, tb) for tb in range(NTB)]
                        KCf = [(KC, tb) for tb in range(NTB)]
                        for ci in range(2):
                            c = 2 * cg + ci
                            rot = (2 * n + ci) % 2
                            p.op("scalar", lambda s, c=c, n=n, rot=rot, ci=ci: s.activation(
                                dAw[rot][:], dlt[ci][:], AF.Exp, scale=nA[:, 16 * c + n:16 * c + n + 1]),
                                r=[("A.dl", ci), ("A.nA",)], w=[("A.dAw", rot)])
                            p.op("vector", lambda v, ci=ci, rot=rot, Bb=Bb: v.tensor_mul(
                                dBw[rot][:], duw[ci][:], Bb[:]),
                                r=[("A.du", ci)] + KBf, w=[("A.dBw", rot)])
                            p.op("vector", lambda v, rot=rot: v.tensor_tensor_scan(
                                Hw[rot][:], dAw[rot][:], dBw[rot][:], 0.0, ALU.mult, ALU.add),
                                r=[("A.dAw", rot), ("A.dBw", rot)], w=[("A.Hw", rot)])
                            tpi = (2 * n + ci) % 4
                            p.op("vector", lambda v, rot=rot, tpi=tpi, Cb=Cb: v.tensor_mul(
                                tmpw[tpi][:], Hw[rot][:], Cb[:]),
                                r=[("A.Hw", rot)] + KCf, w=[("A.tpw", tpi)])
                            if n == 0:
                                p.op("gpsimd", lambda g, ci=ci, tpi=tpi: g.dma_start(
                                    out=yaw[ci][:], in_=tmpw[tpi][:], max_dma_last_dim=2048),
                                    r=[("A.tpw", tpi)], w=[("A.yaw", ci)], dma=True, qchain=True)
                            else:
                                p.op("gpsimd", lambda g, ci=ci, tpi=tpi: g.dma_start(
                                    out=yaw[ci][:], in_=tmpw[tpi][:], accum_op=ALU.add,
                                    max_dma_last_dim=2048),
                                    r=[("A.tpw", tpi), ("A.yaw", ci)], w=[("A.yaw", ci)], dma=True, qchain=True)
                    for ci in range(2):
                        c = 2 * cg + ci
                        p.op("sync", lambda e, c=c, ci=ci: e.dma_start(
                            out=rrw[ci][:], in_=RR[128 * c:128 * (c + 1), :]),
                            r=[("RR", c, tb) for tb in range(NTB)], w=[("A.rrw", ci)], dma=True)
                        p.op("scalar", lambda s, ci=ci: s.activation(rrw[ci][:], rrw[ci][:], AF.Silu),
                             r=[("A.rrw", ci)], w=[("A.rrw", ci)])
                        p.op("vector", lambda v, c=c, ci=ci: v.scalar_tensor_tensor(
                            oinw[ci][:], uw[ci][:], dv16[:, c:c + 1], yaw[ci][:], ALU.mult, ALU.add),
                            r=[("A.uw", ci), ("A.dv",), ("A.yaw", ci)], w=[("A.oiw", ci)])
                        p.op("vector", lambda v, ci=ci: v.tensor_mul(oinw[ci][:], oinw[ci][:], rrw[ci][:]),
                             r=[("A.oiw", ci), ("A.rrw", ci)], w=[("A.oiw", ci)])
                        p.op("scalar", lambda e, c=c, ci=ci: e.dma_start(
                            out=OIN[128 * c:128 * (c + 1), :], in_=oinw[ci][:]),
                            r=[("A.oiw", ci)], w=[("OIN", c, 0), ("OIN", c, 1), ("OIN", c, 2), ("OIN", c, 3)],
                            dma=True)

        def phase_A4(RA):
            with contextlib.ExitStack() as ph:
                p.barrier()
                sbp = lambda name, shape, dt: ph.enter_context(nc.sbuf_tensor(name, list(shape), dt))
                ps0 = PS(ph, nc, "pA40"); ps1 = PS(ph, nc, "pA41")
                oin = [sbp(f"A4o{i}", [128, TB], BF16) for i in range(NC)]
                xs = [sbp(f"A4x{i}", [128, TB], BF16) for i in range(NC)]
                wob = [sbp(f"A4w{i}", [128, TB], BF16) for i in range(2 * NC)]
                xf = [sbp(f"A4f{i}", [128, TB], F32) for i in range(2)]
                rof = [sbp(f"A4r{i}", [128, TB], F32) for i in range(2)]
                for tb in range(NTB):
                    for c in range(NC):
                        p.op("sync", lambda e, c=c, tb=tb: e.dma_start(
                            out=oin[c][:], in_=OIN[128 * c:128 * (c + 1), TB * tb:TB * (tb + 1)]),
                            r=[("OIN", c, tb)], w=[("A4.oin", c)], dma=True)
                        p.op("sync", lambda e, c=c, tb=tb: e.dma_start(
                            out=xs[c][:], in_=XS[128 * c:128 * (c + 1), TB * tb:TB * (tb + 1)]),
                            r=[("XS", c, tb)], w=[("A4.xs", c)], dma=True)
                    for og in range(4):
                        par = og % 2
                        for c in range(NC):
                            p.op("gpsimd", lambda e, c=c, og=og, par=par: e.dma_start(
                                out=wob[NC * par + c][:], in_=w_out[128 * c:128 * (c + 1), TB * og:TB * (og + 1)]),
                                w=[("A4.wo", NC * par + c)], dma=True)
                        for j in range(4):
                            m = 4 * og + j
                            pp = (ps0, ps1)[j % 2]
                            def fm(t, pp=pp, par=par, j=j):
                                for c in range(NC):
                                    ins = t.matmul(pp[:, 0:TB], wob[NC * par + c][:, 128 * j:128 * (j + 1)],
                                                   oin[c][:], start=(c == 0), stop=(c == NC - 1))
                                return ins
                            p.op("tensor", fm,
                                 r=[("A4.wo", NC * par + c) for c in range(NC)] + [("A4.oin", c) for c in range(NC)],
                                 w=[pp.k])
                            jp = j % 2
                            p.op("sync", lambda e, m=m, tb=tb, jp=jp: e.dma_start(
                                out=xf[jp][:], in_=xT[128 * m:128 * (m + 1), TB * tb:TB * (tb + 1)]),
                                w=[("A4.xf", jp)], dma=True)
                            p.op("vector", lambda v, pp=pp, m=m, jp=jp: v.scalar_tensor_tensor(
                                rof[jp][:], pp[:, 0:TB], 1.0, xs[m][:], ALU.mult, ALU.add),
                                r=[pp.k, ("A4.xs", m)], w=[("A4.ro", jp)])
                            p.op("vector", lambda v, jp=jp: v.tensor_add(rof[jp][:], rof[jp][:], xf[jp][:]),
                                 r=[("A4.ro", jp), ("A4.xf", jp)], w=[("A4.ro", jp)])
                            p.op("scalar", lambda e, m=m, tb=tb, jp=jp: e.dma_start(
                                out=RA[128 * m:128 * (m + 1), TB * tb:TB * (tb + 1)], in_=rof[jp][:]),
                                r=[("A4.ro", jp)], w=[(res_key["A"], m, tb)], dma=True)
                            tick_v(10)

        dc_iv = [sb(f"DCiv{i}", [128, INTER // 2], BF16) for i in range(2)]
        dc_pv = sb("DCpv", [128, INTER // 2], BF16)
        dc_av = [sb(f"DCav{i}", [128, INTER // 2], BF16) for i in range(2)]
        dc_ig = [sb(f"DCig{i}", [128, D], BF16) for i in range(2)]
        dc_pg = sb("DCpg", [128, D], BF16)
        dc_ag = [sb(f"DCag{i}", [128, D], BF16) for i in range(2)]

        def decode_gen(eng, ld_eng, st_eng, idxb, prodb, accb, blocks):
            def gen():
                hb = 0
                for (srcv, dstv, F, wkey) in blocks:
                    par = hb % 2
                    hb += 1
                    ib, ac = idxb[par], accb[par]
                    ibk = ("DC", eng, par, "ib")
                    ack = ("DC", eng, par, "ac")
                    prk = ("DC", eng, "pr")
                    p.op(ld_eng, lambda e, srcv=srcv, ib=ib, F=F: e.dma_start(
                        out=ib[:, 0:F], in_=srcv),
                        w=[ibk], dma=True)
                    yield
                    p.op(eng, lambda v, ib=ib, ac=ac, F=F: v.tensor_scalar(
                        ac[:, 0:F], ib[:, 0:F], 0.0, lutbc[:, 0:1],
                        ALU.is_equal, ALU.mult),
                        r=[ibk, LUTK], w=[ack])
                    yield
                    for j in range(1, 16):
                        p.op(eng, lambda v, ib=ib, F=F, j=j: v.tensor_scalar(
                            prodb[:, 0:F], ib[:, 0:F], float(j), lutbc[:, j:j + 1],
                            ALU.is_equal, ALU.mult),
                            r=[ibk, LUTK], w=[prk])
                        yield
                        p.op(eng, lambda v, ac=ac, F=F: v.tensor_add(
                            ac[:, 0:F], ac[:, 0:F], prodb[:, 0:F]),
                            r=[ack, prk], w=[ack])
                        yield
                    p.op(st_eng, lambda e, dstv=dstv, ac=ac, F=F: e.dma_start(
                        out=dstv, in_=ac[:, 0:F]),
                        r=[ack], w=[wkey], dma=True)
                    yield
            return gen()

        HB0 = INTER // 2
        blk_gu = []
        for h in range(2):
            lo, hi = HB0 * h, HB0 * (h + 1)
            for c in range(NC):
                blk_gu.append((gidxT[128 * c:128 * (c + 1), lo:hi],
                               WG[128 * c:128 * (c + 1), lo:hi], HB0, ("WG", c, h)))
                blk_gu.append((uidxT[128 * c:128 * (c + 1), lo:hi],
                               WU[128 * c:128 * (c + 1), lo:hi], HB0, ("WU", c, h)))
        blk_d = [(didxT[128 * o:128 * (o + 1), :], WD[128 * o:128 * (o + 1), :],
                  D, ("WD", o)) for o in range(NO)]
        gen_v = decode_gen("vector", "gpsimd", "scalar", dc_iv, dc_pv, dc_av, blk_gu)
        gen_g = decode_gen("vector", "gpsimd", "sync", dc_ig, dc_pg, dc_ag, blk_d)
        n_v_left = [len(blk_gu) * 33]

        def tick_v(n):
            for _ in range(n):
                if next(gen_v, "end") == "end":
                    if next(gen_g, "end") == "end":
                        return
                else:
                    n_v_left[0] -= 1
        def drain_v_h0(n=10 ** 9):
            while n_v_left[0] > len(blk_gu) // 2 * 33:
                tick_v(1)
        def tick_g(n):
            for _ in range(n):
                if next(gen_g, "end") == "end":
                    return

        def ffn_phase(tag, src_dram, src_keyf, dst_dram, dst_keyn):
            with contextlib.ExitStack() as ph:
                p.barrier()
                sbp = lambda name, shape, dt: ph.enter_context(nc.sbuf_tensor(name, list(shape), dt))
                psA = PS(ph, nc, tag + "p0"); psB = PS(ph, nc, tag + "p1")
                psC = PS(ph, nc, tag + "p2"); psD = PS(ph, nc, tag + "p3")
                xn = [sbp(f"{tag}xn{i}", [128, T], BF16) for i in range(NC)]
                xnk = ln_phase(tag + "0", src_dram, src_keyf, xn, ph, psA, psB, psC,
                               tick=lambda n: tick_v(2))
                wgb = [sbp(f"{tag}wg{i}", [128, TB], BF16) for i in range(2 * NC)]
                wub = [sbp(f"{tag}wu{i}", [128, TB], BF16) for i in range(2 * NC)]
                hg = [sbp(f"{tag}hg{i}", [128, TB], BF16) for i in range(2)]
                hb = [sbp(f"{tag}hb{i}", [128, TB], BF16) for i in range(2)]
                psrot = [psA, psB, psC, psD]
                for og in range(11):
                    if og == 5:
                        tick_v(10 ** 9)
                    par = og % 2
                    ow = 512 if og < 10 else INTER - 5120
                    nj = 4 if og < 10 else 3
                    hks = [0] if og < 5 else ([0, 1] if og == 5 else [1])
                    for c in range(NC):
                        p.op("sync", lambda e, c=c, og=og, par=par, ow=ow: e.dma_start(
                            out=wgb[NC * par + c][:, 0:ow], in_=WG[128 * c:128 * (c + 1), 512 * og:512 * og + ow]),
                            r=[("WG", c, h) for h in hks], w=[(tag, "wg", NC * par + c)], dma=True)
                        p.op("sync", lambda e, c=c, og=og, par=par, ow=ow: e.dma_start(
                            out=wub[NC * par + c][:, 0:ow], in_=WU[128 * c:128 * (c + 1), 512 * og:512 * og + ow]),
                            r=[("WU", c, h) for h in hks], w=[(tag, "wu", NC * par + c)], dma=True)
                    for j in range(nj):
                        ot = 4 * og + j
                        for tb in range(NTB):
                            pg = psrot[(2 * (ot * NTB + tb)) % 4]
                            pu = psrot[(2 * (ot * NTB + tb) + 1) % 4]
                            def fg(t, par=par, j=j, tb=tb, pg=pg):
                                for c in range(NC):
                                    ins = t.matmul(pg[:, 0:TB], wgb[NC * par + c][:, 128 * j:128 * (j + 1)],
                                                   xn[c][:, TB * tb:TB * (tb + 1)],
                                                   start=(c == 0), stop=(c == NC - 1))
                                return ins
                            p.op("tensor", fg,
                                 r=[(tag, "wg", NC * par + c) for c in range(NC)] + [xnk(c, tb) for c in range(NC)],
                                 w=[pg.k])
                            def fu(t, par=par, j=j, tb=tb, pu=pu):
                                for c in range(NC):
                                    ins = t.matmul(pu[:, 0:TB], wub[NC * par + c][:, 128 * j:128 * (j + 1)],
                                                   xn[c][:, TB * tb:TB * (tb + 1)],
                                                   start=(c == 0), stop=(c == NC - 1))
                                return ins
                            p.op("tensor", fu,
                                 r=[(tag, "wu", NC * par + c) for c in range(NC)] + [xnk(c, tb) for c in range(NC)],
                                 w=[pu.k])
                            hpar = (ot * NTB + tb) % 2
                            p.op("scalar", lambda s, hpar=hpar, pg=pg: s.activation(hg[hpar][:], pg[:, 0:TB], AF.Silu),
                                 r=[pg.k], w=[(tag, "hg", hpar)])
                            p.op("vector", lambda v, hpar=hpar, pu=pu: v.tensor_mul(hb[hpar][:], hg[hpar][:], pu[:, 0:TB]),
                                 r=[(tag, "hg", hpar), pu.k], w=[(tag, "hb", hpar)])
                            p.op("scalar", lambda e, ot=ot, tb=tb, hpar=hpar: e.dma_start(
                                out=HB[128 * ot:128 * (ot + 1), TB * tb:TB * (tb + 1)], in_=hb[hpar][:]),
                                r=[(tag, "hb", hpar)], w=[(tag, "HB", ot, tb)], dma=True)
                            tick_v(10)
            tick_g(10 ** 9)
            with contextlib.ExitStack() as ph:
                p.barrier()
                sbp = lambda name, shape, dt: ph.enter_context(nc.sbuf_tensor(name, list(shape), dt))
                pso = [PS(ph, nc, f"{tag}q{i}") for i in range(8)]
                hcol = [sbp(f"{tag}hc{i}", [128, 1024], BF16) for i in range(NO)]
                r1t = [sbp(f"{tag}r1{i}", [128, TB], F32) for i in range(4)]
                wdb = [sbp(f"{tag}wd{i}", [128, TB], BF16) for i in range(4)]
                ev = [sbp(f"{tag}ev{i}", [128, TB], F32) for i in range(2)]
                for th in range(2):
                    for o in range(NO):
                        p.op("sync", lambda e, o=o, th=th: e.dma_start(
                            out=hcol[o][:], in_=HB[128 * o:128 * (o + 1), 1024 * th:1024 * (th + 1)]),
                            r=[(tag, "HB", o, 2 * th), (tag, "HB", o, 2 * th + 1)],
                            w=[(tag, "hc", o)], dma=True)
                    for dmg in range(4):
                        for o in range(NO):
                            par = o % 4
                            eng = "gpsimd" if o % 2 == 0 else "scalar"
                            p.op(eng, lambda e, o=o, dmg=dmg, par=par: e.dma_start(
                                out=wdb[par][:], in_=WD[128 * o:128 * (o + 1), TB * dmg:TB * (dmg + 1)]),
                                r=[("WD", o)], w=[(tag, "wd", par)], dma=True)
                            def fd(t, o=o, par=par):
                                for dj in range(4):
                                    for t2 in range(2):
                                        ins = t.matmul(pso[2 * dj + t2][:, 0:TB],
                                                       wdb[par][:, 128 * dj:128 * (dj + 1)],
                                                       hcol[o][:, TB * t2:TB * (t2 + 1)],
                                                       start=(o == 0), stop=(o == NO - 1))
                                return ins
                            p.op("tensor", fd,
                                 r=[(tag, "wd", par), (tag, "hc", o)],
                                 w=[pso[i].k for i in range(8)])
                        for dj in range(4):
                            m = 4 * dmg + dj
                            for t2 in range(2):
                                epar = t2
                                tb = 2 * th + t2
                                rb = (2 * dj + t2) % 4
                                p.op("sync", lambda e, m=m, tb=tb, rb=rb: e.dma_start(
                                    out=r1t[rb][:], in_=src_dram[128 * m:128 * (m + 1), TB * tb:TB * (tb + 1)]),
                                    r=[src_keyf(m, tb)], w=[(tag, "r1", rb)], dma=True)
                                p.op("vector", lambda v, dj=dj, t2=t2, m=m, epar=epar, rb=rb: v.scalar_tensor_tensor(
                                    ev[epar][:], pso[2 * dj + t2][:, 0:TB], 1.0,
                                    r1t[rb][:], ALU.mult, ALU.add),
                                    r=[pso[2 * dj + t2].k, (tag, "r1", rb)], w=[(tag, "ev", epar)])
                                p.op("scalar", lambda e, m=m, tb=tb, epar=epar: e.dma_start(
                                    out=dst_dram[128 * m:128 * (m + 1), TB * tb:TB * (tb + 1)], in_=ev[epar][:]),
                                    r=[(tag, "ev", epar)], w=[(dst_keyn, m, tb)], dma=True)

        def phase_C(RC):
            with contextlib.ExitStack() as ph:
                p.barrier()
                sbp = lambda name, shape, dt: ph.enter_context(nc.sbuf_tensor(name, list(shape), dt))
                psA = PS(ph, nc, "pC0"); psB = PS(ph, nc, "pC1"); psC = PS(ph, nc, "pC2")
                xn = [sbp(f"Cxn{i}", [128, T], BF16) for i in range(NC)]
                xnk = ln_phase("C0", R2, lambda c, tb: ("RES_B", c, tb), xn, ph, psA, psB, psC)
                wpb = [sbp(f"Cwp{i}", [128, TB], BF16) for i in range(2 * NC)]
                evb = [sbp(f"Cev{i}", [128, TB], BF16) for i in range(2)]
                za = [sbp(f"Cza{i}", [128, TB], F32) for i in range(2)]
                zm = [sbp(f"Czm{i}", [128, TB], F32) for i in range(2)]
                bgt = [sbp(f"Cbg{i}", [128, NC], F32) for i in range(2)]
                p.op("sync", lambda e: e.dma_start(out=bgt[0][:], in_=bg1), w=[("C.bg", 0)], dma=True)
                p.op("sync", lambda e: e.dma_start(out=bgt[1][:], in_=bg2), w=[("C.bg", 1)], dma=True)
                projs = [(wqT, QF, "QF", None), (wkT, KF, "KF", None), (wvT, VF, "VF", None),
                         (wg1T, G1, "G1", 0), (wg2T, G2, "G2", 1)]
                if 1 not in c_sub:
                    projs = []
                elif 4 in c_sub:
                    projs = projs[:3]
                elif 5 in c_sub:
                    projs = projs[3:4]
                elif 6 in c_sub:
                    projs = projs[:4]
                elif 7 in c_sub:
                    projs = projs[3:5]
                for wsrc, dst, dkey, gate_i in projs:
                    for og in range(4):
                        par = og % 2
                        for c in range(NC):
                            p.op("gpsimd", lambda e, c=c, og=og, par=par, wsrc=wsrc: e.dma_start(
                                out=wpb[NC * par + c][:], in_=wsrc[128 * c:128 * (c + 1), TB * og:TB * (og + 1)]),
                                w=[("C.wp", NC * par + c)], dma=True)
                        for j in range(4):
                            it = 4 * og + j
                            for tb in range(NTB):
                                pp = (psA, psB)[(j * NTB + tb) % 2]
                                def fp_(t, par=par, j=j, tb=tb, pp=pp):
                                    for c in range(NC):
                                        ins = t.matmul(pp[:, 0:TB], wpb[NC * par + c][:, 128 * j:128 * (j + 1)],
                                                       xn[c][:, TB * tb:TB * (tb + 1)],
                                                       start=(c == 0), stop=(c == NC - 1))
                                    return ins
                                p.op("tensor", fp_,
                                     r=[("C.wp", NC * par + c) for c in range(NC)] + [xnk(c, tb) for c in range(NC)],
                                     w=[pp.k])
                                ep = (j * NTB + tb) % 2
                                if gate_i is None:
                                    p.op("scalar", lambda s, pp=pp, ep=ep: s.copy(evb[ep][:], pp[:, 0:TB]),
                                         r=[pp.k], w=[("C.ev", ep)])
                                elif 8 in c_sub:
                                    gi = gate_i
                                    p.op("vector", lambda v, pp=pp, ep=ep, it=it, gi=gi: v.tensor_scalar(
                                        evb[ep][:], pp[:, 0:TB], bgt[gi][:, it:it + 1], 0.0, ALU.add, ALU.min),
                                        r=[pp.k, ("C.bg", gi)], w=[("C.ev", ep)])
                                elif 9 in c_sub:
                                    gi = gate_i
                                    p.op("scalar", lambda s, pp=pp, ep=ep, it=it, gi=gi: s.activation(
                                        za[ep][:], pp[:, 0:TB], AF.Abs, bias=bgt[gi][:, it:it + 1]),
                                        r=[pp.k, ("C.bg", gi)], w=[("C.za", ep)])
                                    p.op("vector", lambda v, ep=ep: v.tensor_copy(evb[ep][:], za[ep][:]),
                                         r=[("C.za", ep)], w=[("C.ev", ep)])
                                else:
                                    gi = gate_i
                                    p.op("scalar", lambda s, pp=pp, ep=ep, it=it, gi=gi: s.activation(
                                        za[ep][:], pp[:, 0:TB], AF.Abs, bias=bgt[gi][:, it:it + 1]),
                                        r=[pp.k, ("C.bg", gi)], w=[("C.za", ep)])
                                    p.op("scalar", lambda s, ep=ep: s.activation(za[ep][:], za[ep][:], AF.Exp, scale=-1.0),
                                         r=[("C.za", ep)], w=[("C.za", ep)])
                                    p.op("scalar", lambda s, ep=ep: s.activation(za[ep][:], za[ep][:], AF.Ln, bias=onecol[:, 0:1]),
                                         r=[("C.za", ep), ("onec",)], w=[("C.za", ep)])
                                    p.op("vector", lambda v, pp=pp, ep=ep, it=it, gi=gi: v.tensor_scalar(
                                        zm[ep][:], pp[:, 0:TB], bgt[gi][:, it:it + 1], 0.0, ALU.add, ALU.min),
                                        r=[pp.k, ("C.bg", gi)], w=[("C.zm", ep)])
                                    p.op("vector", lambda v, ep=ep: v.tensor_sub(zm[ep][:], zm[ep][:], za[ep][:]),
                                         r=[("C.zm", ep), ("C.za", ep)], w=[("C.zm", ep)])
                                    p.op("vector", lambda v, ep=ep: v.tensor_scalar(
                                        evb[ep][:], zm[ep][:], 1.0 / 16.0, -50.0, ALU.mult, ALU.max),
                                        r=[("C.zm", ep)], w=[("C.ev", ep)])
                                p.op("scalar", lambda e, dst=dst, it=it, tb=tb, ep=ep: e.dma_start(
                                    out=dst[128 * it:128 * (it + 1), TB * tb:TB * (tb + 1)], in_=evb[ep][:]),
                                    r=[("C.ev", ep)], w=[(dkey, it, tb)], dma=True)

            with contextlib.ExitStack() as ph:
                p.barrier()
                sbp = lambda name, shape, dt: ph.enter_context(nc.sbuf_tensor(name, list(shape), dt))
                psS_ = [PS(ph, nc, f"pS{i}") for i in range(2)]
                psO_ = [PS(ph, nc, f"pO{i}") for i in range(2)]
                psSD_ = [PS(ph, nc, f"pSD{i}") for i in range(2)]
                psT1 = PS(ph, nc, "pT1", BF16); psT2 = PS(ph, nc, "pT2", BF16)
                psAl = psS_[0]
                attn = [sbp(f"Cat{i}", [128, T], BF16) for i in range(NC)]
                qh = [sbp(f"Cq{i}", [128, T], BF16) for i in range(1)]
                kh = [sbp(f"Ck{i}", [128, T], BF16) for i in range(1)]
                vh = [sbp(f"Cv{i}", [128, T], BF16) for i in range(1)]
                g1h = [sbp(f"Cg1{i}", [128, T], BF16) for i in range(1)]
                g2h = [sbp(f"Cg2{i}", [128, T], BF16) for i in range(1)]
                vtm = sbp("Cvtm", [128, T], BF16)
                ocomb = sbp("Coc", [128, T], BF16)
                st_ = [sbp(f"Cst{i}", [128, 128], F32) for i in range(2)]
                Gt_ = [sbp(f"CG{i}", [128, 128], F32) for i in range(2)]
                eq_ = [sbp(f"Ceq{i}", [128, 128], BF16) for i in range(2)]
                ek_ = [sbp(f"Cek{i}", [128, 128], BF16) for i in range(2)]
                qls_ = [sbp(f"Cqls{i}", [128, 128], BF16) for i in range(2)]
                kx_ = [sbp(f"Ckx{i}", [128, 128], BF16) for i in range(2)]
                kd_ = [sbp(f"Ckd{i}", [128, 128], BF16) for i in range(2)]
                ek2_ = [sbp(f"Cek2{i}", [128, 128], BF16) for i in range(2)]
                kdtm_ = [sbp(f"Ckdt{i}", [128, 128], BF16) for i in range(2)]
                Ssb_ = [sbp(f"CSsb{i}", [128, 128], BF16) for i in range(2)]
                ccol_ = [sbp(f"Ccc{i}", [128, 4], F32) for i in range(2)]
                stbf_ = [sbp(f"Cstb{i}", [128, 128], BF16) for i in range(2)]
                al = sbp("Cal", [1, 2], F32)
                al2 = sbp("Cal2", [1, 2], F32)
                mx1 = sbp("Cmx", [1, 1], F32)
                a01 = sbp("Ca01", [128, 2], F32)
                p.op("sync", lambda e: e.dma_start(out=al[:], in_=alpha), w=[("C.al",)], dma=True)
                p.op("vector", lambda v: v.reduce_max(mx1[:], al[:], axis=mybir.AxisListType.X),
                     r=[("C.al",)], w=[("C.mx",)])
                p.op("vector", lambda v: v.tensor_scalar(al2[:], al[:], mx1[0:1, 0:1], None, ALU.subtract),
                     r=[("C.al",), ("C.mx",)], w=[("C.al2",)])
                p.op("scalar", lambda s: s.activation(al2[:], al2[:], AF.Exp),
                     r=[("C.al2",)], w=[("C.al2",)])
                p.op("vector", lambda v: v.reduce_sum(mx1[:], al2[:], axis=mybir.AxisListType.X),
                     r=[("C.al2",)], w=[("C.mx",)])
                p.op("vector", lambda v: v.reciprocal(mx1[:], mx1[:]), r=[("C.mx",)], w=[("C.mx",)])
                p.op("vector", lambda v: v.tensor_scalar(al2[:], al2[:], mx1[0:1, 0:1], None, ALU.mult),
                     r=[("C.al2",), ("C.mx",)], w=[("C.al2",)])
                p.op("tensor", lambda t: t.matmul(psAl[:, 0:2], ones_k1_f[:], al2[:], start=True, stop=True),
                     r=[("C.al2",), ("o2",)], w=[psAl.k])
                p.op("scalar", lambda s: s.copy(a01[:], psAl[:, 0:2]), r=[psAl.k], w=[("C.a01",)])

                NCH = T // 128
                for h in (range(NC) if 2 in c_sub else []):
                    par = 0
                    for (buf, dsrc, dkey) in ((qh, QF, "QF"), (kh, KF, "KF"), (vh, VF, "VF"),
                                              (g1h, G1, "G1"), (g2h, G2, "G2")):
                        p.op("sync", lambda e, buf=buf, dsrc=dsrc, h=h, par=par: e.dma_start(
                            out=buf[par][:], in_=dsrc[128 * h:128 * (h + 1), :]),
                            r=[(dkey, h, tb) for tb in range(NTB)], w=[("C." + dkey, par)], dma=True)
                    p.op("vector", lambda v, par=par: v.tensor_scalar(
                        qh[par][:], qh[par][:], 128.0 ** -0.5, None, ALU.mult),
                        r=[("C.QF", par)], w=[("C.QF", par)])
                    p.op("gpsimd", lambda g, par=par: g.memset(st_[0][:], 0.0), w=[("C.st", 0)])
                    p.op("gpsimd", lambda g, par=par: g.memset(st_[1][:], 0.0), w=[("C.st", 1)])
                    for ch in range(NCH):
                        sl = slice(128 * ch, 128 * (ch + 1))
                        for inst in range(2):
                            gh = (g1h, g2h)[inst]
                            gk = ("C.G1", "C.G2")[inst]
                            stk = ("C.st", inst)
                            cp = inst
                            Gt, eq, ek, qls, kx = Gt_[cp], eq_[cp], ek_[cp], qls_[cp], kx_[cp]
                            kd, kdtm, Ssb, ccol, stbf = kd_[cp], kdtm_[cp], Ssb_[cp], ccol_[cp], stbf_[cp]
                            psS, psO, psSD = psS_[cp], psO_[cp], psSD_[cp]
                            KG = ("C.Gt", cp); KEQ = ("C.eq", cp); KEK = ("C.ek", cp)
                            KQL = ("C.qls", cp); KKX = ("C.kx", cp); KKD = ("C.kd", cp)
                            KKDT = ("C.kdtm", cp); KSS = ("C.Ssb", cp); KSB = ("C.stbf", cp)
                            p.op("vector", lambda v, gh=gh, par=par, sl=sl, Gt=Gt: v.tensor_tensor_scan(
                                Gt[:], gh[par][:, sl], zeros512[:, 0:128], 0.0, ALU.add, ALU.add),
                                r=[(gk, par), ("z5",)], w=[KG])
                            p.op("vector", lambda v, Gt=Gt, ccol=ccol: v.tensor_scalar(
                                ccol[:, 0:1], Gt[:, 63:64], -1.0, None, ALU.mult),
                                r=[KG], w=[("C.cc", cp, 0)])
                            p.op("scalar", lambda s, Gt=Gt, eq=eq, ccol=ccol: s.activation(
                                eq[:], Gt[:], AF.Exp, bias=ccol[:, 0:1]),
                                r=[KG, ("C.cc", cp, 0)], w=[KEQ])
                            p.op("scalar", lambda s, Gt=Gt, ek=ek: s.activation(
                                ek[:], Gt[:], AF.Exp, scale=-1.0, bias=Gt[:, 63:64]),
                                r=[KG], w=[KEK])
                            p.op("gpsimd", lambda g, par=par, sl=sl, qls=qls, eq=eq: g.tensor_mul(
                                qls[:], qh[par][:, sl], eq[:]),
                                r=[("C.QF", par), KEQ], w=[KQL])
                            p.op("gpsimd", lambda g, par=par, sl=sl, kx=kx, ek=ek: g.tensor_mul(
                                kx[:], kh[par][:, sl], ek[:]),
                                r=[("C.KF", par), KEK], w=[KKX])
                            p.op("tensor", lambda t, kx=kx, qls=qls, psS=psS: t.matmul(
                                psS[:, 0:128], kx[:], qls[:], start=True, stop=True),
                                r=[KKX, KQL], w=[psS.k])
                            p.op("vector", lambda v, Ssb=Ssb, psS=psS: v.scalar_tensor_tensor(
                                Ssb[:], psS[:, 0:128], 1.0, mask_ut[:], ALU.mult, ALU.mult),
                                r=[psS.k, ("mut",)], w=[KSS])
                            if inst == 0:
                                p.op("tensor", lambda t, par=par, sl=sl: t.transpose(
                                    psT1[:, 0:128], vh[par][:, sl], ident[:]),
                                    r=[("C.VF", par), ("ident",)], w=[psT1.k])
                                p.op("scalar", lambda s, sl=sl: s.copy(vtm[:, sl], psT1[:, 0:128]),
                                     r=[psT1.k], w=[("C.vtm", sl.start)])
                            p.op("scalar", lambda s, Gt=Gt, ccol=ccol: s.activation(
                                ccol[:, 1:3], Gt[:, 63:128:64], AF.Exp),
                                r=[KG], w=[("C.cc", cp, 1), ("C.cc", cp, 2)])
                            p.op("vector", lambda v, inst=inst, stbf=stbf, ccol=ccol: v.tensor_scalar(
                                stbf[:], st_[inst][:], ccol[:, 1:2], None, ALU.mult),
                                r=[stk, ("C.cc", cp, 1)], w=[KSB])
                            def fo(t, sl=sl, Ssb=Ssb, stbf=stbf, qls=qls, psO=psO):
                                t.matmul(psO[:, 0:128], vtm[:, sl], Ssb[:], start=True, stop=False)
                                return t.matmul(psO[:, 0:128], stbf[:], qls[:], start=False, stop=True)
                            p.op("tensor", fo, r=[("C.vtm", sl.start), KSS, KSB, KQL],
                                 w=[psO.k])
                            if inst == 0:
                                p.op("vector", lambda v, sl=sl, psO=psO: v.tensor_scalar(
                                    ocomb[:, sl], psO[:, 0:128], a01[:, 0:1], None, ALU.mult),
                                    r=[psO.k, ("C.a01",)], w=[("C.oc", sl.start)])
                            else:
                                p.op("vector", lambda v, sl=sl, h=h, psO=psO: v.scalar_tensor_tensor(
                                    attn[h][:, sl], psO[:, 0:128], a01[:, 1:2], ocomb[:, sl],
                                    ALU.mult, ALU.add),
                                    r=[psO.k, ("C.a01",), ("C.oc", sl.start)], w=[("C.attn", h, sl.start)])
                            ek2 = ek2_[cp]
                            KEK2 = ("C.ek2", cp)
                            p.op("scalar", lambda s, Gt=Gt, ek2=ek2: s.activation(
                                ek2[:], Gt[:], AF.Exp, scale=-1.0, bias=Gt[:, 127:128]),
                                r=[KG], w=[KEK2])
                            p.op("gpsimd", lambda g, par=par, sl=sl, kd=kd, ek2=ek2: g.tensor_mul(
                                kd[:], kh[par][:, sl], ek2[:]),
                                r=[("C.KF", par), KEK2], w=[KKD])
                            p.op("tensor", lambda t, kd=kd: t.transpose(psT2[:, 0:128], kd[:], ident[:]),
                                 r=[KKD, ("ident",)], w=[psT2.k])
                            p.op("vector", lambda v, kdtm=kdtm: v.tensor_copy(kdtm[:], psT2[:, 0:128]),
                                 r=[psT2.k], w=[KKDT])
                            p.op("tensor", lambda t, sl=sl, kdtm=kdtm, psSD=psSD: t.matmul(
                                psSD[:, 0:128], kdtm[:], vtm[:, sl], start=True, stop=True),
                                r=[KKDT, ("C.vtm", sl.start)], w=[psSD.k])
                            p.op("vector", lambda v, inst=inst, ccol=ccol, psSD=psSD: v.scalar_tensor_tensor(
                                st_[inst][:], st_[inst][:], ccol[:, 2:3], psSD[:, 0:128], ALU.mult, ALU.add),
                                r=[stk, ("C.cc", cp, 2), psSD.k], w=[stk])


                wob = [sbp(f"C3w{i}", [128, TB], BF16) for i in range(NC)]
                rf = [sbp(f"C3r{i}", [128, TB], F32) for i in range(2)]
                ev3 = [sbp(f"C3e{i}", [128, TB], F32) for i in range(2)]
                for tb in (range(NTB) if 3 in c_sub else []):
                    for og in range(4):
                        par = og % 2
                        for c in range(NC):
                            p.op("gpsimd", lambda e, c=c, og=og: e.dma_start(
                                out=wob[c][:], in_=woT[128 * c:128 * (c + 1), TB * og:TB * (og + 1)]),
                                w=[("C3.wo", c)], dma=True)
                        for j in range(4):
                            m = 4 * og + j
                            pp = (psS_[0], psO_[0])[j % 2]
                            def fm3(t, pp=pp, par=par, j=j, tb=tb):
                                for c in range(NC):
                                    ins = t.matmul(pp[:, 0:TB], wob[c][:, 128 * j:128 * (j + 1)],
                                                   attn[c][:, TB * tb:TB * (tb + 1)],
                                                   start=(c == 0), stop=(c == NC - 1))
                                return ins
                            p.op("tensor", fm3,
                                 r=[("C3.wo", c) for c in range(NC)]
                                   + [("C.attn", c, 128 * q) for c in range(NC) for q in range(4 * tb, 4 * tb + 4)],
                                 w=[pp.k])
                            jp = j % 2
                            p.op("sync", lambda e, m=m, tb=tb, jp=jp: e.dma_start(
                                out=rf[jp][:], in_=R2[128 * m:128 * (m + 1), TB * tb:TB * (tb + 1)]),
                                r=[("RES_B", m, tb)], w=[("C3.rf", jp)], dma=True)
                            p.op("vector", lambda v, pp=pp, jp=jp: v.scalar_tensor_tensor(
                                ev3[jp][:], pp[:, 0:TB], 1.0, rf[jp][:], ALU.mult, ALU.add),
                                r=[pp.k, ("C3.rf", jp)], w=[("C3.ev", jp)])
                            p.op("scalar", lambda e, m=m, tb=tb, jp=jp: e.dma_start(
                                out=RC[128 * m:128 * (m + 1), TB * tb:TB * (tb + 1)], in_=ev3[jp][:]),
                                r=[("C3.ev", jp)], w=[("RES_C", m, tb)], dma=True)

        if "A" in phases:
            phase_A01(res_out["A"])
            phase_A2()
            phase_A3()
            phase_A4(res_out["A"])
        if "B" in phases or "D" in phases:
            drain_v_h0()
        if "B" in phases:
            ffn_phase("B", R1, lambda c, tb: ("RES_A", c, tb), res_out["B"], "RES_B")
        if "C" in phases:
            phase_C(res_out["C"])
        if "D" in phases:
            ffn_phase("Dp", R3, lambda c, tb: ("RES_C", c, tb), res_out["D"], "RES_D")

        lk = res_key[last_phase]
        final_keys = [(lk, m, tb) for m in range(NC) for tb in range(NTB)]
        final_keys = [k for k in final_keys if k in p.lastw]
        p.barrier()
        p.op("vector", lambda v: v.nop(), r=final_keys, w=[])

        mk, counts = p.finalize(sems)
        block = st.enter_context(nc.Block())
        block.sync(mk("sync"))
        block.tensor(mk("tensor"))
        block.vector(mk("vector"))
        block.scalar(mk("scalar"))
        block.gpsimd(mk("gpsimd"))
    return nc, counts


def host_inputs(inputs, core):
    d = {}
    d["xT"] = np.ascontiguousarray(inputs["L0_attn_a_x"][core].T)
    d["w_in"] = inputs["L0_attn_a_W_in"]
    cw = inputs["L0_attn_a_conv_w"]
    d["cw_h"] = np.ascontiguousarray(cw.transpose(2, 1, 0).reshape(4, D, NC, 128).transpose(2, 0, 1, 3))
    d["conv_b"] = np.ascontiguousarray(inputs["L0_attn_a_conv_b"].reshape(NC, 128).T)
    d["w_x"] = inputs["L0_attn_a_W_x"]
    d["w_dt"] = inputs["L0_attn_a_W_dt"]
    d["b_dt"] = np.ascontiguousarray(inputs["L0_attn_a_b_dt"].reshape(NC, 128).T)
    d["a_log"] = np.ascontiguousarray(inputs["L0_attn_a_A_log"].reshape(NC, 128, SSN).transpose(1, 0, 2).reshape(128, NC * SSN))
    d["d_vec"] = np.ascontiguousarray(inputs["L0_attn_a_D"].reshape(NC, 128).T)
    d["w_out"] = inputs["L0_attn_a_W_out"]
    d["wqT"] = np.ascontiguousarray(inputs["L0_attn_b_Wq"].T)
    d["wkT"] = np.ascontiguousarray(inputs["L0_attn_b_Wk"].T)
    d["wvT"] = np.ascontiguousarray(inputs["L0_attn_b_Wv"].T)
    d["woT"] = np.ascontiguousarray(inputs["L0_attn_b_Wo"].T)
    d["wg1T"] = np.ascontiguousarray(inputs["L0_attn_b_Wg1"].T)
    d["wg2T"] = np.ascontiguousarray(inputs["L0_attn_b_Wg2"].T)
    d["bg1"] = np.ascontiguousarray(inputs["L0_attn_b_bg1"].reshape(NC, 128).T)
    d["bg2"] = np.ascontiguousarray(inputs["L0_attn_b_bg2"].reshape(NC, 128).T)
    d["alpha"] = np.ascontiguousarray(inputs["L0_attn_b_alpha_list"].reshape(1, 2))
    d["lut"] = np.ascontiguousarray(inputs["L0_ffn_lut"].reshape(1, 16))
    d["gidxT"] = np.ascontiguousarray(inputs["L0_ffn_gate_idx"].T)
    d["uidxT"] = np.ascontiguousarray(inputs["L0_ffn_up_idx"].T)
    d["didxT"] = np.ascontiguousarray(inputs["L0_ffn_down_idx"].T)
    return d


_PHASES = ("A", "B", "C", "D")


def kernel(**inputs):
    nc, _ = build_nc(_PHASES)
    in_maps = [host_inputs(inputs, i) for i in range(8)]
    res = run_bass_kernel_spmd(nc, in_maps, core_ids=list(range(8)))
    out = np.stack([np.ascontiguousarray(r["outT"].T) for r in res.results])
    return out.astype(np.float32)


# version 59
# speedup vs baseline: 1.3156x; 1.0521x over previous
import contextlib
import numpy as np
import concourse.bass as bass
import concourse.mybir as mybir
from concourse.bass_utils import run_bass_kernel_spmd

F32 = mybir.dt.float32
BF16 = mybir.dt.bfloat16
I32 = mybir.dt.int32
AF = mybir.ActivationFunctionType
ALU = mybir.AluOpType

D = 2048
T = 2048
NC = D // 128
TB = 512
NTB = T // TB
SSN = 16
INTER = 5504
NO = 43

ENGS = ("sync", "tensor", "vector", "scalar", "gpsimd")
DMA_ENGS = ("sync", "scalar", "gpsimd", "vector")


class Prog:

    NSLOT = 8

    def __init__(self):
        self.ops = []
        self.lastw = {}
        self.readers = {}
        self.last_per_eng = {}
        self.last_dma_slot = {}
        self.dma_ord = {}
        self.bar = None

    def op(self, eng, fn, r=(), w=(), dma=False, qchain=False):
        ps_r = [k for k in r if isinstance(k, tuple) and len(k) > 0 and k[0] == "PS"]
        if ps_r:
            r = [k for k in r if k not in ps_r]
            w = list(w) + ps_r
        deps = set()
        for k in r:
            pr = self.lastw.get(k)
            if pr is not None:
                deps.add(pr)
        for k in w:
            pr = self.lastw.get(k)
            if pr is not None:
                deps.add(pr)
            deps.update(self.readers.get(k, ()))
        if self.bar is not None:
            deps.add(self.bar)
        i = len(self.ops)
        deps.discard(i)
        slot = None
        if dma:
            o = self.dma_ord.get(eng, 0)
            self.dma_ord[eng] = o + 1
            slot = o % self.NSLOT
            self.last_dma_slot[(eng, slot)] = i
        else:
            self.last_per_eng[eng] = i
        self.ops.append([eng, fn, dma, deps, False, slot, qchain])
        for k in r:
            self.readers.setdefault(k, set()).add(i)
        for k in w:
            self.lastw[k] = i
            self.readers[k] = set()
        return i

    def barrier(self):
        deps = set(self.last_per_eng.values()) | set(self.last_dma_slot.values())
        i = len(self.ops)
        self.ops.append(["vector", lambda v: v.nop(), False, deps, False, None, False])
        self.last_per_eng["vector"] = i
        self.bar = i

    def finalize(self, sems):
        ops = self.ops
        for o in ops:
            for d in o[3]:
                ops[d][4] = True
        for o in ops:
            if o[2]:
                o[4] = True
        cnt = {}
        val = [None] * len(ops)
        for i, (eng, fn, dma, deps, need, slot, qc) in enumerate(ops):
            if need:
                key = ("d", eng, slot) if dma else ("c", eng)
                cnt[key] = cnt.get(key, 0) + (16 if dma else 1)
                val[i] = (key, cnt[key])
        streams = {e: [] for e in ENGS}
        for i, (eng, fn, dma, deps, need, slot, qc) in enumerate(ops):
            waits = {}
            for d in deps:
                if eng == "tensor" and not dma and ops[d][0] == "tensor" and not ops[d][2]:
                    continue
                assert val[d] is not None
                k, v = val[d]
                if waits.get(k, -1) < v:
                    waits[k] = v
            streams[eng].append((i, fn, sorted(waits.items()), val[i]))
        counts = {e: (len(streams[e]), sum(len(w) for _, _, w, _ in streams[e]))
                  for e in ENGS}

        def mk(eng_name):
            def emit(eng):
                known = {}
                for i, fn, waits, v in streams[eng_name]:
                    for k, req in waits:
                        if known.get(k, -1) < req:
                            eng.wait_ge(sems[k], req)
                            known[k] = req
                    inst = fn(eng)
                    if v is not None:
                        assert inst is not None, f"op {i} on {eng_name} needs inc but emitted nothing"
                        inst.then_inc(sems[v[0]], 16 if v[0][0] == "d" else 1)
            return emit
        return mk, counts


class PS:
    def __init__(self, stack, nc, name, dt=F32):
        n = 512 if dt == F32 else 1024
        self.t = stack.enter_context(nc.psum_tensor(name, [128, n], dt))
        self.k = ("PS", name)

    def __getitem__(self, sl):
        return self.t[sl]


def build_nc(phases=("A", "B", "C", "D"), dbg=False, c_sub=(1, 2, 3)):
    nc = bass.Bass("TRN2", target_bir_lowering=False)
    p = Prog()

    def din(name, shape, dt=F32):
        return nc.dram_tensor(name, list(shape), dt, kind="ExternalInput").ap()

    xT = din("xT", [D, T])
    w_in = din("w_in", [D, 2 * D])
    cw_h = din("cw_h", [NC, 4, D, 128])
    conv_b = din("conv_b", [128, NC])
    w_x = din("w_x", [D, 160])
    w_dt = din("w_dt", [128, D])
    b_dt = din("b_dt", [128, NC])
    a_log = din("a_log", [128, NC * SSN])
    d_vec = din("d_vec", [128, NC])
    w_out = din("w_out", [D, D])
    wqT = din("wqT", [D, D])
    wkT = din("wkT", [D, D])
    wvT = din("wvT", [D, D])
    woT = din("woT", [D, D])
    wg1T = din("wg1T", [D, D])
    wg2T = din("wg2T", [D, D])
    bg1 = din("bg1", [128, NC])
    bg2 = din("bg2", [128, NC])
    alpha = din("alpha", [1, 2])
    lut = din("lut", [1, 16])
    gidxT = din("gidxT", [D, INTER], I32)
    uidxT = din("uidxT", [D, INTER], I32)
    didxT = din("didxT", [INTER, D], I32)

    outT = nc.dram_tensor("outT", [D, T], F32, kind="ExternalOutput").ap()

    def scr(name, shape, dt, addr_space="Local"):
        kind = "ExternalOutput" if dbg else "Internal"
        return nc.dram_tensor(name, list(shape), dt, kind=kind, addr_space=addr_space).ap()

    XC = scr("XC", [D, 3 + T], BF16)
    if dbg:
        DBG_DL = scr("DBG_DL", [D, TB], BF16)
        DBG_DU = scr("DBG_DU", [D, TB], BF16)
        DBG_B = scr("DBG_B", [128, TB], BF16)
        DBG_C = scr("DBG_C", [128, TB], F32)
        DBG_H = scr("DBG_H", [128, NC * TB], F32)
        DBG_Y = scr("DBG_Y", [D, TB], F32)
        DBG_U = scr("DBG_U", [D, TB], BF16)
        DBG_DP = scr("DBG_DP", [128, TB], BF16)
        DBG_BC = scr("DBG_BC", [16, TB], BF16)
        DBG_CC = scr("DBG_CC", [16, TB], BF16)
        DBG_X2 = scr("DBG_X2", [D, TB], BF16)
    RR = scr("RR", [D, T], BF16)
    XS = scr("XS", [D, T], BF16)
    OIN = scr("OIN", [D, T], BF16)
    R1 = scr("R1", [D, T], F32)
    R2 = scr("R2", [D, T], F32)
    R3 = scr("R3", [D, T], F32)
    WG = scr("WGd", [D, INTER], BF16)
    WU = scr("WUd", [D, INTER], BF16)
    WD = scr("WDd", [INTER, D], BF16)
    HB = scr("HBd", [INTER, T], BF16)
    QF = scr("QFd", [D, T], BF16)
    KF = scr("KFd", [D, T], BF16)
    VF = scr("VFd", [D, T], BF16)
    G1 = scr("G1d", [D, T], BF16)
    G2 = scr("G2d", [D, T], BF16)

    last_phase = phases[-1]
    res_out = {"A": R1, "B": R2, "C": R3, "D": outT}
    res_out[last_phase] = outT
    res_key = {"A": "RES_A", "B": "RES_B", "C": "RES_C", "D": "RES_D"}

    st = contextlib.ExitStack()
    with st:
        sems = {}
        for e in ENGS:
            sems[("c", e)] = st.enter_context(nc.semaphore(f"c_{e}"))
        for e in DMA_ENGS:
            for s_ in range(Prog.NSLOT):
                sems[("d", e, s_)] = st.enter_context(nc.semaphore(f"d_{e}{s_}"))

        sb = lambda name, shape, dt: st.enter_context(nc.sbuf_tensor(name, list(shape), dt))
        ident = sb("ident", [128, 128], BF16)
        ones_k1_f = sb("ones_k1_f", [1, 128], F32)
        ones_m1_bf = sb("ones_m1_bf", [128, 1], BF16)
        zeros512 = sb("zeros512", [128, 512], BF16)
        sel = sb("sel", [16, 16 * 128], BF16)
        onesrow = sb("onesrow", [1, 128], BF16)
        epscol = sb("epscol", [128, 1], F32)
        onecol = sb("onecol", [128, 1], F32)

        def g_init(g):
            g.memset(ident[:], 1.0)
            g.affine_select(out=ident[:], in_=ident[:], pattern=[[1, 128]],
                            compare_op=ALU.is_ge, fill=0.0, channel_multiplier=-1)
            return g.affine_select(out=ident[:], in_=ident[:], pattern=[[-1, 128]],
                                   compare_op=ALU.is_ge, fill=0.0, channel_multiplier=1)
        p.op("gpsimd", g_init, w=[("ident",)])
        p.op("gpsimd", lambda g: g.memset(ones_k1_f[:], 1.0), w=[("o2",)])
        p.op("gpsimd", lambda g: g.memset(ones_m1_bf[:], 1.0), w=[("o3",)])
        p.op("gpsimd", lambda g: g.memset(zeros512[:], 0.0), w=[("z5",)])
        p.op("gpsimd", lambda g: g.memset(sel[:], 0.0), w=[("sel",)])
        p.op("gpsimd", lambda g: g.memset(onesrow[:], 1.0), w=[("orow",)])
        p.op("gpsimd", lambda g: g.memset(epscol[:], 1e-5), w=[("eps",)])
        p.op("gpsimd", lambda g: g.memset(onecol[:], 1.0), w=[("onec",)])
        for n in range(16):
            p.op("gpsimd", lambda g, n=n: g.dma_start(out=sel[n:n + 1, 128 * n:128 * (n + 1)], in_=onesrow[:]),
                 r=[("orow",), ("sel",)], w=[("sel", n)], dma=True)

        def sel_n(n):
            return sel[0:16, 128 * n:128 * (n + 1)]

        mask_ut = sb("mask_ut", [128, 128], BF16)
        p.op("gpsimd", lambda g: g.memset(mask_ut[:], 1.0), w=[("mut",)])
        p.op("gpsimd", lambda g: g.affine_select(out=mask_ut[:], in_=mask_ut[:], pattern=[[1, 128]],
                                                 compare_op=ALU.is_ge, fill=0.0, channel_multiplier=-1),
             r=[("mut",)], w=[("mut",)])
        lutbc = sb("lutbc", [128, 16], F32)
        p.op("sync", lambda e: e.dma_start(out=lutbc[0:1, :], in_=lut), w=[("lut", 0)], dma=True)
        for dbl in range(7):
            lo, hi = 1 << dbl, 1 << (dbl + 1)
            p.op("sync", lambda e, lo=lo, hi=hi: e.dma_start(out=lutbc[lo:hi, :], in_=lutbc[0:lo, :]),
                 r=[("lut", dbl)], w=[("lut", dbl + 1)], dma=True)
        LUTK = ("lut", 7)

        def ln_phase(tag, src_dram, src_key, xn_pool, scope, psA, psB, psC, tick=None):
            sbp = lambda name, shape, dt: scope.enter_context(nc.sbuf_tensor(name, list(shape), dt))
            xf = [sbp(f"{tag}xf{i}", [128, TB], BF16) for i in range(4)]
            sq = [sbp(f"{tag}sq{i}", [128, TB], BF16) for i in range(2)]
            rows = [sbp(f"{tag}rw{i}", [1, TB], F32) for i in range(6)]
            bca = [sbp(f"{tag}bc{i}", [128, TB], F32) for i in range(2)]
            K = lambda *a: (tag,) + a
            for tb in range(NTB):
                for c in range(NC):
                    p.op("gpsimd", lambda e, c=c, tb=tb: e.dma_start(
                        out=xf[c % 4][:], in_=src_dram[128 * c:128 * (c + 1), TB * tb:TB * (tb + 1)]),
                        r=[src_key(c, tb)], w=[K("xf", c % 4)], dma=True)
                    if tick:
                        tick(1)
                    p.op("scalar", lambda s, c=c: s.square(sq[c % 2][:], xf[c % 4][:]),
                         r=[K("xf", c % 4)], w=[K("sq", c % 2)])
                    p.op("tensor", lambda t, c=c: t.matmul(psA[0:1, :], ones_m1_bf[:], xf[c % 4][:],
                                                           start=(c == 0), stop=(c == NC - 1)),
                         r=[K("xf", c % 4), ("o3",)], w=[psA.k])
                    p.op("tensor", lambda t, c=c: t.matmul(psB[0:1, :], ones_m1_bf[:], sq[c % 2][:],
                                                           start=(c == 0), stop=(c == NC - 1)),
                         r=[K("sq", c % 2), ("o3",)], w=[psB.k])
                p.op("scalar", lambda s: s.mul(rows[0][:], psA[0:1, :], 1.0 / D),
                     r=[psA.k], w=[K("r0")])
                p.op("scalar", lambda s: s.mul(rows[1][:], psB[0:1, :], 1.0 / D),
                     r=[psB.k], w=[K("r1")])
                p.op("vector", lambda v: v.tensor_mul(rows[2][:], rows[0][:], rows[0][:]),
                     r=[K("r0")], w=[K("r2")])
                p.op("vector", lambda v: v.tensor_sub(rows[3][:], rows[1][:], rows[2][:]),
                     r=[K("r1"), K("r2")], w=[K("r3")])
                p.op("scalar", lambda s: s.activation(rows[4][:], rows[3][:], AF.Ln, bias=epscol[0:1, 0:1]),
                     r=[K("r3"), ("eps",)], w=[K("r4")])
                p.op("scalar", lambda s: s.activation(rows[4][:], rows[4][:], AF.Exp, scale=-0.5),
                     r=[K("r4")], w=[K("r4")])
                p.op("vector", lambda v: v.tensor_mul(rows[5][:], rows[0][:], rows[4][:]),
                     r=[K("r0"), K("r4")], w=[K("r5")])
                p.op("tensor", lambda t: t.matmul(psC[:, 0:TB], ones_k1_f[:], rows[4][:], start=True, stop=True),
                     r=[K("r4"), ("o2",)], w=[psC.k])
                p.op("vector", lambda v: v.tensor_copy(bca[0][:], psC[:, 0:TB]),
                     r=[psC.k], w=[K("bc0")])
                p.op("tensor", lambda t: t.matmul(psC[:, 0:TB], ones_k1_f[:], rows[5][:], start=True, stop=True),
                     r=[K("r5"), ("o2",)], w=[psC.k])
                p.op("vector", lambda v: v.tensor_copy(bca[1][:], psC[:, 0:TB]),
                     r=[psC.k], w=[K("bc1")])
                for c in range(NC):
                    p.op("gpsimd", lambda e, c=c, tb=tb: e.dma_start(
                        out=xf[c % 4][:], in_=src_dram[128 * c:128 * (c + 1), TB * tb:TB * (tb + 1)]),
                        r=[src_key(c, tb)], w=[K("xf", c % 4)], dma=True)
                    p.op("vector", lambda v, c=c, tb=tb: v.scalar_tensor_tensor(
                        xn_pool[c][:, TB * tb:TB * (tb + 1)],
                        xf[c % 4][:], 1.0, bca[0][:], ALU.mult, ALU.mult),
                        r=[K("xf", c % 4), K("bc0")], w=[K("xnt", c)])
                    p.op("vector", lambda v, c=c, tb=tb: v.tensor_sub(
                        xn_pool[c][:, TB * tb:TB * (tb + 1)],
                        xn_pool[c][:, TB * tb:TB * (tb + 1)], bca[1][:]),
                        r=[K("xnt", c), K("bc1")], w=[K("xn", c, tb)])
            return lambda c, tb: K("xn", c, tb)

        def phase_A01(RA):
            with contextlib.ExitStack() as ph:
                sbp = lambda name, shape, dt: ph.enter_context(nc.sbuf_tensor(name, list(shape), dt))
                ps0 = PS(ph, nc, "pA0"); ps1 = PS(ph, nc, "pA1"); ps2 = PS(ph, nc, "pA2")
                xn = [sbp(f"Axn{i}", [128, T], BF16) for i in range(NC)]
                xnk = ln_phase("A0", xT, lambda c, tb: ("xTd", c, tb), xn, ph, ps0, ps1, ps2,
                               tick=lambda n: tick_v(1))
                wh = [sbp(f"Awh{i}", [128, D // 2], BF16) for i in range(NC)]
                ev = [sbp(f"Aev{i}", [128, TB], BF16) for i in range(2)]
                for hf in range(2):
                  for mh in range(2):
                    for c in range(NC):
                        p.op("gpsimd", lambda e, c=c, hf=hf, mh=mh: e.dma_start(
                            out=wh[c][:], in_=w_in[128 * c:128 * (c + 1),
                                                   D * hf + 1024 * mh:D * hf + 1024 * (mh + 1)]),
                            w=[("A.wh", c)], dma=True)
                    for m in range(8 * mh, 8 * mh + 8):
                        for tb in range(NTB):
                            pp = (ps0, ps1)[(m * NTB + tb) % 2]
                            def fm(t, m=m, tb=tb, pp=pp, mh=mh):
                                for c in range(NC):
                                    ins = t.matmul(pp[:, 0:TB],
                                                   wh[c][:, 128 * (m - 8 * mh):128 * (m - 8 * mh + 1)],
                                                   xn[c][:, TB * tb:TB * (tb + 1)],
                                                   start=(c == 0), stop=(c == NC - 1))
                                return ins
                            p.op("tensor", fm,
                                 r=[("A.wh", c) for c in range(NC)] + [xnk(c, tb) for c in range(NC)],
                                 w=[pp.k])
                            eb = ev[(m * NTB + tb) % 2]
                            ebk = ("A.ev", (m * NTB + tb) % 2)
                            p.op("scalar", lambda s, pp=pp, eb=eb: s.copy(eb[:], pp[:, 0:TB]),
                                 r=[pp.k], w=[ebk])
                            if hf == 0:
                                p.op("scalar", lambda e, m=m, tb=tb, eb=eb: e.dma_start(
                                    out=XC[128 * m:128 * (m + 1), 3 + TB * tb:3 + TB * (tb + 1)], in_=eb[:]),
                                    r=[ebk], w=[("XC", m, tb)], dma=True)
                            else:
                                p.op("scalar", lambda e, m=m, tb=tb, eb=eb: e.dma_start(
                                    out=RR[128 * m:128 * (m + 1), TB * tb:TB * (tb + 1)], in_=eb[:]),
                                    r=[ebk], w=[("RR", m, tb)], dma=True)
                            tick_v(4)
                zpad = sbp("Azpad", [128, 3], BF16)
                p.op("gpsimd", lambda g: g.memset(zpad[:], 0.0), w=[("A.zpad",)])
                for m in range(NC):
                    p.op("scalar", lambda e, m=m: e.dma_start(out=XC[128 * m:128 * (m + 1), 0:3], in_=zpad[:]),
                         r=[("A.zpad",)], w=[("XC", m, "pad")], dma=True)

        def phase_A2():
            with contextlib.ExitStack() as ph:
                p.barrier()
                sbp = lambda name, shape, dt: ph.enter_context(nc.sbuf_tensor(name, list(shape), dt))
                ps0 = PS(ph, nc, "pA20"); ps1 = PS(ph, nc, "pA21")
                xc = [sbp(f"A2xc{i}", [128, 3 + T], BF16) for i in range(NC)]
                cwb = [sbp(f"A2cw{i}", [128, D], BF16) for i in range(8)]
                cb16 = sbp("A2cb", [128, 16], F32)
                ev = [sbp(f"A2ev{i}", [128, TB], BF16) for i in range(2)]
                p.op("sync", lambda e: e.dma_start(out=cb16[:], in_=conv_b),
                     w=[("A.cb16",)], dma=True)
                for c in range(NC):
                    p.op("sync", lambda e, c=c: e.dma_start(out=xc[c][:], in_=XC[128 * c:128 * (c + 1), :]),
                         r=[("XC", c, "pad")] + [("XC", c, tb) for tb in range(NTB)],
                         w=[("A.xc", c)], dma=True)
                for m in range(NC):
                    par = m % 2
                    for k in range(4):
                        p.op("gpsimd", lambda e, m=m, k=k, par=par: e.dma_start(
                            out=cwb[4 * par + k][:].rearrange("p (c o) -> p c o", c=NC),
                            in_=cw_h[m, k].rearrange("(c p) o -> p c o", p=128)),
                            w=[("A.cwb", 4 * par + k)], dma=True)
                    for tb in range(NTB):
                        pp = (ps0, ps1)[tb % 2]
                        def fm(t, m=m, tb=tb, pp=pp, par=par):
                            for k in range(4):
                                for c in range(NC):
                                    ins = t.matmul(pp[:, 0:TB], cwb[4 * par + k][:, 128 * c:128 * (c + 1)],
                                                   xc[c][:, TB * tb + k:TB * tb + k + TB],
                                                   start=(k == 0 and c == 0), stop=(k == 3 and c == NC - 1))
                            return ins
                        p.op("tensor", fm,
                             r=[("A.cwb", 4 * par + k) for k in range(4)] + [("A.xc", c) for c in range(NC)],
                             w=[pp.k])
                        eb = ev[tb % 2]
                        ebk = ("A.ev2", tb % 2)
                        p.op("vector", lambda v, pp=pp, eb=eb, m=m: v.tensor_scalar(
                            eb[:], pp[:, 0:TB], cb16[:, m:m + 1], None, ALU.add),
                            r=[pp.k, ("A.cb16",)], w=[ebk])
                        p.op("scalar", lambda e, m=m, tb=tb, eb=eb: e.dma_start(
                            out=XS[128 * m:128 * (m + 1), TB * tb:TB * (tb + 1)], in_=eb[:]),
                            r=[ebk], w=[("XS", m, tb)], dma=True)
                        tick_v(10)

        def phase_A3():
            with contextlib.ExitStack() as ph:
                p.barrier()
                sbp = lambda name, shape, dt: ph.enter_context(nc.sbuf_tensor(name, list(shape), dt))
                ps0 = PS(ph, nc, "pA30"); ps1 = PS(ph, nc, "pA31")
                ps2 = PS(ph, nc, "pA32"); ps3 = PS(ph, nc, "pA33")
                wxr = [sbp(f"A3wx{i}", [128, 160], BF16) for i in range(NC)]
                wdtr = sbp("A3wdt", [128, D], BF16)
                nA = sbp("A3nA", [128, 256], F32)
                bdt16 = sbp("A3bdt", [128, 16], F32)
                dv16 = sbp("A3dv", [128, 16], F32)
                xs = [sbp(f"A3xs{i}", [128, TB], BF16) for i in range(2)]
                u = [sbp(f"A3u{i}", [128, TB], BF16) for i in range(8)]
                dpw = sbp("A3dpw", [128, T], BF16)
                Bw = sbp("A3Bw", [16, T], BF16)
                Cw = sbp("A3Cw", [16, T], BF16)
                uw = [sbp(f"A3uw{i}", [128, T], BF16) for i in range(2)]
                saw = sbp("A3saw", [128, T], BF16)
                sbw = sbp("A3sbw", [128, T], BF16)
                dlt = [sbp(f"A3dl{i}", [128, T], BF16) for i in range(2)]
                duw = [sbp(f"A3du{i}", [128, T], BF16) for i in range(2)]
                dAw = [sbp(f"A3dA{i}", [128, T], BF16) for i in range(2)]
                dBw = [sbp(f"A3dB{i}", [128, T], BF16) for i in range(2)]
                Hw = [sbp(f"A3H{i}", [128, T], BF16) for i in range(2)]
                tmpw = [sbp(f"A3tp{i}", [128, T], BF16) for i in range(4)]
                Bbw = [sbp(f"A3Bb{i}", [128, T], BF16) for i in range(2)]
                Cbw = [sbp(f"A3Cb{i}", [128, T], BF16) for i in range(2)]
                yaw = [sbp(f"A3ya{i}", [128, T], F32) for i in range(2)]
                rrw = [sbp(f"A3rr{i}", [128, T], BF16) for i in range(2)]
                oinw = [sbp(f"A3oi{i}", [128, T], BF16) for i in range(2)]

                for c in range(NC):
                    p.op("gpsimd", lambda e, c=c: e.dma_start(out=wxr[c][:], in_=w_x[128 * c:128 * (c + 1), :]),
                         w=[("A.wx", c)], dma=True)
                p.op("gpsimd", lambda e: e.dma_start(out=wdtr[:], in_=w_dt), w=[("A.wdt",)], dma=True)
                p.op("sync", lambda e: e.dma_start(out=nA[:], in_=a_log),
                     w=[("A.nA",)], dma=True)
                p.op("scalar", lambda s: s.activation(nA[:], nA[:], AF.Exp), r=[("A.nA",)], w=[("A.nA",)])
                p.op("vector", lambda v: v.tensor_scalar(nA[:], nA[:], -1.0, None, ALU.mult),
                     r=[("A.nA",)], w=[("A.nA",)])
                p.op("sync", lambda e: e.dma_start(out=bdt16[:], in_=b_dt),
                     w=[("A.bdt",)], dma=True)
                p.op("sync", lambda e: e.dma_start(out=dv16[:], in_=d_vec),
                     w=[("A.dv",)], dma=True)

                for tb in range(NTB):
                    for chf in range(2):
                        for c8 in range(8):
                            c = 8 * chf + c8
                            p.op("sync", lambda e, c=c, tb=tb: e.dma_start(
                                out=xs[c % 2][:], in_=XS[128 * c:128 * (c + 1), TB * tb:TB * (tb + 1)]),
                                r=[("XS", c, tb)], w=[("A.xs", c % 2)], dma=True)
                            p.op("scalar", lambda s, c=c, c8=c8: s.activation(u[c8][:], xs[c % 2][:], AF.Silu),
                                 r=[("A.xs", c % 2)], w=[("A.u", c8)])
                        uks = [("A.u", c8) for c8 in range(8)]
                        wxk = [("A.wx", 8 * chf + c8) for c8 in range(8)]

                        def f_dp(t, chf=chf):
                            for c8 in range(8):
                                ins = t.matmul(ps0[:, 0:TB], wxr[8 * chf + c8][:, 0:128], u[c8][:],
                                               start=(chf == 0 and c8 == 0), stop=(chf == 1 and c8 == 7))
                            return ins
                        p.op("tensor", f_dp, r=wxk + uks, w=[ps0.k])

                        def f_B(t, chf=chf):
                            for c8 in range(8):
                                ins = t.matmul(ps1[0:16, 0:TB], wxr[8 * chf + c8][:, 128:144], u[c8][:],
                                               start=(chf == 0 and c8 == 0), stop=(chf == 1 and c8 == 7))
                            return ins
                        p.op("tensor", f_B, r=wxk + uks, w=[ps1.k])

                        def f_C(t, chf=chf):
                            for c8 in range(8):
                                ins = t.matmul(ps2[0:16, 0:TB], wxr[8 * chf + c8][:, 144:160], u[c8][:],
                                               start=(chf == 0 and c8 == 0), stop=(chf == 1 and c8 == 7))
                            return ins
                        p.op("tensor", f_C, r=wxk + uks, w=[ps2.k])
                    p.op("scalar", lambda s, tb=tb: s.copy(dpw[:, TB * tb:TB * (tb + 1)], ps0[:, 0:TB]),
                         r=[ps0.k], w=[("A.dpw", tb)])
                    p.op("scalar", lambda s, tb=tb: s.copy(Bw[:, TB * tb:TB * (tb + 1)], ps1[0:16, 0:TB]),
                         r=[ps1.k], w=[("A.Bw", tb)])
                    p.op("scalar", lambda s, tb=tb: s.copy(Cw[:, TB * tb:TB * (tb + 1)], ps2[0:16, 0:TB]),
                         r=[ps2.k], w=[("A.Cw", tb)])
                dpk = [("A.dpw", tb) for tb in range(NTB)]
                Bwk = [("A.Bw", tb) for tb in range(NTB)]
                Cwk = [("A.Cw", tb) for tb in range(NTB)]

                for cg in range(8):
                    for ci in range(2):
                        c = 2 * cg + ci
                        kuw = ("A.uw", c % 2)
                        p.op("sync", lambda e, c=c: e.dma_start(
                            out=uw[c % 2][:], in_=XS[128 * c:128 * (c + 1), :]),
                            r=[("XS", c, tb) for tb in range(NTB)], w=[kuw], dma=True)
                        p.op("scalar", lambda s, c=c: s.activation(uw[c % 2][:], uw[c % 2][:], AF.Silu),
                             r=[kuw], w=[kuw])
                        for tb in range(NTB):
                            pp = (ps0, ps3)[tb % 2]
                            p.op("tensor", lambda t, c=c, tb=tb, pp=pp: t.matmul(
                                pp[:, 0:TB], wdtr[:, 128 * c:128 * (c + 1)], dpw[:, TB * tb:TB * (tb + 1)],
                                start=True, stop=True),
                                r=[("A.wdt",)] + dpk, w=[pp.k])
                            p.op("scalar", lambda s, pp=pp, tb=tb, c=c: s.activation(
                                saw[:, TB * tb:TB * (tb + 1)], pp[:, 0:TB], AF.Abs, bias=bdt16[:, c:c + 1]),
                                r=[pp.k, ("A.bdt",)], w=[("A.saw", tb)])
                            p.op("scalar", lambda s, pp=pp, tb=tb, c=c: s.activation(
                                sbw[:, TB * tb:TB * (tb + 1)], pp[:, 0:TB], AF.Relu, bias=bdt16[:, c:c + 1]),
                                r=[pp.k, ("A.bdt",)], w=[("A.sbw", tb)])
                        sawk = [("A.saw", tb) for tb in range(NTB)]
                        sbwk = [("A.sbw", tb) for tb in range(NTB)]
                        p.op("scalar", lambda s: s.activation(saw[:], saw[:], AF.Exp, scale=-1.0),
                             r=sawk, w=[("A.sawf",)])
                        p.op("scalar", lambda s: s.activation(saw[:], saw[:], AF.Ln, bias=onecol[:, 0:1]),
                             r=[("A.sawf",), ("onec",)], w=[("A.sawf",)])
                        p.op("vector", lambda v, ci=ci: v.tensor_add(dlt[ci][:], saw[:], sbw[:]),
                             r=[("A.sawf",)] + sbwk, w=[("A.dl", ci)])
                        p.op("vector", lambda v, ci=ci: v.tensor_mul(duw[ci][:], dlt[ci][:], uw[ci][:]),
                             r=[("A.dl", ci), ("A.uw", ci)], w=[("A.du", ci)])
                    for n in range(16):
                        tick_v(1)
                        npar = n % 2
                        Bb, Cb = Bbw[npar], Cbw[npar]
                        KB, KC = ("A.Bbw", npar), ("A.Cbw", npar)
                        for tb in range(NTB):
                            pp = (ps1, ps2)[tb % 2]
                            p.op("tensor", lambda t, n=n, tb=tb, pp=pp: t.matmul(
                                pp[:, 0:TB], sel_n(n), Bw[:, TB * tb:TB * (tb + 1)], start=True, stop=True),
                                r=Bwk + [("sel", n)], w=[pp.k])
                            p.op("scalar", lambda s, tb=tb, pp=pp, Bb=Bb: s.copy(
                                Bb[:, TB * tb:TB * (tb + 1)], pp[:, 0:TB]), r=[pp.k], w=[(KB, tb)])
                            p.op("tensor", lambda t, n=n, tb=tb, pp=pp: t.matmul(
                                pp[:, 0:TB], sel_n(n), Cw[:, TB * tb:TB * (tb + 1)], start=True, stop=True),
                                r=Cwk + [("sel", n), (KB, tb)], w=[pp.k])
                            p.op("scalar", lambda s, tb=tb, pp=pp, Cb=Cb: s.copy(
                                Cb[:, TB * tb:TB * (tb + 1)], pp[:, 0:TB]), r=[pp.k], w=[(KC, tb)])
                        KBf = [(KB, tb) for tb in range(NTB)]
                        KCf = [(KC, tb) for tb in range(NTB)]
                        for ci in range(2):
                            c = 2 * cg + ci
                            rot = (2 * n + ci) % 2
                            p.op("scalar", lambda s, c=c, n=n, rot=rot, ci=ci: s.activation(
                                dAw[rot][:], dlt[ci][:], AF.Exp, scale=nA[:, 16 * c + n:16 * c + n + 1]),
                                r=[("A.dl", ci), ("A.nA",)], w=[("A.dAw", rot)])
                            p.op("vector", lambda v, ci=ci, rot=rot, Bb=Bb: v.tensor_mul(
                                dBw[rot][:], duw[ci][:], Bb[:]),
                                r=[("A.du", ci)] + KBf, w=[("A.dBw", rot)])
                            p.op("vector", lambda v, rot=rot: v.tensor_tensor_scan(
                                Hw[rot][:], dAw[rot][:], dBw[rot][:], 0.0, ALU.mult, ALU.add),
                                r=[("A.dAw", rot), ("A.dBw", rot)], w=[("A.Hw", rot)])
                            tpi = (2 * n + ci) % 4
                            p.op("vector", lambda v, rot=rot, tpi=tpi, Cb=Cb: v.tensor_mul(
                                tmpw[tpi][:], Hw[rot][:], Cb[:]),
                                r=[("A.Hw", rot)] + KCf, w=[("A.tpw", tpi)])
                            if n == 0:
                                p.op("gpsimd", lambda g, ci=ci, tpi=tpi: g.dma_start(
                                    out=yaw[ci][:], in_=tmpw[tpi][:], max_dma_last_dim=2048),
                                    r=[("A.tpw", tpi)], w=[("A.yaw", ci)], dma=True, qchain=True)
                            else:
                                p.op("gpsimd", lambda g, ci=ci, tpi=tpi: g.dma_start(
                                    out=yaw[ci][:], in_=tmpw[tpi][:], accum_op=ALU.add,
                                    max_dma_last_dim=2048),
                                    r=[("A.tpw", tpi), ("A.yaw", ci)], w=[("A.yaw", ci)], dma=True, qchain=True)
                    for ci in range(2):
                        c = 2 * cg + ci
                        p.op("sync", lambda e, c=c, ci=ci: e.dma_start(
                            out=rrw[ci][:], in_=RR[128 * c:128 * (c + 1), :]),
                            r=[("RR", c, tb) for tb in range(NTB)], w=[("A.rrw", ci)], dma=True)
                        p.op("scalar", lambda s, ci=ci: s.activation(rrw[ci][:], rrw[ci][:], AF.Silu),
                             r=[("A.rrw", ci)], w=[("A.rrw", ci)])
                        p.op("vector", lambda v, c=c, ci=ci: v.scalar_tensor_tensor(
                            oinw[ci][:], uw[ci][:], dv16[:, c:c + 1], yaw[ci][:], ALU.mult, ALU.add),
                            r=[("A.uw", ci), ("A.dv",), ("A.yaw", ci)], w=[("A.oiw", ci)])
                        p.op("vector", lambda v, ci=ci: v.tensor_mul(oinw[ci][:], oinw[ci][:], rrw[ci][:]),
                             r=[("A.oiw", ci), ("A.rrw", ci)], w=[("A.oiw", ci)])
                        p.op("scalar", lambda e, c=c, ci=ci: e.dma_start(
                            out=OIN[128 * c:128 * (c + 1), :], in_=oinw[ci][:]),
                            r=[("A.oiw", ci)], w=[("OIN", c, 0), ("OIN", c, 1), ("OIN", c, 2), ("OIN", c, 3)],
                            dma=True)

        def phase_A4(RA):
            with contextlib.ExitStack() as ph:
                p.barrier()
                sbp = lambda name, shape, dt: ph.enter_context(nc.sbuf_tensor(name, list(shape), dt))
                ps0 = PS(ph, nc, "pA40"); ps1 = PS(ph, nc, "pA41")
                oin = [sbp(f"A4o{i}", [128, TB], BF16) for i in range(NC)]
                xs = [sbp(f"A4x{i}", [128, TB], BF16) for i in range(NC)]
                wob = [sbp(f"A4w{i}", [128, TB], BF16) for i in range(2 * NC)]
                xf = [sbp(f"A4f{i}", [128, TB], F32) for i in range(2)]
                rof = [sbp(f"A4r{i}", [128, TB], F32) for i in range(2)]
                for tb in range(NTB):
                    for c in range(NC):
                        p.op("sync", lambda e, c=c, tb=tb: e.dma_start(
                            out=oin[c][:], in_=OIN[128 * c:128 * (c + 1), TB * tb:TB * (tb + 1)]),
                            r=[("OIN", c, tb)], w=[("A4.oin", c)], dma=True)
                        p.op("sync", lambda e, c=c, tb=tb: e.dma_start(
                            out=xs[c][:], in_=XS[128 * c:128 * (c + 1), TB * tb:TB * (tb + 1)]),
                            r=[("XS", c, tb)], w=[("A4.xs", c)], dma=True)
                    for og in range(4):
                        par = og % 2
                        for c in range(NC):
                            p.op("gpsimd", lambda e, c=c, og=og, par=par: e.dma_start(
                                out=wob[NC * par + c][:], in_=w_out[128 * c:128 * (c + 1), TB * og:TB * (og + 1)]),
                                w=[("A4.wo", NC * par + c)], dma=True)
                        for j in range(4):
                            m = 4 * og + j
                            pp = (ps0, ps1)[j % 2]
                            def fm(t, pp=pp, par=par, j=j):
                                for c in range(NC):
                                    ins = t.matmul(pp[:, 0:TB], wob[NC * par + c][:, 128 * j:128 * (j + 1)],
                                                   oin[c][:], start=(c == 0), stop=(c == NC - 1))
                                return ins
                            p.op("tensor", fm,
                                 r=[("A4.wo", NC * par + c) for c in range(NC)] + [("A4.oin", c) for c in range(NC)],
                                 w=[pp.k])
                            jp = j % 2
                            p.op("sync", lambda e, m=m, tb=tb, jp=jp: e.dma_start(
                                out=xf[jp][:], in_=xT[128 * m:128 * (m + 1), TB * tb:TB * (tb + 1)]),
                                w=[("A4.xf", jp)], dma=True)
                            p.op("vector", lambda v, pp=pp, m=m, jp=jp: v.scalar_tensor_tensor(
                                rof[jp][:], pp[:, 0:TB], 1.0, xs[m][:], ALU.mult, ALU.add),
                                r=[pp.k, ("A4.xs", m)], w=[("A4.ro", jp)])
                            p.op("vector", lambda v, jp=jp: v.tensor_add(rof[jp][:], rof[jp][:], xf[jp][:]),
                                 r=[("A4.ro", jp), ("A4.xf", jp)], w=[("A4.ro", jp)])
                            p.op("scalar", lambda e, m=m, tb=tb, jp=jp: e.dma_start(
                                out=RA[128 * m:128 * (m + 1), TB * tb:TB * (tb + 1)], in_=rof[jp][:]),
                                r=[("A4.ro", jp)], w=[(res_key["A"], m, tb)], dma=True)
                            tick_v(10)

        dc_iv = [sb(f"DCiv{i}", [128, INTER // 2], BF16) for i in range(2)]
        dc_pv = sb("DCpv", [128, INTER // 2], BF16)
        dc_av = [sb(f"DCav{i}", [128, INTER // 2], BF16) for i in range(2)]
        dc_ig = [sb(f"DCig{i}", [128, D], BF16) for i in range(2)]
        dc_pg = sb("DCpg", [128, D], BF16)
        dc_ag = [sb(f"DCag{i}", [128, D], BF16) for i in range(2)]

        def decode_gen(eng, ld_eng, st_eng, idxb, prodb, accb, blocks):
            def gen():
                hb = 0
                for (srcv, dstv, F, wkey) in blocks:
                    par = hb % 2
                    hb += 1
                    ib, ac = idxb[par], accb[par]
                    ibk = ("DC", eng, par, "ib")
                    ack = ("DC", eng, par, "ac")
                    prk = ("DC", eng, "pr")
                    p.op(ld_eng, lambda e, srcv=srcv, ib=ib, F=F: e.dma_start(
                        out=ib[:, 0:F], in_=srcv),
                        w=[ibk], dma=True)
                    yield
                    p.op(eng, lambda v, ib=ib, ac=ac, F=F: v.tensor_scalar(
                        ac[:, 0:F], ib[:, 0:F], 0.0, lutbc[:, 0:1],
                        ALU.is_equal, ALU.mult),
                        r=[ibk, LUTK], w=[ack])
                    yield
                    for j in range(1, 16):
                        p.op(eng, lambda v, ib=ib, F=F, j=j: v.tensor_scalar(
                            prodb[:, 0:F], ib[:, 0:F], float(j), lutbc[:, j:j + 1],
                            ALU.is_equal, ALU.mult),
                            r=[ibk, LUTK], w=[prk])
                        yield
                        p.op(eng, lambda v, ac=ac, F=F: v.tensor_add(
                            ac[:, 0:F], ac[:, 0:F], prodb[:, 0:F]),
                            r=[ack, prk], w=[ack])
                        yield
                    p.op(st_eng, lambda e, dstv=dstv, ac=ac, F=F: e.dma_start(
                        out=dstv, in_=ac[:, 0:F]),
                        r=[ack], w=[wkey], dma=True)
                    yield
            return gen()

        HB0 = INTER // 2
        blk_gu = []
        for h in range(2):
            lo, hi = HB0 * h, HB0 * (h + 1)
            for c in range(NC):
                blk_gu.append((gidxT[128 * c:128 * (c + 1), lo:hi],
                               WG[128 * c:128 * (c + 1), lo:hi], HB0, ("WG", c, h)))
                blk_gu.append((uidxT[128 * c:128 * (c + 1), lo:hi],
                               WU[128 * c:128 * (c + 1), lo:hi], HB0, ("WU", c, h)))
        blk_d = [(didxT[128 * o:128 * (o + 1), :], WD[128 * o:128 * (o + 1), :],
                  D, ("WD", o)) for o in range(NO)]
        gen_v = decode_gen("vector", "gpsimd", "scalar", dc_iv, dc_pv, dc_av, blk_gu)
        gen_g = decode_gen("vector", "gpsimd", "sync", dc_ig, dc_pg, dc_ag, blk_d)
        n_v_left = [len(blk_gu) * 33]

        def tick_v(n):
            for _ in range(n):
                if next(gen_v, "end") == "end":
                    if next(gen_g, "end") == "end":
                        return
                else:
                    n_v_left[0] -= 1
        def drain_v_h0(n=10 ** 9):
            while n_v_left[0] > len(blk_gu) // 2 * 33:
                tick_v(1)
        def tick_g(n):
            for _ in range(n):
                if next(gen_g, "end") == "end":
                    return

        def ffn_phase(tag, src_dram, src_keyf, dst_dram, dst_keyn):
            with contextlib.ExitStack() as ph:
                p.barrier()
                sbp = lambda name, shape, dt: ph.enter_context(nc.sbuf_tensor(name, list(shape), dt))
                psA = PS(ph, nc, tag + "p0"); psB = PS(ph, nc, tag + "p1")
                psC = PS(ph, nc, tag + "p2"); psD = PS(ph, nc, tag + "p3")
                xn = [sbp(f"{tag}xn{i}", [128, T], BF16) for i in range(NC)]
                xnk = ln_phase(tag + "0", src_dram, src_keyf, xn, ph, psA, psB, psC,
                               tick=lambda n: tick_v(2))
                wgb = [sbp(f"{tag}wg{i}", [128, TB], BF16) for i in range(2 * NC)]
                wub = [sbp(f"{tag}wu{i}", [128, TB], BF16) for i in range(2 * NC)]
                hg = [sbp(f"{tag}hg{i}", [128, TB], BF16) for i in range(2)]
                hb = [sbp(f"{tag}hb{i}", [128, TB], BF16) for i in range(2)]
                psrot = [psA, psB, psC, psD]
                for og in range(11):
                    if og == 5:
                        tick_v(10 ** 9)
                    par = og % 2
                    ow = 512 if og < 10 else INTER - 5120
                    nj = 4 if og < 10 else 3
                    hks = [0] if og < 5 else ([0, 1] if og == 5 else [1])
                    for c in range(NC):
                        p.op("sync", lambda e, c=c, og=og, par=par, ow=ow: e.dma_start(
                            out=wgb[NC * par + c][:, 0:ow], in_=WG[128 * c:128 * (c + 1), 512 * og:512 * og + ow]),
                            r=[("WG", c, h) for h in hks], w=[(tag, "wg", NC * par + c)], dma=True)
                        p.op("sync", lambda e, c=c, og=og, par=par, ow=ow: e.dma_start(
                            out=wub[NC * par + c][:, 0:ow], in_=WU[128 * c:128 * (c + 1), 512 * og:512 * og + ow]),
                            r=[("WU", c, h) for h in hks], w=[(tag, "wu", NC * par + c)], dma=True)
                    for j in range(nj):
                        ot = 4 * og + j
                        for tb in range(NTB):
                            pg = psrot[(2 * (ot * NTB + tb)) % 4]
                            pu = psrot[(2 * (ot * NTB + tb) + 1) % 4]
                            def fg(t, par=par, j=j, tb=tb, pg=pg):
                                for c in range(NC):
                                    ins = t.matmul(pg[:, 0:TB], wgb[NC * par + c][:, 128 * j:128 * (j + 1)],
                                                   xn[c][:, TB * tb:TB * (tb + 1)],
                                                   start=(c == 0), stop=(c == NC - 1))
                                return ins
                            p.op("tensor", fg,
                                 r=[(tag, "wg", NC * par + c) for c in range(NC)] + [xnk(c, tb) for c in range(NC)],
                                 w=[pg.k])
                            def fu(t, par=par, j=j, tb=tb, pu=pu):
                                for c in range(NC):
                                    ins = t.matmul(pu[:, 0:TB], wub[NC * par + c][:, 128 * j:128 * (j + 1)],
                                                   xn[c][:, TB * tb:TB * (tb + 1)],
                                                   start=(c == 0), stop=(c == NC - 1))
                                return ins
                            p.op("tensor", fu,
                                 r=[(tag, "wu", NC * par + c) for c in range(NC)] + [xnk(c, tb) for c in range(NC)],
                                 w=[pu.k])
                            hpar = (ot * NTB + tb) % 2
                            p.op("scalar", lambda s, hpar=hpar, pg=pg: s.activation(hg[hpar][:], pg[:, 0:TB], AF.Silu),
                                 r=[pg.k], w=[(tag, "hg", hpar)])
                            p.op("vector", lambda v, hpar=hpar, pu=pu: v.tensor_mul(hb[hpar][:], hg[hpar][:], pu[:, 0:TB]),
                                 r=[(tag, "hg", hpar), pu.k], w=[(tag, "hb", hpar)])
                            p.op("scalar", lambda e, ot=ot, tb=tb, hpar=hpar: e.dma_start(
                                out=HB[128 * ot:128 * (ot + 1), TB * tb:TB * (tb + 1)], in_=hb[hpar][:]),
                                r=[(tag, "hb", hpar)], w=[(tag, "HB", ot, tb)], dma=True)
                            tick_v(10)
            tick_g(10 ** 9)
            with contextlib.ExitStack() as ph:
                p.barrier()
                sbp = lambda name, shape, dt: ph.enter_context(nc.sbuf_tensor(name, list(shape), dt))
                pso = [PS(ph, nc, f"{tag}q{i}") for i in range(8)]
                hcol = [sbp(f"{tag}hc{i}", [128, 1024], BF16) for i in range(NO)]
                r1t = [sbp(f"{tag}r1{i}", [128, TB], F32) for i in range(4)]
                wdb = [sbp(f"{tag}wd{i}", [128, TB], BF16) for i in range(4)]
                ev = [sbp(f"{tag}ev{i}", [128, TB], F32) for i in range(2)]
                for th in range(2):
                    for o in range(NO):
                        p.op("sync", lambda e, o=o, th=th: e.dma_start(
                            out=hcol[o][:], in_=HB[128 * o:128 * (o + 1), 1024 * th:1024 * (th + 1)]),
                            r=[(tag, "HB", o, 2 * th), (tag, "HB", o, 2 * th + 1)],
                            w=[(tag, "hc", o)], dma=True)
                    for dmg in range(4):
                        for o in range(NO):
                            par = o % 4
                            eng = "gpsimd" if o % 2 == 0 else "scalar"
                            p.op(eng, lambda e, o=o, dmg=dmg, par=par: e.dma_start(
                                out=wdb[par][:], in_=WD[128 * o:128 * (o + 1), TB * dmg:TB * (dmg + 1)]),
                                r=[("WD", o)], w=[(tag, "wd", par)], dma=True)
                            def fd(t, o=o, par=par):
                                for dj in range(4):
                                    for t2 in range(2):
                                        ins = t.matmul(pso[2 * dj + t2][:, 0:TB],
                                                       wdb[par][:, 128 * dj:128 * (dj + 1)],
                                                       hcol[o][:, TB * t2:TB * (t2 + 1)],
                                                       start=(o == 0), stop=(o == NO - 1))
                                return ins
                            p.op("tensor", fd,
                                 r=[(tag, "wd", par), (tag, "hc", o)],
                                 w=[pso[i].k for i in range(8)])
                        for dj in range(4):
                            m = 4 * dmg + dj
                            for t2 in range(2):
                                epar = t2
                                tb = 2 * th + t2
                                rb = (2 * dj + t2) % 4
                                p.op("sync", lambda e, m=m, tb=tb, rb=rb: e.dma_start(
                                    out=r1t[rb][:], in_=src_dram[128 * m:128 * (m + 1), TB * tb:TB * (tb + 1)]),
                                    r=[src_keyf(m, tb)], w=[(tag, "r1", rb)], dma=True)
                                p.op("vector", lambda v, dj=dj, t2=t2, m=m, epar=epar, rb=rb: v.scalar_tensor_tensor(
                                    ev[epar][:], pso[2 * dj + t2][:, 0:TB], 1.0,
                                    r1t[rb][:], ALU.mult, ALU.add),
                                    r=[pso[2 * dj + t2].k, (tag, "r1", rb)], w=[(tag, "ev", epar)])
                                p.op("scalar", lambda e, m=m, tb=tb, epar=epar: e.dma_start(
                                    out=dst_dram[128 * m:128 * (m + 1), TB * tb:TB * (tb + 1)], in_=ev[epar][:]),
                                    r=[(tag, "ev", epar)], w=[(dst_keyn, m, tb)], dma=True)

        def phase_C(RC):
            with contextlib.ExitStack() as ph:
                p.barrier()
                sbp = lambda name, shape, dt: ph.enter_context(nc.sbuf_tensor(name, list(shape), dt))
                psA = PS(ph, nc, "pC0"); psB = PS(ph, nc, "pC1"); psC = PS(ph, nc, "pC2")
                xn = [sbp(f"Cxn{i}", [128, T], BF16) for i in range(NC)]
                xnk = ln_phase("C0", R2, lambda c, tb: ("RES_B", c, tb), xn, ph, psA, psB, psC)
                wpb = [sbp(f"Cwp{i}", [128, TB], BF16) for i in range(2 * NC)]
                evb = [sbp(f"Cev{i}", [128, TB], BF16) for i in range(2)]
                za = [sbp(f"Cza{i}", [128, TB], F32) for i in range(2)]
                zm = [sbp(f"Czm{i}", [128, TB], F32) for i in range(2)]
                bgt = [sbp(f"Cbg{i}", [128, NC], F32) for i in range(2)]
                p.op("sync", lambda e: e.dma_start(out=bgt[0][:], in_=bg1), w=[("C.bg", 0)], dma=True)
                p.op("sync", lambda e: e.dma_start(out=bgt[1][:], in_=bg2), w=[("C.bg", 1)], dma=True)
                projs = [(wqT, QF, "QF", None), (wkT, KF, "KF", None), (wvT, VF, "VF", None),
                         (wg1T, G1, "G1", 0), (wg2T, G2, "G2", 1)]
                if 1 not in c_sub:
                    projs = []
                elif 4 in c_sub:
                    projs = projs[:3]
                elif 5 in c_sub:
                    projs = projs[3:4]
                elif 6 in c_sub:
                    projs = projs[:4]
                elif 7 in c_sub:
                    projs = projs[3:5]
                for wsrc, dst, dkey, gate_i in projs:
                    for og in range(4):
                        par = og % 2
                        for c in range(NC):
                            p.op("gpsimd", lambda e, c=c, og=og, par=par, wsrc=wsrc: e.dma_start(
                                out=wpb[NC * par + c][:], in_=wsrc[128 * c:128 * (c + 1), TB * og:TB * (og + 1)]),
                                w=[("C.wp", NC * par + c)], dma=True)
                        for j in range(4):
                            it = 4 * og + j
                            for tb in range(NTB):
                                pp = (psA, psB)[(j * NTB + tb) % 2]
                                def fp_(t, par=par, j=j, tb=tb, pp=pp):
                                    for c in range(NC):
                                        ins = t.matmul(pp[:, 0:TB], wpb[NC * par + c][:, 128 * j:128 * (j + 1)],
                                                       xn[c][:, TB * tb:TB * (tb + 1)],
                                                       start=(c == 0), stop=(c == NC - 1))
                                    return ins
                                p.op("tensor", fp_,
                                     r=[("C.wp", NC * par + c) for c in range(NC)] + [xnk(c, tb) for c in range(NC)],
                                     w=[pp.k])
                                ep = (j * NTB + tb) % 2
                                if gate_i is None:
                                    p.op("scalar", lambda s, pp=pp, ep=ep: s.copy(evb[ep][:], pp[:, 0:TB]),
                                         r=[pp.k], w=[("C.ev", ep)])
                                elif 8 in c_sub:
                                    gi = gate_i
                                    p.op("vector", lambda v, pp=pp, ep=ep, it=it, gi=gi: v.tensor_scalar(
                                        evb[ep][:], pp[:, 0:TB], bgt[gi][:, it:it + 1], 0.0, ALU.add, ALU.min),
                                        r=[pp.k, ("C.bg", gi)], w=[("C.ev", ep)])
                                elif 9 in c_sub:
                                    gi = gate_i
                                    p.op("scalar", lambda s, pp=pp, ep=ep, it=it, gi=gi: s.activation(
                                        za[ep][:], pp[:, 0:TB], AF.Abs, bias=bgt[gi][:, it:it + 1]),
                                        r=[pp.k, ("C.bg", gi)], w=[("C.za", ep)])
                                    p.op("vector", lambda v, ep=ep: v.tensor_copy(evb[ep][:], za[ep][:]),
                                         r=[("C.za", ep)], w=[("C.ev", ep)])
                                else:
                                    gi = gate_i
                                    p.op("scalar", lambda s, pp=pp, ep=ep, it=it, gi=gi: s.activation(
                                        za[ep][:], pp[:, 0:TB], AF.Abs, bias=bgt[gi][:, it:it + 1]),
                                        r=[pp.k, ("C.bg", gi)], w=[("C.za", ep)])
                                    p.op("scalar", lambda s, ep=ep: s.activation(za[ep][:], za[ep][:], AF.Exp, scale=-1.0),
                                         r=[("C.za", ep)], w=[("C.za", ep)])
                                    p.op("scalar", lambda s, ep=ep: s.activation(za[ep][:], za[ep][:], AF.Ln, bias=onecol[:, 0:1]),
                                         r=[("C.za", ep), ("onec",)], w=[("C.za", ep)])
                                    p.op("vector", lambda v, pp=pp, ep=ep, it=it, gi=gi: v.tensor_scalar(
                                        zm[ep][:], pp[:, 0:TB], bgt[gi][:, it:it + 1], 0.0, ALU.add, ALU.min),
                                        r=[pp.k, ("C.bg", gi)], w=[("C.zm", ep)])
                                    p.op("vector", lambda v, ep=ep: v.tensor_sub(zm[ep][:], zm[ep][:], za[ep][:]),
                                         r=[("C.zm", ep), ("C.za", ep)], w=[("C.zm", ep)])
                                    p.op("vector", lambda v, ep=ep: v.tensor_scalar(
                                        evb[ep][:], zm[ep][:], 1.0 / 16.0, -50.0, ALU.mult, ALU.max),
                                        r=[("C.zm", ep)], w=[("C.ev", ep)])
                                p.op("scalar", lambda e, dst=dst, it=it, tb=tb, ep=ep: e.dma_start(
                                    out=dst[128 * it:128 * (it + 1), TB * tb:TB * (tb + 1)], in_=evb[ep][:]),
                                    r=[("C.ev", ep)], w=[(dkey, it, tb)], dma=True)

            with contextlib.ExitStack() as ph:
                p.barrier()
                sbp = lambda name, shape, dt: ph.enter_context(nc.sbuf_tensor(name, list(shape), dt))
                psS_ = [PS(ph, nc, f"pS{i}") for i in range(2)]
                psO_ = [PS(ph, nc, f"pO{i}") for i in range(2)]
                psSD_ = [PS(ph, nc, f"pSD{i}") for i in range(2)]
                psT1 = PS(ph, nc, "pT1", BF16); psT2 = PS(ph, nc, "pT2", BF16)
                psAl = psS_[0]
                qc = [sbp(f"Cq{i}", [128, T], BF16) for i in range(2)]
                kc = [sbp(f"Ck{i}", [128, T], BF16) for i in range(2)]
                vc = [sbp(f"Cv{i}", [128, T], BF16) for i in range(2)]
                g1c = [sbp(f"Cg1{i}", [128, T], BF16) for i in range(2)]
                g2c = [sbp(f"Cg2{i}", [128, T], BF16) for i in range(2)]
                vtm = sbp("Cvtm", [128, T], BF16)
                ocomb = sbp("Coc", [128, T], BF16)
                st_ = [sbp(f"Cst{i}", [128, T], F32) for i in range(2)]
                NRT = 8
                Gt_ = [sbp(f"CG{i}", [128, 128], F32) for i in range(NRT)]
                kdtm_ = [sbp(f"Ckdt{i}", [128, 128], BF16) for i in range(4)]
                Ssb_ = [sbp(f"CSsb{i}", [128, 128], BF16) for i in range(NRT)]
                ccol_ = [sbp(f"Ccc{i}", [128, 4], F32) for i in range(NRT)]
                stbf_ = [sbp(f"Cstb{i}", [128, 128], BF16) for i in range(NRT)]
                ob_ = [sbp(f"Cob{i}", [128, 128], BF16) for i in range(2)]
                eqp_ = [sbp(f"Ceqp{i}", [128, 1024], F32) for i in range(2)]
                ekp_ = [sbp(f"Cekp{i}", [128, 1024], F32) for i in range(2)]
                eqb_ = [sbp(f"Ceqb{i}", [128, 1024], BF16) for i in range(2)]
                ekb_ = [sbp(f"Cekb{i}", [128, 1024], BF16) for i in range(2)]
                ek2b_ = [sbp(f"Cek2b{i}", [128, 1024], BF16) for i in range(2)]
                qlsb_ = [sbp(f"Cqlsb{i}", [128, 1024], BF16) for i in range(2)]
                kxb_ = [sbp(f"Ckxb{i}", [128, 1024], BF16) for i in range(2)]
                kdb_ = [sbp(f"Ckdb{i}", [128, 1024], BF16) for i in range(2)]
                al = sbp("Cal", [1, 2], F32)
                al2 = sbp("Cal2", [1, 2], F32)
                mx1 = sbp("Cmx", [1, 1], F32)
                a01 = sbp("Ca01", [128, 2], F32)
                p.op("sync", lambda e: e.dma_start(out=al[:], in_=alpha), w=[("C.al",)], dma=True)
                p.op("vector", lambda v: v.reduce_max(mx1[:], al[:], axis=mybir.AxisListType.X),
                     r=[("C.al",)], w=[("C.mx",)])
                p.op("vector", lambda v: v.tensor_scalar(al2[:], al[:], mx1[0:1, 0:1], None, ALU.subtract),
                     r=[("C.al",), ("C.mx",)], w=[("C.al2",)])
                p.op("scalar", lambda s: s.activation(al2[:], al2[:], AF.Exp),
                     r=[("C.al2",)], w=[("C.al2",)])
                p.op("vector", lambda v: v.reduce_sum(mx1[:], al2[:], axis=mybir.AxisListType.X),
                     r=[("C.al2",)], w=[("C.mx",)])
                p.op("vector", lambda v: v.reciprocal(mx1[:], mx1[:]), r=[("C.mx",)], w=[("C.mx",)])
                p.op("vector", lambda v: v.tensor_scalar(al2[:], al2[:], mx1[0:1, 0:1], None, ALU.mult),
                     r=[("C.al2",), ("C.mx",)], w=[("C.al2",)])
                p.op("tensor", lambda t: t.matmul(psAl[:, 0:2], ones_k1_f[:], al2[:], start=True, stop=True),
                     r=[("C.al2",), ("o2",)], w=[psAl.k])
                p.op("scalar", lambda s: s.copy(a01[:], psAl[:, 0:2]), r=[psAl.k], w=[("C.a01",)])

                LNS = float(-0.5 * np.log(128.0))
                p.op("gpsimd", lambda g: g.memset(st_[0][:], 0.0), w=[("C.st", 0, h2) for h2 in range(NC)])
                p.op("gpsimd", lambda g: g.memset(st_[1][:], 0.0), w=[("C.st", 1, h2) for h2 in range(NC)])
                NCH = T // 128
                ld_engs = ("sync", "scalar", "gpsimd")
                for ch in (range(NCH) if 2 in c_sub else []):
                    cb = ch % 2
                    sl = slice(128 * ch, 128 * (ch + 1))
                    for ti, (buf, dsrc, dkey) in enumerate(((qc, QF, "QF"), (kc, KF, "KF"), (vc, VF, "VF"),
                                                            (g1c, G1, "G1"), (g2c, G2, "G2"))):
                        for h in range(NC):
                            eng = ld_engs[(ti * NC + h) % 3]
                            p.op(eng, lambda e, buf=buf, dsrc=dsrc, h=h, cb=cb, sl=sl: e.dma_start(
                                out=buf[cb][:, 128 * h:128 * (h + 1)], in_=dsrc[128 * h:128 * (h + 1), sl]),
                                r=[(dkey, h, sl.start // TB)], w=[("C." + dkey, cb, h)], dma=True)
                    chains = []
                    for h in range(NC):
                        for inst in range(2):
                            cp = (2 * h + inst) % NRT
                            chains.append(dict(
                                h=h, inst=inst, cp=cp, hs=slice(128 * h, 128 * (h + 1)),
                                gh=(g1c, g2c)[inst],
                                gk=("C." + ("G1", "G2")[inst], cb, h),
                                stk=("C.st", inst, h),
                                stv=st_[inst][:, 128 * h:128 * (h + 1)],
                                Gt=Gt_[cp], kdtm=kdtm_[cp % 4], Ssb=Ssb_[cp], ccol=ccol_[cp],
                                stbf=stbf_[cp],
                                psS=psS_[cp % 2], psO=psO_[cp % 2], psSD=psSD_[cp % 2],
                                KG=("C.Gt", cp), KEQ=("C.eq", cp), KEK=("C.ek", cp),
                                KQL=("C.qls", cp), KKX=("C.kx", cp), KKD=("C.kd", cp),
                                KKDT=("C.kdtm", cp % 4), KSS=("C.Ssb", cp), KSB=("C.stbf", cp),
                                KEK2=("C.ek2", cp)))
                    for wv in range(4):
                      wp = wv % 2
                      eqp, ekp, eqb, ekb = eqp_[wp], ekp_[wp], eqb_[wp], ekb_[wp]
                      ek2b, qlsb, kxb, kdb = ek2b_[wp], qlsb_[wp], kxb_[wp], kdb_[wp]
                      KW = lambda nm, wp=wp: ("C." + nm, wp)
                      chains_w = chains[8 * wv:8 * wv + 8]
                      for ii, x in enumerate(chains_w):
                          p.op("vector", lambda v, x=x, cb=cb: v.tensor_tensor_scan(
                              x["Gt"][:], x["gh"][cb][:, x["hs"]], zeros512[:, 0:128], 0.0,
                              ALU.add, ALU.add),
                              r=[x["gk"], ("z5",)], w=[x["KG"]])
                          if x["inst"] == 0:
                              pst = (psT1, psT2)[x["h"] % 2]
                              p.op("tensor", lambda t, x=x, cb=cb, pst=pst: t.transpose(
                                  pst[:, 0:128], vc[cb][:, x["hs"]], ident[:]),
                                  r=[("C.VF", cb, x["h"]), ("ident",)], w=[pst.k])
                              p.op("scalar", lambda s2, x=x, pst=pst: s2.copy(vtm[:, x["hs"]], pst[:, 0:128]),
                                   r=[pst.k], w=[("C.vtm", x["hs"].start)])
                      for ii, x in enumerate(chains_w):
                          ss = slice(512 * x["inst"] + 128 * (x["h"] - 4 * wv),
                                     512 * x["inst"] + 128 * (x["h"] - 4 * wv) + 128)
                          x["ss"] = ss
                          p.op("vector", lambda v, x=x, ss=ss, eqp=eqp: v.tensor_scalar(
                              eqp[:, ss], x["Gt"][:], x["Gt"][:, 63:64], LNS,
                              ALU.subtract, ALU.add),
                              r=[x["KG"]], w=[KW("eqp") + (ii,)])
                          p.op("vector", lambda v, x=x, ss=ss, ekp=ekp: v.tensor_scalar(
                              ekp[:, ss], x["Gt"][:], x["Gt"][:, 63:64], -1.0,
                              ALU.subtract, ALU.mult),
                              r=[x["KG"]], w=[KW("ekp") + (ii,)])
                          p.op("vector", lambda v, x=x: v.tensor_sub(
                              x["ccol"][:, 3:4], x["Gt"][:, 127:128], x["Gt"][:, 63:64]),
                              r=[x["KG"]], w=[("C.cc", x["cp"], 3)])
                          p.op("scalar", lambda s, x=x: s.activation(
                              x["ccol"][:, 1:3], x["Gt"][:, 63:128:64], AF.Exp),
                              r=[x["KG"]], w=[("C.cc", x["cp"], 1), ("C.cc", x["cp"], 2)])
                          p.op("scalar", lambda s, x=x: s.activation(
                              x["ccol"][:, 3:4], x["ccol"][:, 3:4], AF.Exp),
                              r=[("C.cc", x["cp"], 3)], w=[("C.cc", x["cp"], 3)])
                      p.op("scalar", lambda s, eqb=eqb, eqp=eqp: s.activation(eqb[:], eqp[:], AF.Exp),
                           r=[KW("eqp") + (ii2,) for ii2 in range(8)], w=[KW("eqb")])
                      p.op("scalar", lambda s, ekb=ekb, ekp=ekp: s.activation(ekb[:], ekp[:], AF.Exp),
                           r=[KW("ekp") + (ii2,) for ii2 in range(8)], w=[KW("ekb")])
                      for ii, x in enumerate(chains_w):
                          p.op("vector", lambda v, x=x, ek2b=ek2b, ekb=ekb: v.tensor_scalar(
                              ek2b[:, x["ss"]], ekb[:, x["ss"]], x["ccol"][:, 3:4], None, ALU.mult),
                              r=[KW("ekb"), ("C.cc", x["cp"], 3)], w=[KW("e2") + (ii,)])
                      wsl = slice(512 * wv, 512 * (wv + 1))
                      for inst in range(2):
                          isl = slice(512 * inst, 512 * (inst + 1))
                          p.op("vector", lambda v, cb=cb, isl=isl, wsl=wsl, qlsb=qlsb, eqb=eqb: v.tensor_mul(
                              qlsb[:, isl], qc[cb][:, wsl], eqb[:, isl]),
                              r=[KW("eqb")] + [("C.QF", cb, 4 * wv + hh) for hh in range(4)],
                              w=[KW("qlsb") + (inst,)])
                          p.op("vector", lambda v, cb=cb, isl=isl, wsl=wsl, kxb=kxb, ekb=ekb: v.tensor_mul(
                              kxb[:, isl], kc[cb][:, wsl], ekb[:, isl]),
                              r=[KW("ekb")] + [("C.KF", cb, 4 * wv + hh) for hh in range(4)],
                              w=[KW("kxb") + (inst,)])
                          p.op("vector", lambda v, cb=cb, isl=isl, wsl=wsl, kdb=kdb, ek2b=ek2b: v.tensor_mul(
                              kdb[:, isl], kc[cb][:, wsl], ek2b[:, isl]),
                              r=[KW("e2") + (2 * hh + inst,) for hh in range(4)]
                                + [("C.KF", cb, 4 * wv + hh) for hh in range(4)],
                              w=[KW("kdb") + (inst,)])
                      for ii, x in enumerate(chains_w):
                          p.op("tensor", lambda t, x=x, kxb=kxb, qlsb=qlsb: t.matmul(
                              x["psS"][:, 0:128], kxb[:, x["ss"]], qlsb[:, x["ss"]],
                              start=True, stop=True),
                              r=[KW("kxb") + (x["inst"],), KW("qlsb") + (x["inst"],)], w=[x["psS"].k])
                          p.op("vector", lambda v, x=x: v.scalar_tensor_tensor(
                              x["Ssb"][:], x["psS"][:, 0:128], 1.0, mask_ut[:], ALU.mult, ALU.mult),
                              r=[x["psS"].k, ("mut",)], w=[x["KSS"]])
                          p.op("vector", lambda v, x=x: v.tensor_scalar(
                              x["stbf"][:], x["stv"], x["ccol"][:, 1:2], None, ALU.mult),
                              r=[x["stk"], ("C.cc", x["cp"], 1)], w=[x["KSB"]])
                      for ii, x in enumerate(chains_w):
                          def fo(t, x=x, qlsb=qlsb):
                              t.matmul(x["psO"][:, 0:128], vtm[:, x["hs"]], x["Ssb"][:],
                                       start=True, stop=False)
                              return t.matmul(x["psO"][:, 0:128], x["stbf"][:], qlsb[:, x["ss"]],
                                              start=False, stop=True)
                          p.op("tensor", fo, r=[("C.vtm", x["hs"].start), x["KSS"], x["KSB"],
                                                KW("qlsb") + (x["inst"],)],
                               w=[x["psO"].k])
                          if x["inst"] == 0:
                              p.op("vector", lambda v, x=x: v.tensor_scalar(
                                  ocomb[:, x["hs"]], x["psO"][:, 0:128], a01[:, 0:1], None, ALU.mult),
                                  r=[x["psO"].k, ("C.a01",)], w=[("C.oc", x["hs"].start)])
                          else:
                              obp = x["h"] % 2
                              p.op("vector", lambda v, x=x, obp=obp: v.scalar_tensor_tensor(
                                  ob_[obp][:], x["psO"][:, 0:128], a01[:, 1:2], ocomb[:, x["hs"]],
                                  ALU.mult, ALU.add),
                                  r=[x["psO"].k, ("C.a01",), ("C.oc", x["hs"].start)], w=[("C.ob", obp)])
                              p.op("sync", lambda e, x=x, sl=sl, obp=obp: e.dma_start(
                                  out=OIN[128 * x["h"]:128 * (x["h"] + 1), sl], in_=ob_[obp][:]),
                                  r=[("C.ob", obp)], w=[("C.attn", x["h"], sl.start)], dma=True)
                      for ii, x in enumerate(chains_w):
                          pst = (psT1, psT2)[ii % 2]
                          p.op("tensor", lambda t, x=x, pst=pst, kdb=kdb: t.transpose(
                              pst[:, 0:128], kdb[:, x["ss"]], ident[:]),
                              r=[KW("kdb") + (x["inst"],), ("ident",)], w=[pst.k])
                          p.op("scalar", lambda s, x=x, pst=pst: s.copy(x["kdtm"][:], pst[:, 0:128]),
                               r=[pst.k], w=[x["KKDT"]])
                          p.op("tensor", lambda t, x=x: t.matmul(
                              x["psSD"][:, 0:128], x["kdtm"][:], vtm[:, x["hs"]], start=True, stop=True),
                              r=[x["KKDT"], ("C.vtm", x["hs"].start)], w=[x["psSD"].k])
                          p.op("vector", lambda v, x=x: v.scalar_tensor_tensor(
                              x["stv"], x["stv"], x["ccol"][:, 2:3], x["psSD"][:, 0:128],
                              ALU.mult, ALU.add),
                              r=[x["stk"], ("C.cc", x["cp"], 2), x["psSD"].k], w=[x["stk"]])

                wob = [sbp(f"C3w{i}", [128, TB], BF16) for i in range(NC)]
                atb = [sbp(f"C3a{i}", [128, TB], BF16) for i in range(NC)]
                rf = [sbp(f"C3r{i}", [128, TB], F32) for i in range(2)]
                ev3 = [sbp(f"C3e{i}", [128, TB], F32) for i in range(2)]
                for tb in (range(NTB) if 3 in c_sub else []):
                    for c in range(NC):
                        p.op("sync", lambda e, c=c, tb=tb: e.dma_start(
                            out=atb[c][:], in_=OIN[128 * c:128 * (c + 1), TB * tb:TB * (tb + 1)]),
                            r=[("C.attn", c, 128 * q) for q in range(4 * tb, 4 * tb + 4)],
                            w=[("C3.at", c)], dma=True)
                    for og in range(4):
                        par = og % 2
                        for c in range(NC):
                            p.op("gpsimd", lambda e, c=c, og=og: e.dma_start(
                                out=wob[c][:], in_=woT[128 * c:128 * (c + 1), TB * og:TB * (og + 1)]),
                                w=[("C3.wo", c)], dma=True)
                        for j in range(4):
                            m = 4 * og + j
                            pp = (psS_[0], psO_[0])[j % 2]
                            def fm3(t, pp=pp, par=par, j=j, tb=tb):
                                for c in range(NC):
                                    ins = t.matmul(pp[:, 0:TB], wob[c][:, 128 * j:128 * (j + 1)],
                                                   atb[c][:],
                                                   start=(c == 0), stop=(c == NC - 1))
                                return ins
                            p.op("tensor", fm3,
                                 r=[("C3.wo", c) for c in range(NC)]
                                   + [("C3.at", c) for c in range(NC)],
                                 w=[pp.k])
                            jp = j % 2
                            p.op("sync", lambda e, m=m, tb=tb, jp=jp: e.dma_start(
                                out=rf[jp][:], in_=R2[128 * m:128 * (m + 1), TB * tb:TB * (tb + 1)]),
                                r=[("RES_B", m, tb)], w=[("C3.rf", jp)], dma=True)
                            p.op("vector", lambda v, pp=pp, jp=jp: v.scalar_tensor_tensor(
                                ev3[jp][:], pp[:, 0:TB], 1.0, rf[jp][:], ALU.mult, ALU.add),
                                r=[pp.k, ("C3.rf", jp)], w=[("C3.ev", jp)])
                            p.op("scalar", lambda e, m=m, tb=tb, jp=jp: e.dma_start(
                                out=RC[128 * m:128 * (m + 1), TB * tb:TB * (tb + 1)], in_=ev3[jp][:]),
                                r=[("C3.ev", jp)], w=[("RES_C", m, tb)], dma=True)

        if "A" in phases:
            phase_A01(res_out["A"])
            phase_A2()
            phase_A3()
            phase_A4(res_out["A"])
        if "B" in phases or "D" in phases:
            drain_v_h0()
        if "B" in phases:
            ffn_phase("B", R1, lambda c, tb: ("RES_A", c, tb), res_out["B"], "RES_B")
        if "C" in phases:
            phase_C(res_out["C"])
        if "D" in phases:
            ffn_phase("Dp", R3, lambda c, tb: ("RES_C", c, tb), res_out["D"], "RES_D")

        lk = res_key[last_phase]
        final_keys = [(lk, m, tb) for m in range(NC) for tb in range(NTB)]
        final_keys = [k for k in final_keys if k in p.lastw]
        p.barrier()
        p.op("vector", lambda v: v.nop(), r=final_keys, w=[])

        mk, counts = p.finalize(sems)
        block = st.enter_context(nc.Block())
        block.sync(mk("sync"))
        block.tensor(mk("tensor"))
        block.vector(mk("vector"))
        block.scalar(mk("scalar"))
        block.gpsimd(mk("gpsimd"))
    return nc, counts


def host_inputs(inputs, core):
    d = {}
    d["xT"] = np.ascontiguousarray(inputs["L0_attn_a_x"][core].T)
    d["w_in"] = inputs["L0_attn_a_W_in"]
    cw = inputs["L0_attn_a_conv_w"]
    d["cw_h"] = np.ascontiguousarray(cw.transpose(2, 1, 0).reshape(4, D, NC, 128).transpose(2, 0, 1, 3))
    d["conv_b"] = np.ascontiguousarray(inputs["L0_attn_a_conv_b"].reshape(NC, 128).T)
    d["w_x"] = inputs["L0_attn_a_W_x"]
    d["w_dt"] = inputs["L0_attn_a_W_dt"]
    d["b_dt"] = np.ascontiguousarray(inputs["L0_attn_a_b_dt"].reshape(NC, 128).T)
    d["a_log"] = np.ascontiguousarray(inputs["L0_attn_a_A_log"].reshape(NC, 128, SSN).transpose(1, 0, 2).reshape(128, NC * SSN))
    d["d_vec"] = np.ascontiguousarray(inputs["L0_attn_a_D"].reshape(NC, 128).T)
    d["w_out"] = inputs["L0_attn_a_W_out"]
    d["wqT"] = np.ascontiguousarray(inputs["L0_attn_b_Wq"].T)
    d["wkT"] = np.ascontiguousarray(inputs["L0_attn_b_Wk"].T)
    d["wvT"] = np.ascontiguousarray(inputs["L0_attn_b_Wv"].T)
    d["woT"] = np.ascontiguousarray(inputs["L0_attn_b_Wo"].T)
    d["wg1T"] = np.ascontiguousarray(inputs["L0_attn_b_Wg1"].T)
    d["wg2T"] = np.ascontiguousarray(inputs["L0_attn_b_Wg2"].T)
    d["bg1"] = np.ascontiguousarray(inputs["L0_attn_b_bg1"].reshape(NC, 128).T)
    d["bg2"] = np.ascontiguousarray(inputs["L0_attn_b_bg2"].reshape(NC, 128).T)
    d["alpha"] = np.ascontiguousarray(inputs["L0_attn_b_alpha_list"].reshape(1, 2))
    d["lut"] = np.ascontiguousarray(inputs["L0_ffn_lut"].reshape(1, 16))
    d["gidxT"] = np.ascontiguousarray(inputs["L0_ffn_gate_idx"].T)
    d["uidxT"] = np.ascontiguousarray(inputs["L0_ffn_up_idx"].T)
    d["didxT"] = np.ascontiguousarray(inputs["L0_ffn_down_idx"].T)
    return d


_PHASES = ("A", "B", "C", "D")


def kernel(**inputs):
    nc, _ = build_nc(_PHASES)
    in_maps = [host_inputs(inputs, i) for i in range(8)]
    res = run_bass_kernel_spmd(nc, in_maps, core_ids=list(range(8)))
    out = np.stack([np.ascontiguousarray(r["outT"].T) for r in res.results])
    return out.astype(np.float32)
